# version 06
# baseline (speedup 1.0000x reference)
import os
import math
import contextlib
import numpy as np
import ml_dtypes
import concourse.bass as bass
import concourse.mybir as mybir
from concourse.bass_utils import run_bass_kernel_spmd

F32 = mybir.dt.float32
BF16 = mybir.dt.bfloat16
AF = mybir.ActivationFunctionType
ALU = mybir.AluOpType
AX = mybir.AxisListType

GROUPS = [[0, 1], [2, 3], [4, 5], [6, 7]]
EVEN_TILES = [0, 3, 4, 7]
ODD_TILES = [1, 2, 5, 6]
NEG_BIG = -1.0e30
EPS = 1e-5

FULL_CFG = dict(D=2048, DE=1408, DS=2816, E=8, HD=128, NH=16, TT=512)
TINY_CFG = dict(D=512, DE=256, DS=512, E=8, HD=128, NH=4, TT=128)
MIDI_CFG = dict(D=1024, DE=512, DS=1024, E=8, HD=128, NH=8, TT=256)


def derive(cfg):
    c = dict(cfg)
    c["DC"] = c["D"] // 128
    c["EHC"] = c["DE"] // 128
    c["SHC"] = c["DS"] // 128
    c["TC"] = 4 * c["TT"]
    c["S"] = 8 * c["TT"]
    c["CPT"] = c["TT"] // 128
    c["KCH"] = c["S"] // 128
    assert c["NH"] * c["HD"] == c["D"]
    return c


def snake_rows(TT):
    rows = {}
    for par, tiles in ((0, EVEN_TILES), (1, ODD_TILES)):
        r = []
        for t in tiles:
            r.extend(range(t * TT, (t + 1) * TT))
        rows[par] = np.array(r, dtype=np.int64)
    return rows


def bufcol_map(CPT):
    m = {}
    for par, tiles in ((0, EVEN_TILES), (1, ODD_TILES)):
        for i, t in enumerate(tiles):
            for j in range(CPT):
                m[t * CPT + j] = par * 4 * CPT + i * CPT + j
    return m


class Sem:
    def __init__(self, handle, name):
        self.h = handle
        self.name = name
        self.n = 0

    def inc(self, instr, k=1):
        instr.then_inc(self.h, k)
        self.n += k
        return self.n


class B:

    def __init__(self, nc, cfg, layers=(0, 1), in_name="x_in", out_name="out"):
        self.nc = nc
        self.c = derive(cfg)
        self.layers = layers
        self.in_name = in_name
        self.out_name = out_name
        self.es = contextlib.ExitStack()
        self.dram = {}
        self.sb = {}
        self.sems = {}
        self.cast_ready = {}
        self.bufcol = bufcol_map(self.c["CPT"])
        self.wguard = {"A": None, "B": None}
        self.next_slot = "A"
        self.xn_guard = None
        self.scope_guards = []
        self.h_ready = 0
        self.kvq_extra_guard = None

    def declare(self):
        nc, c = self.nc, self.c
        D, DE, DS, E = c["D"], c["DE"], c["DS"], c["E"]
        TC, HD, NH = c["TC"], c["HD"], c["NH"]

        def din(name, shape):
            self.dram[name] = nc.dram_tensor(name, shape, F32, kind="ExternalInput").ap()

        def dint(name, shape, dt=BF16):
            self.dram[name] = nc.dram_tensor(name, shape, dt).ap()

        DC, EHC, SHC, NH2 = c["DC"], c["EHC"], c["SHC"], c["NH"]
        nhh = NH2 // 2
        din(self.in_name, [128, c["DC"] * TC])
        shp = {
            "wsg": [SHC, 128, DC * 128], "wsu": [SHC, 128, DC * 128],
            "wsd": [DC, 128, SHC * 128],
            "weg": [E, EHC, 128, DC * 128], "weu": [E, EHC, 128, DC * 128],
            "wed": [E, DC, 128, EHC * 128],
            "qw": [NH2, 128, DC * 128], "kw": [NH2, 128, DC * 128],
            "vw": [D // 512, 2, 128, (DC // 2) * 512],
            "ow": [2, DC // 4, 128, nhh * 512],
            "gp": [E, EHC, 128, DC * 128], "up": [E, EHC, 128, DC * 128],
            "dp": [E, DC, 128, EHC * 128],
        }
        if 0 in self.layers:
            din("w_dec", [2, D]); din("b_dec", [1, D]); din("dvv", [2, 1])
            din("wr0", [E, D])
            for nm in ("wsg", "wsu", "wsd", "weg", "weu", "wed"):
                din(nm, shp[nm]); dint("c_" + nm, shp[nm])
            dint("c_wdec", [c["DC"] // 2, 128, c["DC"] * 256])
        if 1 in self.layers:
            din("gw", [HD, HD]); din("gb", [1, HD]); din("wr1", [E, D])
            for nm in ("qw", "kw", "vw", "ow", "gp", "up", "dp"):
                din(nm, shp[nm]); dint("c_" + nm, shp[nm])
            dint("kvloc", [2 * D, TC])
            dint("kvg", [4 * D, TC])
            dint("qloc", [D, TC])
        self.dram[self.out_name] = nc.dram_tensor(self.out_name, [128, c["DC"] * TC],
                                                  F32, kind="ExternalOutput").ap()

    def alloc_sems(self):
        names = ["dIn", "dWA", "dWB", "dWd", "dSc", "dGp", "dKtA", "dKtB",
                 "sPE", "sDVE", "sACT", "sGP", "sCC", "dOut", "sAH",
                 "dKsA", "dKsB", "dVsA", "dVsB", "dQsA", "dQsB"]
        if 0 in self.layers:
            names += ["cgws"] + [f"cgwe{e}" for e in range(self.c["E"])]
        if 1 in self.layers:
            names += ["cgqk"] + [f"cgl1e{e}" for e in range(self.c["E"])]
        for n in names:
            self.sems[n] = Sem(self.es.enter_context(self.nc.semaphore(n)), n)

    def alloc_psum(self):
        self.ps = [self.es.enter_context(self.nc.psum_tensor(f"ps{i}", [128, 512], F32))
                   for i in range(6)]
        self.pr = [self.es.enter_context(self.nc.psum_tensor(f"pr{i}", [1, 512], F32))
                   for i in range(2)]
        self.bank_guard = [None] * 6
        self.pr_guard = None
        self._rr = 0

    def alloc_persistent(self):
        nc, c = self.nc, self.c
        TT, DC = c["TT"], c["DC"]

        def S(name, shape, dt):
            self.sb[name] = self.es.enter_context(nc.sbuf_tensor(name, shape, dt))

        S("x_res", [128, DC * c["TC"]], F32)
        S("wA", [128, max(4096, 2 * DC * 128)], BF16)
        S("wB", [128, max(4096, 2 * DC * 128)], BF16)
        S("ident", [128, 128], F32)
        S("onesc", [128, 1], BF16)
        S("ones1", [1, 128], F32)
        S("selmat", [8, 8 * 128], BF16)
        S("wr0_sb", [128, DC * 8], BF16)
        S("wr1_sb", [128, DC * 8], BF16)
        S("gwT_sb", [128, 128], BF16)
        S("gbneg", [128, 1], F32)
        S("bias_sb", [128, DC], F32)
        S("rlog", [128, 4 * 8], F32)
        S("escr", [128, 8], F32)
        S("escr2", [128, 8], F32)
        S("e8", [128, 8], F32)
        S("r_a", [128, 1], F32)
        S("r_b", [128, 1], F32)
        S("r_c", [128, 1], F32)
        S("r_d", [128, 1], F32)
        S("cb_tm", [128, 4 * 8], F32)
        S("cb_fm", [8, TT], BF16)
        S("cb_b", [128, TT], BF16)
        S("xtmp", [128, TT], F32)
        S("rowA", [1, TT], F32)
        S("lnd0", [128, 1], F32)
        S("lnd1", [128, 1], F32)
        S("w0col", [128, DC], F32)
        S("w1col", [128, DC], F32)
        S("stg16", [16, 128], F32)
        S("identB", [128, 128], BF16)
        S("dvt", [2, 1], F32)

    def take_bank(self, t):
        i = self._rr
        self._rr = (self._rr + 1) % 6
        g = self.bank_guard[i]
        if g is not None:
            t.wait_ge(g[0].h, g[1])
        return i

    def slot(self):
        s = self.next_slot
        self.next_slot = "B" if s == "A" else "A"
        return s

    def load_slab(self, slot, src_ap, off=0, first=True, ready=None):
        sems = self.sems
        dW = sems["dWA" if slot == "A" else "dWB"]
        buf = self.sb["wA" if slot == "A" else "wB"]
        guard = self.wguard[slot] if first else None
        a, bcols = int(src_ap.shape[1]), int(src_ap.shape[2])
        dest = buf[:, off:off + a * bcols].rearrange("p (a b) -> p a b", b=bcols)

        @self.block.sync
        def _(sy):
            if guard is not None:
                sy.wait_ge(guard[0].h, guard[1])
            if ready is not None:
                sy.wait_ge(ready[0].h, ready[1])
            dW.inc(sy.dma_start(out=dest, in_=src_ap), 16)
        return buf, dW.n

    def xres(self, dc, t0, n):
        c = self.c
        return self.sb["x_res"][:, dc * c["TC"] + t0: dc * c["TC"] + t0 + n]

    def emit_prologue(self):
        nc, c, block = self.nc, self.c, self.block
        sems, sb, dram = self.sems, self.sb, self.dram
        D, DC, TC, E = c["D"], c["DC"], c["TC"], c["E"]

        @block.scalar
        def _(s):
            sems["dIn"].inc(s.dma_start(out=sb["x_res"][:], in_=dram[self.in_name]), 16)
            if 0 in self.layers:
                sems["dSc"].inc(s.dma_start(out=sb["dvt"][:], in_=dram["dvv"]), 16)
                s.wait_ge(sems["dSc"].h, sems["dSc"].n)

        @block.gpsimd
        def _(g):
            g.iota(sb["ident"][:], [[1, 128]], base=0, channel_multiplier=-1,
                   allow_small_or_imprecise_dtypes=True)
            sems["sGP"].inc(g.iota(sb["wA"][0:8, 0:8 * 128], [[1, 8], [0, 128]],
                                   base=0, channel_multiplier=-1,
                                   allow_small_or_imprecise_dtypes=True))
        @block.vector
        def _(v):
            v.wait_ge(sems["sGP"].h, sems["sGP"].n)
            v.tensor_scalar(sb["ident"][:], sb["ident"][:], 0.0, None, ALU.is_equal)
            v.tensor_scalar(sb["selmat"][:], sb["wA"][0:8, 0:8 * 128], 0.0, None, ALU.is_equal)
            v.memset(sb["onesc"][:], 1.0)
            v.memset(sb["ones1"][:], 1.0)
            v.drain()
            sems["sDVE"].inc(v.tensor_copy(sb["identB"][:], sb["ident"][:]))
        self.wguard["A"] = (sems["sDVE"], sems["sDVE"].n)
        self.wguard["B"] = (sems["sDVE"], sems["sDVE"].n)

        ps = self.ps
        stage_jobs = []
        if 0 in self.layers:
            stage_jobs.append((dram["b_dec"], "bias_sb"))
            stage_jobs.append((dram["w_dec"][0:1, :], "w0col"))
            stage_jobs.append((dram["w_dec"][1:2, :], "w1col"))
        for srow, dname in stage_jobs:
            @block.scalar
            def _(s, srow=srow):
                s.wait_ge(sems["sPE"].h, sems["sPE"].n)
                sems["dSc"].inc(s.dma_start(
                    out=sb["stg16"][0:DC, :],
                    in_=srow.rearrange("o (dc p) -> (o dc) p", p=128)), 16)
                s.wait_ge(sems["dSc"].h, sems["dSc"].n)
            @block.tensor
            def _(t):
                t.wait_ge(sems["dSc"].h, sems["dSc"].n)
                t.wait_ge(sems["sDVE"].h, sems["sDVE"].n)
                sems["sPE"].inc(t.transpose(ps[0][:, 0:DC], sb["stg16"][0:DC, :],
                                            sb["ident"][0:DC, 0:DC]))
            @block.vector
            def _(v, dname=dname):
                v.wait_ge(sems["sPE"].h, sems["sPE"].n)
                v.drain()
                sems["sDVE"].inc(v.tensor_copy(sb[dname][:, 0:DC], ps[0][:, 0:DC]))

        for li, dname in ((0, "wr0"), (1, "wr1")):
            if li not in self.layers:
                continue
            for dc in range(DC):
                @block.scalar
                def _(s, li=li, dc=dc):
                    s.wait_ge(sems["sPE"].h, sems["sPE"].n)
                    sems["dSc"].inc(s.dma_start(
                        out=sb["stg16"][0:8, :],
                        in_=dram["wr0" if li == 0 else "wr1"][:, dc * 128:(dc + 1) * 128]), 16)
                    s.wait_ge(sems["dSc"].h, sems["dSc"].n)
                @block.tensor
                def _(t, dc=dc):
                    t.wait_ge(sems["dSc"].h, sems["dSc"].n)
                    if dc == 0:
                        t.wait_ge(sems["sDVE"].h, sems["sDVE"].n)
                    sems["sPE"].inc(t.transpose(ps[1][:, dc * 8:(dc + 1) * 8],
                                                sb["stg16"][0:8, :],
                                                sb["ident"][0:8, 0:8]))
            @block.vector
            def _(v, dname=dname):
                v.wait_ge(sems["sPE"].h, sems["sPE"].n)
                v.drain()
                sems["sDVE"].inc(v.tensor_copy(sb[dname + "_sb"][:], ps[1][:, 0:DC * 8]))
        self.bank_guard[0] = (sems["sDVE"], sems["sDVE"].n)
        self.bank_guard[1] = (sems["sDVE"], sems["sDVE"].n)

        @block.scalar
        def _(s):
            if 1 in self.layers:
                s.wait_ge(sems["sPE"].h, sems["sPE"].n)
                sems["dSc"].inc(s.dma_start(out=sb["xtmp"][:, 0:128], in_=dram["gw"]), 16)
                s.wait_ge(sems["dSc"].h, sems["dSc"].n)
                sems["dSc"].inc(s.dma_start(out=sb["stg16"][0:1, 0:128], in_=dram["gb"]), 16)
                s.wait_ge(sems["dSc"].h, sems["dSc"].n)
        @block.vector
        def _(v):
            if 1 in self.layers:
                v.wait_ge(sems["dSc"].h, sems["dSc"].n)
                v.drain()
                sems["sDVE"].inc(v.tensor_copy(sb["gwT_sb"][:], sb["xtmp"][:, 0:128]))
        @block.tensor
        def _(t):
            if 1 in self.layers:
                t.wait_ge(sems["dSc"].h, sems["dSc"].n)
                sems["sPE"].inc(t.transpose(ps[2][:, 0:1], sb["stg16"][0:1, 0:128],
                                            sb["ident"][0:1, 0:1]))
        @block.vector
        def _(v):
            if 1 in self.layers:
                v.wait_ge(sems["sPE"].h, sems["sPE"].n)
                v.drain()
                sems["sDVE"].inc(v.tensor_scalar(sb["gbneg"][:], ps[2][:, 0:1],
                                                 -1.0, None, ALU.mult))
        self.bank_guard[2] = (sems["sDVE"], sems["sDVE"].n)

        self.emit_casts()
        if 0 in self.layers:
            self.emit_wdecay()

    def emit_casts(self):
        c, block = self.c, self.block
        sems, dram = self.sems, self.dram
        E = c["E"]
        CH = 1 << 19

        def pieces(g, cg, dst, srcname):
            if "!" in srcname:
                nm, e = srcname.split("!")
                srcap = dram[nm][int(e)]
            else:
                srcap = dram[srcname]
            n = 1
            for d in srcap.shape:
                n *= d
            flat_src = srcap.rearrange(
                " ".join(f"d{i}" for i in range(len(srcap.shape))) + " -> ("
                + " ".join(f"d{i}" for i in range(len(srcap.shape))) + ")")
            flat_dst = dst.rearrange(
                " ".join(f"d{i}" for i in range(len(dst.shape))) + " -> ("
                + " ".join(f"d{i}" for i in range(len(dst.shape))) + ")")
            o = 0
            while o < n:
                m = min(CH, n - o)
                cg.inc(g.dma_start(out=flat_dst[o:o + m], in_=flat_src[o:o + m]), 16)
                o += m

        @block.gpsimd
        def _(g):
            g.wait_ge(sems["dIn"].h, 16)
            g.wait_ge(sems["dSc"].h, sems["dSc"].n)
            g.wait_ge(sems["dGp"].h, sems["dGp"].n)
            if 0 in self.layers:
                cg = sems["cgws"]
                for nm in ("wsg", "wsu", "wsd"):
                    pieces(g, cg, dram["c_" + nm], nm)
                self.cast_ready["ws"] = (cg, cg.n)
                for e in range(E):
                    cge = sems[f"cgwe{e}"]
                    for nm in ("weg", "weu", "wed"):
                        pieces(g, cge, dram["c_" + nm][e], nm + f"!{e}")
                    self.cast_ready[f"we{e}"] = (cge, cge.n)
            if 1 in self.layers:
                cg = sems["cgqk"]
                for nm in ("qw", "kw", "vw", "ow"):
                    pieces(g, cg, dram["c_" + nm], nm)
                self.cast_ready["qkvo"] = (cg, cg.n)
                for e in range(E):
                    cge = sems[f"cgl1e{e}"]
                    for nm in ("gp", "up", "dp"):
                        pieces(g, cge, dram["c_" + nm][e], nm + f"!{e}")
                    self.cast_ready[f"l1e{e}"] = (cge, cge.n)

    def emit_wdecay(self):
        nc, c, block = self.nc, self.c, self.block
        sems, sb, dram = self.sems, self.sb, self.dram
        D, DC = c["D"], c["DC"]
        ps, pr = self.ps, self.pr

        @block.vector
        def _(v):
            v.wait_ge(sems["dSc"].h, sems["dSc"].n)
            sems["sDVE"].inc(v.tensor_scalar(sb["dvt"][:], sb["dvt"][:], 0.9, 1.0,
                                             ALU.max, ALU.min))
        @block.scalar
        def _(s):
            s.wait_ge(sems["sDVE"].h, sems["sDVE"].n)
            sems["sACT"].inc(s.activation(sb["dvt"][:], sb["dvt"][:], AF.Ln))
        @block.scalar
        def _(s):
            s.wait_ge(sems["sACT"].h, sems["sACT"].n)
            sems["dSc"].inc(s.dma_start(out=sb["r_a"][0:1, 0:1], in_=sb["dvt"][0:1, 0:1]), 16)
            s.wait_ge(sems["dSc"].h, sems["dSc"].n)
            sems["dSc"].inc(s.dma_start(out=sb["r_b"][0:1, 0:1], in_=sb["dvt"][1:2, 0:1]), 16)
            s.wait_ge(sems["dSc"].h, sems["dSc"].n)
        @block.tensor
        def _(t):
            t.wait_ge(sems["dSc"].h, sems["dSc"].n)
            t.wait_ge(sems["sDVE"].h, sems["sDVE"].n)
            t.matmul(ps[0][:, 0:1], sb["ones1"][0:1, :], sb["r_a"][0:1, 0:1],
                     start=True, stop=True)
            sems["sPE"].inc(t.matmul(ps[1][:, 0:1], sb["ones1"][0:1, :],
                                     sb["r_b"][0:1, 0:1], start=True, stop=True))
        @block.vector
        def _(v):
            v.wait_ge(sems["sPE"].h, sems["sPE"].n)
            v.tensor_copy(sb["lnd0"][:], ps[0][:, 0:1])
            sems["sDVE"].inc(v.tensor_copy(sb["lnd1"][:], ps[1][:, 0:1]))

        with contextlib.ExitStack() as scope:
            vrow = scope.enter_context(nc.sbuf_tensor("wd_v", [1, D], F32))
            r0 = scope.enter_context(nc.sbuf_tensor("wd_r0", [1, D], F32))
            r1 = scope.enter_context(nc.sbuf_tensor("wd_r1", [1, D], F32))
            w1r = scope.enter_context(nc.sbuf_tensor("wd_w1r", [1, D], F32))
            mbig0 = scope.enter_context(nc.sbuf_tensor("wd_m", [128, 2 * D], BF16))
            wbfF = scope.enter_context(nc.sbuf_tensor("wd_bf", [128, D], BF16))
            ecol0 = scope.enter_context(nc.sbuf_tensor("wd_e0", [128, 1], F32))
            ecol1 = scope.enter_context(nc.sbuf_tensor("wd_e1", [128, 1], F32))
            cA = scope.enter_context(nc.sbuf_tensor("wd_ca", [128, 1], F32))
            c0T = scope.enter_context(nc.sbuf_tensor("wd_c0t", [1, 128], F32))
            c1T = scope.enter_context(nc.sbuf_tensor("wd_c1t", [1, 128], F32))

            @block.vector
            def _(v):
                v.drain()
                sems["sDVE"].inc(v.memset(mbig0[:], 1.0))
            @block.gpsimd
            def _(g):
                g.wait_ge(sems["sDVE"].h, sems["sDVE"].n)
                g.affine_select(mbig0[:], mbig0[:], [[1, 2 * D]], ALU.is_ge, 0.0,
                                base=-D, channel_multiplier=-1)
                sems["sGP"].inc(g.iota(ecol0[:], [[1, 1]], base=0, channel_multiplier=-1,
                                       allow_small_or_imprecise_dtypes=True))
            @block.scalar
            def _(s):
                sems["dSc"].inc(s.dma_start(out=w1r[0:1, :], in_=dram["w_dec"][1:2, :]), 16)
                s.wait_ge(sems["dSc"].h, sems["dSc"].n)
            @block.vector
            def _(v):
                v.wait_ge(sems["sGP"].h, sems["sGP"].n)
                v.drain()
                v.tensor_scalar(ecol1[:], ecol0[:], sb["lnd1"][:, 0:1], None, ALU.mult)
                v.drain()
                sems["sDVE"].inc(v.tensor_scalar(ecol0[:], ecol0[:], sb["lnd0"][:, 0:1],
                                                 None, ALU.mult))
            @block.scalar
            def _(s):
                s.wait_ge(sems["sDVE"].h, sems["sDVE"].n)
                s.drain()
                s.activation(ecol0[:], ecol0[:], AF.Exp)
                s.drain()
                sems["sACT"].inc(s.activation(ecol1[:], ecol1[:], AF.Exp))
            @block.tensor
            def _(t):
                t.wait_ge(sems["sACT"].h, sems["sACT"].n)
                sems["sPE"].inc(t.transpose(ps[0][0:1, 0:128], ecol1[:],
                                            sb["ident"][:]))
            @block.vector
            def _(v):
                v.wait_ge(sems["sPE"].h, sems["sPE"].n)
                v.drain()
                sems["sDVE"].inc(v.tensor_copy(c1T[0:1, :], ps[0][0:1, 0:128]))

            nwin = D // 512
            for kc in range(DC):
                @block.vector
                def _(v, kc=kc):
                    v.wait_ge(sems["sPE"].h, sems["sPE"].n)
                    v.drain()
                    sems["sDVE"].inc(v.tensor_scalar(cA[:], ecol0[:],
                                                     sb["w0col"][:, kc:kc + 1],
                                                     None, ALU.mult))
                @block.gpsimd
                def _(g, kc=kc):
                    g.wait_ge(sems["sDVE"].h, sems["sDVE"].n)
                    sems["sGP"].inc(g.iota(vrow[0:1, :], [[1, D]], base=-128 * kc,
                                           channel_multiplier=0,
                                           allow_small_or_imprecise_dtypes=True))
                @block.tensor
                def _(t, kc=kc):
                    t.wait_ge(sems["sDVE"].h, sems["sDVE"].n)
                    sems["sPE"].inc(t.transpose(ps[0][0:1, 0:128], cA[:],
                                                sb["ident"][:]))
                @block.vector
                def _(v, kc=kc):
                    v.wait_ge(sems["sPE"].h, sems["sPE"].n)
                    v.wait_ge(sems["sGP"].h, sems["sGP"].n)
                    v.drain()
                    v.tensor_copy(c0T[0:1, :], ps[0][0:1, 0:128])
                    v.tensor_scalar(vrow[0:1, :], vrow[0:1, :], 0.0, None, ALU.max)
                    v.drain()
                    v.tensor_scalar(r0[0:1, :], vrow[0:1, :], sb["lnd0"][0:1, 0:1],
                                    None, ALU.mult)
                    v.drain()
                    sems["sDVE"].inc(v.tensor_scalar(r1[0:1, :], vrow[0:1, :],
                                                     sb["lnd1"][0:1, 0:1], None, ALU.mult))
                @block.scalar
                def _(s, kc=kc):
                    s.wait_ge(sems["sDVE"].h, sems["sDVE"].n)
                    s.drain()
                    s.activation(r0[0:1, :], r0[0:1, :], AF.Exp)
                    s.drain()
                    sems["sACT"].inc(s.activation(r1[0:1, :], r1[0:1, :], AF.Exp))
                @block.vector
                def _(v, kc=kc):
                    v.wait_ge(sems["sACT"].h, sems["sACT"].n)
                    v.drain()
                    sems["sDVE"].inc(v.tensor_tensor(r1[0:1, :], r1[0:1, :],
                                                     w1r[0:1, :], ALU.mult))
                pe_w = sems["sPE"].n
                dve_w = sems["sDVE"].n
                @block.tensor
                def _(t, kc=kc, pe_w=pe_w, dve_w=dve_w):
                    t.wait_ge(sems["sDVE"].h, dve_w)
                    for win in range(nwin):
                        if win >= 2:
                            t.wait_ge(sems["sDVE"].h, dve_w + win - 1)
                        bkw = 2 + win % 2
                        t.matmul(ps[bkw][:, 0:512], c0T[0:1, :],
                                 r0[0:1, win * 512:(win + 1) * 512],
                                 start=True, stop=False)
                        t.matmul(ps[bkw][:, 0:512], c1T[0:1, :],
                                 r1[0:1, win * 512:(win + 1) * 512],
                                 start=False, stop=True).then_inc(sems["sPE"].h, 1)
                sems["sPE"].n = pe_w + nwin
                dwd_prev = sems["dWd"].n
                @block.vector
                def _(v, kc=kc, pe_w=pe_w, dwd_prev=dwd_prev):
                    v.wait_ge(sems["dWd"].h, dwd_prev)
                    for win in range(nwin):
                        v.wait_ge(sems["sPE"].h, pe_w + win + 1)
                        v.drain()
                        v.tensor_tensor(
                            wbfF[:, win * 512:(win + 1) * 512],
                            ps[2 + win % 2][:, 0:512],
                            mbig0[:, D - 128 * kc + win * 512: D - 128 * kc + (win + 1) * 512],
                            ALU.mult).then_inc(sems["sDVE"].h, 1)
                sems["sDVE"].n = dve_w + nwin
                @block.gpsimd
                def _(sy, kc=kc):
                    sy.wait_ge(sems["sDVE"].h, sems["sDVE"].n)
                    sems["dWd"].inc(sy.dma_start(
                        out=dram["c_wdec"][:, :, kc * 256:(kc + 1) * 256].rearrange(
                            "dcp p j -> p dcp j"),
                        in_=wbfF[:].rearrange("p (dcp j) -> p dcp j", j=256)), 16)
            self.cast_ready["wdec"] = (sems["dWd"], sems["dWd"].n)
            self.scope_guards.append((sems["dWd"], sems["dWd"].n))
            self.scope_guards.append((sems["sDVE"], sems["sDVE"].n))

    def emit_stats_xn(self, tile, extra_guards=()):
        c, block = self.c, self.block
        sems, sb = self.sems, self.sb
        DC, TT, D = c["DC"], c["TT"], c["D"]
        t0 = tile * TT
        ps, pr = self.ps, self.pr
        xnb = sb["xnb"]
        xsq = sb["H"]

        @block.vector
        def _(v):
            v.wait_ge(sems["dIn"].h, 16)
            if self.xn_guard is not None:
                v.wait_ge(self.xn_guard[0].h, self.xn_guard[1])
            for g in extra_guards:
                v.wait_ge(g[0].h, g[1])
            v.drain()
            for dc in range(DC):
                v.tensor_copy(xnb[:, dc * TT:(dc + 1) * TT], self.xres(dc, t0, TT))
            for dc in range(DC - 1):
                v.tensor_tensor(xsq[:, dc * TT:(dc + 1) * TT],
                                self.xres(dc, t0, TT), self.xres(dc, t0, TT), ALU.mult)
            dc = DC - 1
            sems["sDVE"].inc(v.tensor_tensor(xsq[:, dc * TT:(dc + 1) * TT],
                                             self.xres(dc, t0, TT),
                                             self.xres(dc, t0, TT), ALU.mult))
        @block.tensor
        def _(t):
            t.wait_ge(sems["sDVE"].h, sems["sDVE"].n)
            if self.pr_guard is not None:
                t.wait_ge(self.pr_guard[0].h, self.pr_guard[1])
            for dc in range(DC):
                t.matmul(pr[0][0:1, 0:TT], sb["onesc"][:, 0:1],
                         xnb[:, dc * TT:(dc + 1) * TT],
                         start=(dc == 0), stop=(dc == DC - 1))
            for dc in range(DC):
                last = t.matmul(pr[1][0:1, 0:TT], sb["onesc"][:, 0:1],
                                xsq[:, dc * TT:(dc + 1) * TT],
                                start=(dc == 0), stop=(dc == DC - 1))
            sems["sPE"].inc(last)
        @block.vector
        def _(v):
            v.wait_ge(sems["sPE"].h, sems["sPE"].n)
            v.drain()
            v.tensor_scalar(sb["rowA"][0:1, 0:TT], pr[0][0:1, 0:TT], 1.0 / D, None, ALU.mult)
            v.tensor_scalar(sb["rowB"][0:1, 0:TT], pr[1][0:1, 0:TT], 1.0 / D, None, ALU.mult)
            v.drain()
            v.tensor_tensor(sb["rowC"][0:1, 0:TT], sb["rowA"][0:1, 0:TT],
                            sb["rowA"][0:1, 0:TT], ALU.mult)
            v.drain()
            v.tensor_tensor(sb["rowB"][0:1, 0:TT], sb["rowB"][0:1, 0:TT],
                            sb["rowC"][0:1, 0:TT], ALU.subtract)
            v.drain()
            sems["sDVE"].inc(v.tensor_scalar(sb["rowB"][0:1, 0:TT], sb["rowB"][0:1, 0:TT],
                                             EPS, None, ALU.add))
        self.pr_guard = (sems["sDVE"], sems["sDVE"].n)
        @block.scalar
        def _(s):
            s.wait_ge(sems["sDVE"].h, sems["sDVE"].n)
            s.drain()
            s.activation(sb["rowB"][0:1, 0:TT], sb["rowB"][0:1, 0:TT], AF.Ln)
            s.drain()
            sems["sACT"].inc(s.activation(sb["rowB"][0:1, 0:TT], sb["rowB"][0:1, 0:TT],
                                          AF.Exp, scale=-0.5))
        @block.tensor
        def _(t):
            t.wait_ge(sems["sACT"].h, sems["sACT"].n)
            bm = self.take_bank(t)
            br = self.take_bank(t)
            self._mb, self._rb = bm, br
            t.matmul(ps[bm][:, 0:TT], sb["ones1"][0:1, :], sb["rowA"][0:1, 0:TT],
                     start=True, stop=True)
            sems["sPE"].inc(t.matmul(ps[br][:, 0:TT], sb["ones1"][0:1, :],
                                     sb["rowB"][0:1, 0:TT], start=True, stop=True))
        @block.vector
        def _(v):
            v.wait_ge(sems["sPE"].h, sems["sPE"].n)
            v.drain()
            v.tensor_copy(sb["mr_b"][:, 0:TT], ps[self._mb][:, 0:TT])
            sems["sDVE"].inc(v.tensor_copy(sb["mr_b"][:, TT:2 * TT], ps[self._rb][:, 0:TT]))
        self.bank_guard[self._mb] = (sems["sDVE"], sems["sDVE"].n)
        self.bank_guard[self._rb] = (sems["sDVE"], sems["sDVE"].n)
        @block.vector
        def _(v):
            v.drain()
            for dc in range(DC):
                v.tensor_tensor(sb["xtmp"][:, 0:TT], self.xres(dc, t0, TT),
                                sb["mr_b"][:, 0:TT], ALU.subtract)
                v.drain()
                v.tensor_tensor(xnb[:, dc * TT:(dc + 1) * TT], sb["xtmp"][:, 0:TT],
                                sb["mr_b"][:, TT:2 * TT], ALU.mult)
                v.drain()
            sems["sDVE"].inc(v.tensor_copy(xnb[:, 0:1], xnb[:, 0:1]))
        self.xn_ready = sems["sDVE"].n

    def emit_l0_attn(self, tile):
        c, block = self.c, self.block
        sems, sb, dram = self.sems, self.sb, self.dram
        DC, TT, TC = c["DC"], c["TT"], c["TC"]
        t0 = tile * TT
        ps = self.ps
        wdec_ready = self.cast_ready["wdec"]

        for dcp in range(DC // 2):
            slot = self.slot()
            src = dram["c_wdec"][dcp].rearrange("p (kc j) -> p kc j", j=256)
            buf, wcnt = self.load_slab(slot, src, ready=wdec_ready)
            for sub in range(2):
                dc = dcp * 2 + sub
                @block.tensor
                def _(t, dc=dc, sub=sub, buf=buf, wcnt=wcnt, slot=slot):
                    dW = sems["dWA" if slot == "A" else "dWB"]
                    t.wait_ge(dW.h, wcnt)
                    t.wait_ge(sems["sDVE"].h, self.xn_ready)
                    bk = self.take_bank(t)
                    self._bk = bk
                    for kc in range(DC):
                        last = t.matmul(ps[bk][:, 0:TT],
                                        buf[:, kc * 256 + sub * 128: kc * 256 + sub * 128 + 128],
                                        sb["xnb"][:, kc * TT:(kc + 1) * TT],
                                        start=(kc == 0), stop=(kc == DC - 1))
                    sems["sPE"].inc(last)
                self.wguard[slot] = (sems["sPE"], sems["sPE"].n)
                @block.vector
                def _(v, dc=dc):
                    v.wait_ge(sems["sPE"].h, sems["sPE"].n)
                    v.drain()
                    sems["sDVE"].inc(v.scalar_tensor_tensor(
                        self.xres(dc, t0, TT), ps[self._bk][:, 0:TT],
                        sb["bias_sb"][:, dc:dc + 1], self.xres(dc, t0, TT),
                        ALU.add, ALU.add))
                self.bank_guard[self._bk] = (sems["sDVE"], sems["sDVE"].n)

    def emit_router(self, tile, layer):
        c, block = self.c, self.block
        sems, sb = self.sems, self.sb
        DC, TT, CPT = c["DC"], c["TT"], c["CPT"]
        ps = self.ps
        wr = sb["wr0_sb"] if layer == 0 else sb["wr1_sb"]

        @block.tensor
        def _(t):
            t.wait_ge(sems["sDVE"].h, self.xn_ready)
            rb = self.take_bank(t)
            self._rbank = rb
            for tsub in range(CPT):
                for dc in range(DC):
                    last = t.matmul(ps[rb][:, tsub * 8:(tsub + 1) * 8],
                                    sb["xnb"][:, dc * TT + tsub * 128: dc * TT + tsub * 128 + 128],
                                    wr[:, dc * 8:(dc + 1) * 8],
                                    start=(dc == 0), stop=(dc == DC - 1))
            sems["sPE"].inc(last)
        @block.vector
        def _(v):
            v.wait_ge(sems["sPE"].h, sems["sPE"].n)
            v.drain()
            sems["sDVE"].inc(v.tensor_copy(sb["rlog"][:, 0:CPT * 8], ps[self._rbank][:, 0:CPT * 8]))
        self.bank_guard[self._rbank] = (sems["sDVE"], sems["sDVE"].n)

        for tsub in range(CPT):
            sl = slice(tsub * 8, tsub * 8 + 8)
            if layer == 0:
                @block.vector
                def _(v, sl=sl):
                    v.drain()
                    sems["sDVE"].inc(v.reduce_max(sb["r_a"][:], sb["rlog"][:, sl],
                                                  AX.X, negate=True))
                @block.scalar
                def _(s, sl=sl):
                    s.wait_ge(sems["sDVE"].h, sems["sDVE"].n)
                    s.drain()
                    sems["sACT"].inc(s.activation(sb["escr"][:], sb["rlog"][:, sl],
                                                  AF.Exp, bias=sb["r_a"][:, 0:1]))
                @block.vector
                def _(v, tsub=tsub, sl=sl):
                    v.wait_ge(sems["sACT"].h, sems["sACT"].n)
                    v.drain()
                    v.tensor_reduce(sb["r_b"][:], sb["escr"][:], AX.X, ALU.add)
                    v.drain()
                    v.reciprocal(sb["r_c"][:], sb["r_b"][:])
                    v.drain()
                    v.tensor_scalar(sb["escr"][:], sb["escr"][:], sb["r_c"][:, 0:1],
                                    None, ALU.mult)
                    v.drain()
                    v.reduce_max(sb["r_a"][:], sb["escr"][:], AX.X)
                    v.drain()
                    v.tensor_scalar(sb["escr2"][:], sb["escr"][:], sb["r_a"][:, 0:1],
                                    None, ALU.is_equal)
                    v.drain()
                    v.scalar_tensor_tensor(sb["e8"][:], sb["escr2"][:], NEG_BIG,
                                           sb["escr"][:], ALU.mult, ALU.add)
                    v.drain()
                    v.reduce_max(sb["r_b"][:], sb["e8"][:], AX.X)
                    v.drain()
                    v.tensor_scalar(sb["e8"][:], sb["escr"][:], sb["r_b"][:, 0:1],
                                    None, ALU.is_ge)
                    v.drain()
                    sems["sDVE"].inc(v.tensor_tensor(sb["cb_tm"][:, sl], sb["escr"][:],
                                                     sb["e8"][:], ALU.mult))
            else:
                @block.vector
                def _(v, sl=sl):
                    v.drain()
                    v.reduce_max(sb["r_a"][:], sb["rlog"][:, sl], AX.X)
                    v.drain()
                    v.tensor_scalar(sb["escr2"][:], sb["rlog"][:, sl], sb["r_a"][:, 0:1],
                                    None, ALU.is_equal)
                    v.drain()
                    v.scalar_tensor_tensor(sb["e8"][:], sb["escr2"][:], NEG_BIG,
                                           sb["rlog"][:, sl], ALU.mult, ALU.add)
                    v.drain()
                    v.reduce_max(sb["r_b"][:], sb["e8"][:], AX.X)
                    v.drain()
                    v.tensor_scalar(sb["escr"][:], sb["rlog"][:, sl], sb["r_b"][:, 0:1],
                                    None, ALU.is_equal)
                    v.drain()
                    sems["sDVE"].inc(v.tensor_tensor(sb["r_c"][:], sb["r_a"][:],
                                                     sb["r_b"][:], ALU.subtract))
                @block.scalar
                def _(s):
                    s.wait_ge(sems["sDVE"].h, sems["sDVE"].n)
                    s.drain()
                    sems["sACT"].inc(s.activation(sb["r_d"][:], sb["r_c"][:], AF.Exp,
                                                  scale=-1.0))
                @block.vector
                def _(v, sl=sl):
                    v.wait_ge(sems["sACT"].h, sems["sACT"].n)
                    v.drain()
                    v.tensor_scalar(sb["r_d"][:], sb["r_d"][:], 1.0, None, ALU.add)
                    v.drain()
                    v.reciprocal(sb["r_c"][:], sb["r_d"][:])
                    v.drain()
                    v.tensor_scalar(sb["r_d"][:], sb["r_c"][:], -1.0, 1.0,
                                    ALU.mult, ALU.add)
                    v.drain()
                    v.tensor_scalar(sb["escr2"][:], sb["escr2"][:], sb["r_c"][:, 0:1],
                                    None, ALU.mult)
                    v.drain()
                    sems["sDVE"].inc(v.scalar_tensor_tensor(
                        sb["cb_tm"][:, sl], sb["escr"][:], sb["r_d"][:, 0:1],
                        sb["escr2"][:], ALU.mult, ALU.add))
        @block.tensor
        def _(t):
            t.wait_ge(sems["sDVE"].h, sems["sDVE"].n)
            cbk = self.take_bank(t)
            self._cbk = cbk
            for tsub in range(CPT):
                last = t.transpose(ps[cbk][0:8, tsub * 128:(tsub + 1) * 128],
                                   sb["cb_tm"][:, tsub * 8:(tsub + 1) * 8], sb["ident"][:])
            sems["sPE"].inc(last)
        @block.vector
        def _(v):
            v.wait_ge(sems["sPE"].h, sems["sPE"].n)
            v.drain()
            sems["sDVE"].inc(v.tensor_copy(sb["cb_fm"][0:8, 0:TT],
                                           ps[self._cbk][0:8, 0:CPT * 128]))
        self.bank_guard[self._cbk] = (sems["sDVE"], sems["sDVE"].n)
        self.cb_ready = sems["sDVE"].n

    def emit_cb_broadcast(self, e):
        sems, sb, block = self.sems, self.sb, self.block
        TT = self.c["TT"]
        ps = self.ps

        @block.tensor
        def _(t):
            t.wait_ge(sems["sDVE"].h, self.cb_ready)
            bk = self.take_bank(t)
            self._cbb = bk
            sems["sPE"].inc(t.matmul(ps[bk][:, 0:TT],
                                     sb["selmat"][0:8, e * 128:(e + 1) * 128],
                                     sb["cb_fm"][0:8, 0:TT], start=True, stop=True))
        @block.vector
        def _(v):
            v.wait_ge(sems["sPE"].h, sems["sPE"].n)
            v.drain()
            sems["sDVE"].inc(v.tensor_copy(sb["cb_b"][:, 0:TT], ps[self._cbb][:, 0:TT]))
        self.bank_guard[self._cbb] = (sems["sDVE"], sems["sDVE"].n)
        self.cbb_ready = sems["sDVE"].n

    def emit_gate_up_down(self, tile, gsrc, usrc, dsrc, nhc, with_cb, ready):
        c, block = self.c, self.block
        sems, sb = self.sems, self.sb
        DC, TT, TC = c["DC"], c["TT"], c["TC"]
        t0 = tile * TT
        ps = self.ps
        H = sb["H"]
        pe0, act0, dve0 = sems["sPE"].n, sems["sACT"].n, sems["sDVE"].n

        wcnt = {}
        slots = {}
        for hc in range(nhc):
            slot = self.slot()
            slots[hc] = slot
            gs = gsrc[hc].rearrange("p (dc j) -> p dc j", j=128)
            us = usrc[hc].rearrange("p (dc j) -> p dc j", j=128)
            self.load_slab(slot, gs, off=0, ready=ready if hc == 0 else None)
            _, wcnt[hc] = self.load_slab(slot, us, off=DC * 128, first=False)
            self.wguard[slot] = (sems["sPE"], pe0 + hc + 1)

        banks = {}
        @block.tensor
        def _(t):
            t.wait_ge(sems["sDVE"].h, self.xn_ready)
            for hc in range(nhc):
                slot = slots[hc]
                buf = sb["wA" if slot == "A" else "wB"]
                t.wait_ge(sems["dWA" if slot == "A" else "dWB"].h, wcnt[hc])
                bg = self.take_bank(t)
                bu = self.take_bank(t)
                banks[hc] = (bg, bu)
                for dc in range(DC):
                    t.matmul(ps[bg][:, 0:TT], buf[:, dc * 128:(dc + 1) * 128],
                             sb["xnb"][:, dc * TT:(dc + 1) * TT],
                             start=(dc == 0), stop=(dc == DC - 1))
                for dc in range(DC):
                    last = t.matmul(ps[bu][:, 0:TT],
                                    buf[:, DC * 128 + dc * 128: DC * 128 + (dc + 1) * 128],
                                    sb["xnb"][:, dc * TT:(dc + 1) * TT],
                                    start=(dc == 0), stop=(dc == DC - 1))
                last.then_inc(sems["sPE"].h, 1)
                self.bank_guard[bg] = (sems["sACT"], act0 + hc + 1)
                self.bank_guard[bu] = (sems["sDVE"], dve0 + 2 * (hc + 1))
        sems["sPE"].n = pe0 + nhc

        @block.scalar
        def _(s):
            for hc in range(nhc):
                s.wait_ge(sems["sPE"].h, pe0 + hc + 1)
                s.drain()
                s.activation(H[:, hc * TT:(hc + 1) * TT], ps[banks[hc][0]][:, 0:TT],
                             AF.Silu).then_inc(sems["sACT"].h, 1)
        sems["sACT"].n = act0 + nhc

        @block.vector
        def _(v):
            if with_cb:
                v.wait_ge(sems["sDVE"].h, self.cbb_ready)
            for hc in range(nhc):
                v.wait_ge(sems["sPE"].h, pe0 + hc + 1)
                v.wait_ge(sems["sACT"].h, act0 + hc + 1)
                v.drain()
                if with_cb:
                    v.tensor_tensor(sb["ue"][:, 0:TT], ps[banks[hc][1]][:, 0:TT],
                                    sb["cb_b"][:, 0:TT], ALU.mult
                                    ).then_inc(sems["sDVE"].h, 1)
                else:
                    v.tensor_copy(sb["ue"][:, 0:TT], ps[banks[hc][1]][:, 0:TT]
                                  ).then_inc(sems["sDVE"].h, 1)
                v.drain()
                v.tensor_tensor(H[:, hc * TT:(hc + 1) * TT],
                                H[:, hc * TT:(hc + 1) * TT],
                                sb["ue"][:, 0:TT], ALU.mult).then_inc(sems["sDVE"].h, 1)
        sems["sDVE"].n = dve0 + 2 * nhc
        self.h_ready = sems["sDVE"].n

        pe1, dve1 = sems["sPE"].n, sems["sDVE"].n
        dwcnt = {}
        dslots = {}
        for dc in range(DC):
            slot = self.slot()
            dslots[dc] = slot
            dsl = dsrc[dc].rearrange("p (hc j) -> p hc j", j=128)
            _, dwcnt[dc] = self.load_slab(slot, dsl)
            self.wguard[slot] = (sems["sPE"], pe1 + dc + 1)
        dbanks = {}
        @block.tensor
        def _(t):
            t.wait_ge(sems["sDVE"].h, self.h_ready)
            for dc in range(DC):
                slot = dslots[dc]
                buf = sb["wA" if slot == "A" else "wB"]
                t.wait_ge(sems["dWA" if slot == "A" else "dWB"].h, dwcnt[dc])
                bk = self.take_bank(t)
                dbanks[dc] = bk
                for hc in range(nhc):
                    last = t.matmul(ps[bk][:, 0:TT], buf[:, hc * 128:(hc + 1) * 128],
                                    H[:, hc * TT:(hc + 1) * TT],
                                    start=(hc == 0), stop=(hc == nhc - 1))
                last.then_inc(sems["sPE"].h, 1)
                self.bank_guard[bk] = (sems["sDVE"], dve1 + dc + 1)
        sems["sPE"].n = pe1 + DC
        @block.vector
        def _(v):
            for dc in range(DC):
                v.wait_ge(sems["sPE"].h, pe1 + dc + 1)
                v.drain()
                v.scalar_tensor_tensor(
                    self.xres(dc, t0, TT), ps[dbanks[dc]][:, 0:TT], 0.0,
                    self.xres(dc, t0, TT), ALU.add, ALU.add).then_inc(sems["sDVE"].h, 1)
        sems["sDVE"].n = dve1 + DC

    def emit_l0_tile(self, tile, guards=()):
        c, dram = self.c, self.dram
        self.emit_stats_xn(tile, extra_guards=guards)
        self.emit_router(tile, layer=0)
        self.emit_gate_up_down(tile, dram["c_wsg"], dram["c_wsu"], dram["c_wsd"],
                               c["SHC"], with_cb=False, ready=self.cast_ready["ws"])
        for e in range(c["E"]):
            self.emit_cb_broadcast(e)
            self.emit_gate_up_down(tile, dram["c_weg"][e], dram["c_weu"][e],
                                   dram["c_wed"][e], c["EHC"], with_cb=True,
                                   ready=self.cast_ready[f"we{e}"])
        self.emit_l0_attn(tile)
        self.xn_guard = (self.sems["sPE"], self.sems["sPE"].n)

    def emit_kvq_tile(self, tile):
        c, block = self.c, self.block
        sems, sb, dram = self.sems, self.sb, self.dram
        DC, TT, TC, NH, CPT, D = c["DC"], c["TT"], c["TC"], c["NH"], c["CPT"], c["D"]
        ps = self.ps
        qk_ready = self.cast_ready["qkvo"]
        nh_half = DC // 2
        nhb = 512 // 128

        units = []
        for h in range(NH):
            units.append(("K", h))
            units.append(("Q", h))
        for eb in range(D // 512):
            for tsub in range(CPT):
                units.append(("V", (eb, tsub)))

        pe0 = sems["sPE"].n
        wcnt = {}
        uslots = {}
        vslabs = {}
        ui = 0
        for kind, arg in units:
            if kind in ("K", "Q"):
                slot = self.slot()
                uslots[ui] = slot
                src = dram["c_kw" if kind == "K" else "c_qw"][arg].rearrange(
                    "p (dc j) -> p dc j", j=128)
                _, wcnt[ui] = self.load_slab(
                    slot, src, ready=qk_ready if (ui == 0 and tile == 0) else None)
                self.wguard[slot] = (sems["sPE"], pe0 + ui + 1)
            else:
                eb, tsub = arg
                if tsub == 0:
                    slot = self.slot()
                    srcA = dram["c_vw"][eb, 0].rearrange("p (di e) -> p di e", e=512)
                    _, wA_c = self.load_slab(slot, srcA)
                    slotB = self.slot()
                    srcB = dram["c_vw"][eb, 1].rearrange("p (di e) -> p di e", e=512)
                    _, wB_c = self.load_slab(slotB, srcB)
                    vslabs[eb] = (slot, slotB, wA_c, wB_c)
                    self.wguard[slot] = (sems["sPE"], pe0 + ui + CPT)
                    self.wguard[slotB] = (sems["sPE"], pe0 + ui + CPT)
                uslots[ui] = vslabs[eb]
            ui += 1

        @block.tensor
        def _(t):
            t.wait_ge(sems["sDVE"].h, self.xn_ready)
            ui = 0
            for kind, arg in units:
                bk = self.take_bank(t)
                self._kv_banks = getattr(self, "_kv_banks", {})
                self._kv_banks[ui] = bk
                if kind in ("K", "Q"):
                    slot = uslots[ui]
                    buf = sb["wA" if slot == "A" else "wB"]
                    t.wait_ge(sems["dWA" if slot == "A" else "dWB"].h, wcnt[ui])
                    for dc in range(DC):
                        last = t.matmul(ps[bk][:, 0:TT], buf[:, dc * 128:(dc + 1) * 128],
                                        sb["xnb"][:, dc * TT:(dc + 1) * TT],
                                        start=(dc == 0), stop=(dc == DC - 1))
                else:
                    eb, tsub = arg
                    slot, slotB, wA_c, wB_c = uslots[ui]
                    t.wait_ge(sems["dWA" if slot == "A" else "dWB"].h, wA_c)
                    t.wait_ge(sems["dWA" if slotB == "A" else "dWB"].h, wB_c)
                    for dc in range(DC):
                        sl = slot if dc < nh_half else slotB
                        buf = sb["wA" if sl == "A" else "wB"]
                        di = dc if dc < nh_half else dc - nh_half
                        last = t.matmul(ps[bk][:, 0:512],
                                        sb["xnb"][:, dc * TT + tsub * 128: dc * TT + tsub * 128 + 128],
                                        buf[:, di * 512:(di + 1) * 512],
                                        start=(dc == 0), stop=(dc == DC - 1))
                last.then_inc(sems["sPE"].h, 1)
                self.bank_guard[bk] = (sems["sDVE"], sems["sDVE"].n + ui + 1)
                ui += 1
        n_units = len(units)
        pe_end = pe0 + n_units
        sems["sPE"].n = pe_end

        dve0 = sems["sDVE"].n
        ktn = {}
        ktsem_cnt = {}
        for ui in range(n_units):
            i = (self._kvt_rr if hasattr(self, "_kvt_rr") else 0)
            name = "ktmpA" if (ui % 2 == 0) else "ktmpB"
            ktn[ui] = name
        kt_base = {"ktmpA": sems["dKtA"].n, "ktmpB": sems["dKtB"].n}
        kt_cnt = {"ktmpA": 0, "ktmpB": 0}
        @block.vector
        def _(v):
            for ui, (kind, arg) in enumerate(units):
                ncols = TT if kind in ("K", "Q") else 512
                ktsem = sems["dKtA" if ui % 2 == 0 else "dKtB"]
                v.wait_ge(ktsem.h, kt_base[ktn[ui]] + 16 * kt_cnt[ktn[ui]])
                kt_cnt[ktn[ui]] += 1
                v.wait_ge(sems["sPE"].h, pe0 + ui + 1)
                v.drain()
                v.tensor_copy(sb[ktn[ui]][:, 0:ncols],
                              ps[self._kv_banks[ui]][:, 0:ncols]).then_inc(sems["sDVE"].h, 1)
        sems["sDVE"].n = dve0 + n_units

        @block.gpsimd
        def _(g):
            for ui, (kind, arg) in enumerate(units):
                ktsem = sems["dKtA" if ui % 2 == 0 else "dKtB"]
                g.wait_ge(sems["sDVE"].h, dve0 + ui + 1)
                if kind == "K":
                    dst = dram["kvloc"][arg * 128:(arg + 1) * 128, tile * TT:(tile + 1) * TT]
                    ktsem.inc(g.dma_start(out=dst, in_=sb[ktn[ui]][:, 0:TT]), 16)
                elif kind == "Q":
                    dst = dram["qloc"][arg * 128:(arg + 1) * 128, tile * TT:(tile + 1) * TT]
                    ktsem.inc(g.dma_start(out=dst, in_=sb[ktn[ui]][:, 0:TT]), 16)
                else:
                    eb, tsub = arg
                    h0 = eb * nhb
                    cloc = tile * CPT + tsub
                    dst = bass.AP(
                        dram["kvloc"].tensor,
                        (NH * 128 + h0 * 128) * TC + cloc * 128,
                        [[TC, 128], [128 * TC, nhb], [1, 128]])
                    ktsem.inc(g.dma_start(
                        out=dst,
                        in_=sb[ktn[ui]][:, 0:512].rearrange("p (hh hd) -> p hh hd", hd=128)), 16)
        self.xn_guard = (sems["sPE"], sems["sPE"].n)

    def emit_collective(self):
        sems, dram, block = self.sems, self.dram, self.block

        @block.gpsimd
        def _(g):
            g.wait_ge(sems["dKtA"].h, sems["dKtA"].n)
            g.wait_ge(sems["dKtB"].h, sems["dKtB"].n)
            sems["sCC"].inc(g.collective_compute(
                "AllGather", ALU.bypass, replica_groups=GROUPS,
                ins=[dram["kvloc"]], outs=[dram["kvg"]]))

    def head_sched(self, parity):
        tiles = EVEN_TILES if parity == 0 else ODD_TILES
        CPT, TT = self.c["CPT"], self.c["TT"]
        out = []
        for s in range(4):
            gt = tiles[s]
            q0 = gt * TT
            lst = []
            for G in range(CPT * (gt + 1)):
                d = 128 * G - q0
                lst.append((self.bufcol[G], (TT - d) if d >= 0 else None))
            out.append(lst)
        return out

    def emit_attention(self):
        c, block, nc = self.c, self.block, self.nc
        sems, sb, dram = self.sems, self.sb, self.dram
        DC, TT, TC, NH, CPT, D = c["DC"], c["TT"], c["TC"], c["NH"], c["CPT"], c["D"]
        KCH = c["KCH"]
        ps, pr = self.ps, self.pr
        nhh = NH // 2
        sc = [self.head_sched(0), self.head_sched(1)]
        inv_sqrt_hd = 1.0 / math.sqrt(c["HD"])

        Ks = [sb["KsA"], sb["KsB"]]
        Vs = [sb["VsA"], sb["VsB"]]
        Qs = [sb["QsA"], sb["QsB"]]
        dKs = [sems["dKsA"], sems["dKsB"]]
        dVs = [sems["dVsA"], sems["dVsB"]]
        dQs = [sems["dQsA"], sems["dQsB"]]

        @block.vector
        def _(v):
            for g in self.scope_guards2:
                v.wait_ge(g[0].h, g[1])
            v.drain()
            sems["sDVE"].inc(v.memset(sb["Mbig"][:], 1.0))
        @block.gpsimd
        def _(g):
            g.wait_ge(sems["sDVE"].h, sems["sDVE"].n)
            sems["sGP"].inc(g.affine_select(sb["Mbig"][:], sb["Mbig"][:], [[1, 3 * TT]],
                                            ALU.is_ge, 0.0, base=-TT, channel_multiplier=-1))
        mbig_ready = (sems["sGP"], sems["sGP"].n)

        pe0, act0, dve0 = sems["sPE"].n, sems["sACT"].n, sems["sDVE"].n

        def nmax_s(s):
            return max(len(sc[0][s]), len(sc[1][s]))

        def bases(par, s, h):
            n_max = nmax_s(s)
            pe, act, dve = pe0, act0, dve0
            for s2 in range(s):
                nm2 = nmax_s(s2)
                pe += NH * (3 * nm2 + 2) + 2 * DC
                act += NH * (1 + nm2)
                dve += NH * (CPT + 6) + 2 * DC
            pe += h * (3 * n_max + 2) + (DC if h > nhh - 1 else 0)
            act += h * (1 + n_max)
            dve += h * (CPT + 6) + (DC if h > nhh - 1 else 0)
            return pe, act, dve, n_max

        def pe_head_end(s, h):
            pe, _, _, n_max = bases(0, s, h)
            return pe + 3 * n_max + 2

        last_use = {}
        load_cnt = {}
        oslab = {}
        first_load = True

        for s in range(4):
            for h in range(NH):
                sl = h % 2
                @block.sync
                def _(sy, s=s, h=h, sl=sl, first=first_load):
                    if first:
                        for g in self.scope_guards2:
                            sy.wait_ge(g[0].h, g[1])
                        sy.wait_ge(sems["sCC"].h, 1)
                        sy.wait_ge(sems["sGP"].h, mbig_ready[1])
                    lu = last_use.get(("K", sl))
                    if lu is not None:
                        sy.wait_ge(sems["sPE"].h, lu)
                    dKs[sl].inc(sy.dma_start(
                        out=Ks[sl][:, 0:TC], in_=dram["kvg"][h * 128:(h + 1) * 128, :]), 16)
                    dKs[sl].inc(sy.dma_start(
                        out=Ks[sl][:, TC:2 * TC],
                        in_=dram["kvg"][2 * D + h * 128: 2 * D + (h + 1) * 128, :]), 16)
                    dVs[sl].inc(sy.dma_start(
                        out=Vs[sl][:, 0:TC],
                        in_=dram["kvg"][D + h * 128: D + (h + 1) * 128, :]), 16)
                    dVs[sl].inc(sy.dma_start(
                        out=Vs[sl][:, TC:2 * TC],
                        in_=dram["kvg"][3 * D + h * 128: 3 * D + (h + 1) * 128, :]), 16)
                    dQs[sl].inc(sy.dma_start(
                        out=Qs[sl][:, 0:TT],
                        in_=dram["qloc"][h * 128:(h + 1) * 128, s * TT:(s + 1) * TT]), 16)
                first_load = False
                load_cnt[("K", s, h)] = dKs[sl].n
                load_cnt[("V", s, h)] = dVs[sl].n
                load_cnt[("Q", s, h)] = dQs[sl].n
                last_use[("K", sl)] = pe_head_end(s, h)
                if h == nhh - 1 or h == NH - 1:
                    half = 0 if h == nhh - 1 else 1
                    nq = DC // 4
                    pe_done_half = pe_head_end(s, h)
                    for q in range(nq):
                        slot = self.slot()
                        osrc = dram["c_ow"][half, q].rearrange("p (kc j) -> p kc j", j=512)
                        gspec = self.wguard[slot]
                        @block.sync
                        def _(sy, slot=slot, osrc=osrc, gspec=gspec):
                            if gspec is not None:
                                sy.wait_ge(gspec[0].h, gspec[1])
                            dW = sems["dWA" if slot == "A" else "dWB"]
                            a, b = int(osrc.shape[1]), int(osrc.shape[2])
                            buf = sb["wA" if slot == "A" else "wB"]
                            dest = buf[:, 0:a * b].rearrange("p (a b) -> p a b", b=b)
                            dW.inc(sy.dma_start(out=dest, in_=osrc), 16)
                        dWs = sems["dWA" if slot == "A" else "dWB"]
                        oslab[(s, half, q)] = (slot, sb["wA" if slot == "A" else "wB"], dWs.n)
                        self.wguard[slot] = (sems["sPE"], pe_done_half + 4 * (q + 1))

        def slot_end(s):
            pe, act, dve, _ = bases(0, s, NH - 1)
            nm = max(len(sc[0][s]), len(sc[1][s]))
            return (pe + (3 * nm + 2) + DC, act + (1 + nm), dve + (CPT + 6) + DC)

        for s in range(4):
            @block.tensor
            def _(t, s=s):
                def body(par):
                    for h in range(NH):
                        pe_b, act_b, dve_b, n_max = bases(par, s, h)
                        n = len(sc[par][s])
                        t.wait_ge(dQs[h % 2].h, load_cnt[("Q", s, h)])
                        t.wait_ge(sems["sACT"].h, act_b)
                        t.matmul(ps[3][:, 0:TT], sb["gwT_sb"][:], Qs[h % 2][:, 0:TT],
                                 start=True, stop=True, skip_group_check=True
                                 ).then_inc(sems["sPE"].h, 1)
                        t.wait_ge(dKs[h % 2].h, load_cnt[("K", s, h)])
                        t.wait_ge(dVs[h % 2].h, load_cnt[("V", s, h)])

                        def score(j):
                            bc, off = sc[par][s][j]
                            if j >= 2:
                                t.wait_ge(sems["sACT"].h, act_b + 2 + (j - 2))
                            else:
                                t.wait_ge(sems["sACT"].h, act_b)
                            t.matmul(ps[j % 2][:, 0:TT],
                                     Ks[h % 2][:, bc * 128:(bc + 1) * 128],
                                     Qs[h % 2][:, 0:TT], start=True, stop=True,
                                     skip_group_check=True).then_inc(sems["sPE"].h, 1)

                        def avsum(j):
                            bc, off = sc[par][s][j]
                            t.wait_ge(sems["sACT"].h,
                                      act_b + 2 + j if j < n - 1 else act_b + 1 + n_max)
                            if off is not None:
                                jd = j - (n - CPT)
                                t.wait_ge(sems["sDVE"].h, dve_b + 3 + jd)
                            if j == 0:
                                t.wait_ge(sems["sDVE"].h, dve_b)
                            t.matmul(ps[2][:, 0:TT],
                                     Vs[h % 2][:, bc * 128:(bc + 1) * 128],
                                     sb["PT0" if j % 2 == 0 else "PT1"][:, 0:TT],
                                     start=(j == 0), stop=(j == n - 1),
                                     skip_group_check=True).then_inc(sems["sPE"].h, 1)
                            t.matmul(pr[0][0:1, 0:TT], sb["onesc"][:, 0:1],
                                     sb["PT0" if j % 2 == 0 else "PT1"][:, 0:TT],
                                     start=(j == 0), stop=(j == n - 1),
                                     skip_group_check=True).then_inc(
                                sems["sPE"].h, 1 + (3 * (n_max - n) if j == n - 1 else 0))

                        score(0)
                        for j in range(1, n):
                            score(j)
                            avsum(j - 1)
                        avsum(n - 1)
                        t.wait_ge(sems["sDVE"].h, dve_b + 3 + CPT)
                        t.matmul(ps[4][:, 0:TT], sb["ones1"][0:1, :],
                                 sb["rowA"][0:1, 0:TT], start=True, stop=True,
                                 skip_group_check=True).then_inc(sems["sPE"].h, 1)
                        if h == nhh - 1 or h == NH - 1:
                            half = 0 if h == nhh - 1 else 1
                            pe_done = pe_b + 3 * n_max + 2
                            dve_done = dve_b + CPT + 6
                            for dc in range(DC):
                                slot, buf, wcnt = oslab[(s, half, dc // 4)]
                                dsub = dc % 4
                                t.wait_ge(sems["dWA" if slot == "A" else "dWB"].h, wcnt)
                                t.wait_ge(sems["sDVE"].h, dve_done)
                                bk = 5 if dc % 2 == 0 else 4
                                if dc >= 2:
                                    t.wait_ge(sems["sDVE"].h, dve_done + dc - 1)
                                for kc in range(nhh):
                                    last = t.matmul(
                                        ps[bk][:, 0:TT],
                                        buf[:, kc * 512 + dsub * 128: kc * 512 + dsub * 128 + 128],
                                        sb["out8"][:, kc * TT:(kc + 1) * TT],
                                        start=(kc == 0), stop=(kc == nhh - 1),
                                        skip_group_check=True)
                                last.then_inc(sems["sPE"].h, 1)
                with t.register(f"prt_t{s}") as prt:
                    t.cc_rank_ld(prt, replica_groups=GROUPS)
                    with t.If_eq(prt, 0):
                        body(0)
                    with t.Else():
                        body(1)

            @block.scalar
            def _(sx, s=s):
                def body(par):
                    for h in range(NH):
                        pe_b, act_b, dve_b, n_max = bases(par, s, h)
                        n = len(sc[par][s])
                        sx.wait_ge(sems["sPE"].h, pe_b + 1)
                        sx.wait_ge(sems["sDVE"].h, dve_b)
                        sx.drain()
                        sx.activation(sb["gatebuf"][:, 0:TT], ps[3][:, 0:TT], AF.Exp,
                                      scale=-1.0, bias=sb["gbneg"][:, 0:1]
                                      ).then_inc(sems["sACT"].h, 1)
                        for j in range(n):
                            pos = 2 if j == 0 else 3 * j
                            sx.wait_ge(sems["sPE"].h, pe_b + pos)
                            sx.drain()
                            a = sx.activation(sb["PT0" if j % 2 == 0 else "PT1"][:, 0:TT],
                                              ps[j % 2][:, 0:TT], AF.Exp, scale=inv_sqrt_hd)
                            a.then_inc(sems["sACT"].h,
                                       1 + (n_max - n) if j == n - 1 else 1)
                with sx.register(f"prt_s{s}") as prt:
                    sx.cc_rank_ld(prt, replica_groups=GROUPS)
                    with sx.If_eq(prt, 0):
                        body(0)
                    with sx.Else():
                        body(1)

            @block.vector
            def _(v, s=s):
                def body(par):
                    for h in range(NH):
                        pe_b, act_b, dve_b, n_max = bases(par, s, h)
                        n = len(sc[par][s])
                        v.wait_ge(sems["sACT"].h, act_b + 1)
                        v.drain()
                        v.tensor_scalar(sb["gatebuf"][:, 0:TT], sb["gatebuf"][:, 0:TT],
                                        1.0, None, ALU.add).then_inc(sems["sDVE"].h, 1)
                        v.drain()
                        v.reciprocal(sb["gatebuf"][:, 0:TT], sb["gatebuf"][:, 0:TT]
                                     ).then_inc(sems["sDVE"].h, 1)
                        for jd in range(CPT):
                            j = n - CPT + jd
                            bc, off = sc[par][s][j]
                            v.wait_ge(sems["sACT"].h,
                                      act_b + 2 + j if j < n - 1 else act_b + 1 + n_max)
                            v.drain()
                            v.tensor_tensor(sb["PT0" if j % 2 == 0 else "PT1"][:, 0:TT],
                                            sb["PT0" if j % 2 == 0 else "PT1"][:, 0:TT],
                                            sb["Mbig"][:, off:off + TT], ALU.mult
                                            ).then_inc(sems["sDVE"].h, 1)
                        v.wait_ge(sems["sPE"].h, pe_b + 3 * n_max + 1)
                        v.drain()
                        v.reciprocal(sb["rowA"][0:1, 0:TT], pr[0][0:1, 0:TT]
                                     ).then_inc(sems["sDVE"].h, 1)
                        v.wait_ge(sems["sPE"].h, pe_b + 3 * n_max + 2)
                        v.drain()
                        v.tensor_copy(sb["rb_sb"][:, 0:TT], ps[4][:, 0:TT]
                                      ).then_inc(sems["sDVE"].h, 1)
                        if h >= nhh:
                            pass
                        v.drain()
                        v.tensor_tensor(sb["out8"][:, (h % nhh) * TT:(h % nhh + 1) * TT],
                                        ps[2][:, 0:TT], sb["rb_sb"][:, 0:TT], ALU.mult
                                        ).then_inc(sems["sDVE"].h, 1)
                        v.drain()
                        v.tensor_tensor(sb["out8"][:, (h % nhh) * TT:(h % nhh + 1) * TT],
                                        sb["out8"][:, (h % nhh) * TT:(h % nhh + 1) * TT],
                                        sb["gatebuf"][:, 0:TT], ALU.mult
                                        ).then_inc(sems["sDVE"].h, 1)
                        if h == nhh - 1 or h == NH - 1:
                            pe_done = pe_b + 3 * n_max + 2
                            t0 = s * TT
                            for dc in range(DC):
                                v.wait_ge(sems["sPE"].h, pe_done + dc + 1)
                                v.drain()
                                v.scalar_tensor_tensor(
                                    self.xres(dc, t0, TT),
                                    ps[5 if dc % 2 == 0 else 4][:, 0:TT], 0.0,
                                    self.xres(dc, t0, TT), ALU.add, ALU.add
                                    ).then_inc(sems["sDVE"].h, 1)
                with v.register(f"prt_v{s}") as prt:
                    v.cc_rank_ld(prt, replica_groups=GROUPS)
                    with v.If_eq(prt, 0):
                        body(0)
                    with v.Else():
                        body(1)

            pe_e, act_e, dve_e = slot_end(s)
            sems["sPE"].n = pe_e
            sems["sACT"].n = act_e
            sems["sDVE"].n = dve_e
        for i in range(6):
            self.bank_guard[i] = (sems["sDVE"], sems["sDVE"].n)
        self.pr_guard = (sems["sDVE"], sems["sDVE"].n)

    def emit_l1_ffn_tile(self, tile):
        c, dram = self.c, self.dram
        self.emit_stats_xn(tile)
        self.emit_router(tile, layer=1)
        for e in range(c["E"]):
            self.emit_cb_broadcast(e)
            self.emit_gate_up_down(tile, dram["c_gp"][e], dram["c_up"][e],
                                   dram["c_dp"][e], c["EHC"], with_cb=True,
                                   ready=self.cast_ready[f"l1e{e}"])
        self.xn_guard = (self.sems["sPE"], self.sems["sPE"].n)

    def emit_epilogue(self):
        sems, sb, dram, block = self.sems, self.sb, self.dram, self.block
        TC = self.c["TC"]

        @block.scalar
        def _(s):
            s.wait_ge(sems["sDVE"].h, sems["sDVE"].n)
            sems["dOut"].inc(s.dma_start(out=dram[self.out_name], in_=sb["x_res"][:]), 16)
            s.wait_ge(sems["dOut"].h, sems["dOut"].n)


def build_program(cfg, layers=(0, 1), in_name="x_in", out_name="out"):
    nc = bass.Bass("TRN2", target_bir_lowering=False)
    b = B(nc, cfg, layers, in_name, out_name)
    b.declare()
    c = b.c
    TT, DC, SHC, EHC = c["TT"], c["DC"], c["SHC"], c["EHC"]
    with b.es:
        b.alloc_sems()
        b.alloc_psum()
        b.alloc_persistent()
        with nc.Block() as block:
            b.block = block
            b.emit_prologue()
            with contextlib.ExitStack() as s1:
                def S1(name, shape, dt):
                    b.sb[name] = s1.enter_context(nc.sbuf_tensor(name, shape, dt))
                hchunks = max(DC, EHC, SHC if 0 in layers else 0)
                S1("xnb", [128, DC * TT], BF16)
                S1("H", [128, hchunks * TT], BF16)
                S1("mr_b", [128, 2 * TT], F32)
                S1("rowB", [1, TT], F32)
                S1("rowC", [1, TT], F32)
                S1("ktmpA2", [128, 512], BF16)
                S1("ktmpB2", [128, 512], BF16)
                S1("ue", [128, TT], BF16)
                b.sb["ktmpA"] = b.sb["ktmpA2"]
                b.sb["ktmpB"] = b.sb["ktmpB2"]
                if 0 in layers:
                    for tile in range(4):
                        b.emit_l0_tile(tile, guards=tuple(b.scope_guards) if tile == 0 else ())
                if 1 in layers:
                    g0 = tuple(b.scope_guards) if 0 not in layers else ()
                    for tile in range(4):
                        b.emit_stats_xn(tile, extra_guards=g0 if tile == 0 else ())
                        b.emit_kvq_tile(tile)
                    b.emit_collective()
                    for tile in range(4):
                        b.emit_l1_ffn_tile(tile)
                b.scope_guards2 = [(b.sems["sPE"], b.sems["sPE"].n),
                                   (b.sems["sDVE"], b.sems["sDVE"].n),
                                   (b.sems["sACT"], b.sems["sACT"].n)]
            if 1 in layers:
                with contextlib.ExitStack() as s2:
                    def S2(name, shape, dt):
                        b.sb[name] = s2.enter_context(nc.sbuf_tensor(name, shape, dt))
                    TC, NH = c["TC"], c["NH"]
                    S2("KsA", [128, 2 * TC], BF16)
                    S2("KsB", [128, 2 * TC], BF16)
                    S2("VsA", [128, 2 * TC], BF16)
                    S2("VsB", [128, 2 * TC], BF16)
                    S2("QsA", [128, TT], BF16)
                    S2("QsB", [128, TT], BF16)
                    S2("PT0", [128, TT], BF16)
                    S2("PT1", [128, TT], BF16)
                    S2("out8", [128, (NH // 2) * TT], BF16)
                    S2("rb_sb", [128, TT], F32)
                    S2("Mbig", [128, 3 * TT], BF16)
                    b.sb["gatebuf"] = b.sb["xtmp"]
                    b.emit_attention()
            b.emit_epilogue()
    return nc


def _slabify(a, inC, outC):
    return np.ascontiguousarray(
        a.reshape(inC, 128, outC, 128).transpose(2, 1, 0, 3).reshape(outC, 128, inC * 128))


def _shard_inputs(cfg, inputs):
    c = derive(cfg)
    TT, TC, D = c["TT"], c["TC"], c["D"]
    DC, EHC, SHC, NH, E = c["DC"], c["EHC"], c["SHC"], c["NH"], c["E"]
    nhh = NH // 2
    rows = snake_rows(TT)
    x = np.asarray(inputs["L0_attn_x"])
    maps = []
    common0 = {}
    common1 = {}
    common0["w_dec"] = np.ascontiguousarray(inputs["L0_attn_weight"])
    common0["b_dec"] = np.ascontiguousarray(inputs["L0_attn_bias"]).reshape(1, -1)
    common0["dvv"] = np.ascontiguousarray(inputs["L0_attn_decay_value"])
    common0["wr0"] = np.ascontiguousarray(np.asarray(inputs["L0_ffn_W_router"]).T)
    common0["wsg"] = _slabify(np.asarray(inputs["L0_ffn_Ws_gate"]), DC, SHC)
    common0["wsu"] = _slabify(np.asarray(inputs["L0_ffn_Ws_up"]), DC, SHC)
    common0["wsd"] = _slabify(np.asarray(inputs["L0_ffn_Ws_down"]), SHC, DC)
    common0["weg"] = np.stack([_slabify(np.asarray(inputs["L0_ffn_We_gate"][e]), DC, EHC)
                               for e in range(E)])
    common0["weu"] = np.stack([_slabify(np.asarray(inputs["L0_ffn_We_up"][e]), DC, EHC)
                               for e in range(E)])
    common0["wed"] = np.stack([_slabify(np.asarray(inputs["L0_ffn_We_down"][e]), EHC, DC)
                               for e in range(E)])
    qwT = np.asarray(inputs["L1_attn_q_w"]).T
    kwT = np.asarray(inputs["L1_attn_k_w"]).T
    vwT = np.asarray(inputs["L1_attn_v_w"]).T
    owT = np.asarray(inputs["L1_attn_o_w"]).T
    common1["qw"] = _slabify(qwT, DC, NH)
    common1["kw"] = _slabify(kwT, DC, NH)
    common1["vw"] = np.ascontiguousarray(
        vwT.reshape(2, DC // 2, 128, D // 512, 512).transpose(3, 0, 2, 1, 4)
        .reshape(D // 512, 2, 128, (DC // 2) * 512))
    common1["ow"] = np.ascontiguousarray(
        owT.reshape(2, nhh, 128, DC // 4, 512).transpose(0, 3, 2, 1, 4)
        .reshape(2, DC // 4, 128, nhh * 512))
    common1["gw"] = np.ascontiguousarray(np.asarray(inputs["L1_attn_gate_w"]).T)
    common1["gb"] = np.ascontiguousarray(inputs["L1_attn_gate_b"]).reshape(1, -1)
    common1["wr1"] = np.ascontiguousarray(np.asarray(inputs["L1_ffn_gate_w"]))
    gpT = np.asarray(inputs["L1_ffn_gate_proj"]).transpose(0, 2, 1)
    upT = np.asarray(inputs["L1_ffn_up_proj"]).transpose(0, 2, 1)
    dpT = np.asarray(inputs["L1_ffn_down_proj"]).transpose(0, 2, 1)
    common1["gp"] = np.stack([_slabify(gpT[e], DC, EHC) for e in range(E)])
    common1["up"] = np.stack([_slabify(upT[e], DC, EHC) for e in range(E)])
    common1["dp"] = np.stack([_slabify(dpT[e], EHC, DC) for e in range(E)])
    for core in range(8):
        bat, par = core // 2, core % 2
        xc = x[bat][rows[par], :].T
        xr = np.ascontiguousarray(
            xc.reshape(DC, 128, TC).transpose(1, 0, 2).reshape(128, DC * TC))
        maps.append((xr, dict(common0), dict(common1)))
    return maps, rows


def _gather_output(cfg, results, rows, out_name="out"):
    c = derive(cfg)
    D, TC, TT = c["D"], c["TC"], c["TT"]
    B = 4
    S = c["S"]
    DC = c["DC"]
    out = np.empty((B, S, D), dtype=np.float32)
    for core in range(8):
        bat, par = core // 2, core % 2
        oc = np.asarray(results[core][out_name])
        od = oc.reshape(128, DC, TC).transpose(1, 0, 2).reshape(D, TC)
        out[bat][rows[par], :] = od.T
    return out


def kernel(**inputs):
    cfg = TINY_CFG if os.environ.get("MK_TINY") else FULL_CFG
    mode = os.environ.get("MK_MODE", "fused")
    maps, rows = _shard_inputs(cfg, inputs)

    if mode == "fused":
        nc = build_program(cfg, layers=(0, 1))
        in_maps = []
        for xc, c0, c1 in maps:
            m = {"x_in": xc}
            m.update(c0)
            m.update(c1)
            in_maps.append(m)
        res = run_bass_kernel_spmd(nc, in_maps, core_ids=list(range(8)))
        return _gather_output(cfg, res.results, rows)

    nc0 = build_program(cfg, layers=(0,), in_name="x_in", out_name="out")
    in_maps0 = []
    for xc, c0, c1 in maps:
        m = {"x_in": xc}
        m.update(c0)
        in_maps0.append(m)
    res0 = run_bass_kernel_spmd(nc0, in_maps0, core_ids=list(range(8)))
    nc1 = build_program(cfg, layers=(1,), in_name="x_in", out_name="out")
    in_maps1 = []
    for i, (xc, c0, c1) in enumerate(maps):
        m = {"x_in": np.ascontiguousarray(np.asarray(res0.results[i]["out"]))}
        m.update(c1)
        in_maps1.append(m)
    res1 = run_bass_kernel_spmd(nc1, in_maps1, core_ids=list(range(8)))
    return _gather_output(cfg, res1.results, rows)


# version 07
# speedup vs baseline: 1.0660x; 1.0660x over previous
import os
import math
import contextlib
import numpy as np
import ml_dtypes
import concourse.bass as bass
import concourse.mybir as mybir
from concourse.bass_utils import run_bass_kernel_spmd

F32 = mybir.dt.float32
BF16 = mybir.dt.bfloat16
AF = mybir.ActivationFunctionType
ALU = mybir.AluOpType
AX = mybir.AxisListType

GROUPS = [[0, 1], [2, 3], [4, 5], [6, 7]]
EVEN_TILES = [0, 3, 4, 7]
ODD_TILES = [1, 2, 5, 6]
NEG_BIG = -1.0e30
EPS = 1e-5

FULL_CFG = dict(D=2048, DE=1408, DS=2816, E=8, HD=128, NH=16, TT=512)
TINY_CFG = dict(D=512, DE=256, DS=512, E=8, HD=128, NH=4, TT=128)
MIDI_CFG = dict(D=1024, DE=512, DS=1024, E=8, HD=128, NH=8, TT=256)


def derive(cfg):
    c = dict(cfg)
    c["DC"] = c["D"] // 128
    c["EHC"] = c["DE"] // 128
    c["SHC"] = c["DS"] // 128
    c["TC"] = 4 * c["TT"]
    c["S"] = 8 * c["TT"]
    c["CPT"] = c["TT"] // 128
    c["KCH"] = c["S"] // 128
    assert c["NH"] * c["HD"] == c["D"]
    return c


def snake_rows(TT):
    rows = {}
    for par, tiles in ((0, EVEN_TILES), (1, ODD_TILES)):
        r = []
        for t in tiles:
            r.extend(range(t * TT, (t + 1) * TT))
        rows[par] = np.array(r, dtype=np.int64)
    return rows


def bufcol_map(CPT):
    m = {}
    for par, tiles in ((0, EVEN_TILES), (1, ODD_TILES)):
        for i, t in enumerate(tiles):
            for j in range(CPT):
                m[t * CPT + j] = par * 4 * CPT + i * CPT + j
    return m


class Sem:
    def __init__(self, handle, name):
        self.h = handle
        self.name = name
        self.n = 0

    def inc(self, instr, k=1):
        instr.then_inc(self.h, k)
        self.n += k
        return self.n


class B:

    def __init__(self, nc, cfg, layers=(0, 1), in_name="x_in", out_name="out"):
        self.nc = nc
        self.c = derive(cfg)
        self.layers = layers
        self.in_name = in_name
        self.out_name = out_name
        self.es = contextlib.ExitStack()
        self.dram = {}
        self.sb = {}
        self.sems = {}
        self.cast_ready = {}
        self.bufcol = bufcol_map(self.c["CPT"])
        self.wguard = {"A": None, "B": None}
        self.next_slot = "A"
        self.xn_guard = None
        self.scope_guards = []
        self.h_ready = 0
        self.kvq_extra_guard = None

    def declare(self):
        nc, c = self.nc, self.c
        D, DE, DS, E = c["D"], c["DE"], c["DS"], c["E"]
        TC, HD, NH = c["TC"], c["HD"], c["NH"]

        def din(name, shape):
            self.dram[name] = nc.dram_tensor(name, shape, F32, kind="ExternalInput").ap()

        def dint(name, shape, dt=BF16):
            self.dram[name] = nc.dram_tensor(name, shape, dt).ap()

        DC, EHC, SHC, NH2 = c["DC"], c["EHC"], c["SHC"], c["NH"]
        nhh = NH2 // 2
        din(self.in_name, [128, c["DC"] * TC])
        shp = {
            "wsg": [SHC, 128, DC * 128], "wsu": [SHC, 128, DC * 128],
            "wsd": [DC, 128, SHC * 128],
            "weg": [E, EHC, 128, DC * 128], "weu": [E, EHC, 128, DC * 128],
            "wed": [E, DC, 128, EHC * 128],
            "qw": [NH2, 128, DC * 128], "kw": [NH2, 128, DC * 128],
            "vw": [D // 512, 2, 128, (DC // 2) * 512],
            "ow": [2, DC // 4, 128, nhh * 512],
            "gp": [E, EHC, 128, DC * 128], "up": [E, EHC, 128, DC * 128],
            "dp": [E, DC, 128, EHC * 128],
        }
        if 0 in self.layers:
            din("w_dec", [2, D]); din("b_dec", [1, D]); din("dvv", [2, 1])
            din("wr0", [E, D])
            for nm in ("wsg", "wsu", "wsd", "weg", "weu", "wed"):
                din(nm, shp[nm]); dint("c_" + nm, shp[nm])
            dint("c_wdec", [c["DC"] // 2, 128, c["DC"] * 256])
        if 1 in self.layers:
            din("gw", [HD, HD]); din("gb", [1, HD]); din("wr1", [E, D])
            for nm in ("qw", "kw", "vw", "ow", "gp", "up", "dp"):
                din(nm, shp[nm]); dint("c_" + nm, shp[nm])
            dint("kvloc", [2 * D, TC])
            dint("kvg", [4 * D, TC])
            dint("qloc", [D, TC])
        self.dram[self.out_name] = nc.dram_tensor(self.out_name, [128, c["DC"] * TC],
                                                  F32, kind="ExternalOutput").ap()

    def alloc_sems(self):
        names = ["dIn", "dWA", "dWB", "dWd", "dSc", "dGp", "dKtA", "dKtB",
                 "sPE", "sDVE", "sACT", "sGP", "sCC", "dOut", "sAH",
                 "dKsA", "dKsB", "dVsA", "dVsB", "dQsA", "dQsB"]
        if 0 in self.layers:
            names += ["cgws"] + [f"cgwe{e}" for e in range(self.c["E"])]
        if 1 in self.layers:
            names += ["cgqk"] + [f"cgl1e{e}" for e in range(self.c["E"])]
        for n in names:
            self.sems[n] = Sem(self.es.enter_context(self.nc.semaphore(n)), n)

    def alloc_psum(self):
        self.ps = [self.es.enter_context(self.nc.psum_tensor(f"ps{i}", [128, 512], F32))
                   for i in range(6)]
        self.pr = [self.es.enter_context(self.nc.psum_tensor(f"pr{i}", [1, 512], F32))
                   for i in range(2)]
        self.bank_guard = [None] * 6
        self.pr_guard = None
        self._rr = 0

    def alloc_persistent(self):
        nc, c = self.nc, self.c
        TT, DC = c["TT"], c["DC"]

        def S(name, shape, dt):
            self.sb[name] = self.es.enter_context(nc.sbuf_tensor(name, shape, dt))

        S("x_res", [128, DC * c["TC"]], F32)
        S("wA", [128, max(4096, 2 * DC * 128)], BF16)
        S("wB", [128, max(4096, 2 * DC * 128)], BF16)
        S("ident", [128, 128], F32)
        S("onesc", [128, 1], BF16)
        S("ones1", [1, 128], F32)
        S("selmat", [8, 8 * 128], BF16)
        S("wr0_sb", [128, DC * 8], BF16)
        S("wr1_sb", [128, DC * 8], BF16)
        S("gwT_sb", [128, 128], BF16)
        S("gbneg", [128, 1], F32)
        S("bias_sb", [128, DC], F32)
        S("rlog", [128, 4 * 8], F32)
        S("escr", [128, 8], F32)
        S("escr2", [128, 8], F32)
        S("e8", [128, 8], F32)
        S("r_a", [128, 1], F32)
        S("r_b", [128, 1], F32)
        S("r_c", [128, 1], F32)
        S("r_d", [128, 1], F32)
        S("cb_tm", [128, 4 * 8], F32)
        S("cb_fm", [8, TT], BF16)
        S("cb_b", [128, TT], BF16)
        S("xtmp", [128, TT], F32)
        S("rowA", [1, TT], F32)
        S("lnd0", [128, 1], F32)
        S("lnd1", [128, 1], F32)
        S("w0col", [128, DC], F32)
        S("w1col", [128, DC], F32)
        S("stg16", [16, 128], F32)
        S("identB", [128, 128], BF16)
        S("dvt", [2, 1], F32)

    def take_bank(self, t):
        i = self._rr
        self._rr = (self._rr + 1) % 6
        g = self.bank_guard[i]
        if g is not None:
            t.wait_ge(g[0].h, g[1])
        return i

    def slot(self):
        s = self.next_slot
        self.next_slot = "B" if s == "A" else "A"
        return s

    def load_slab(self, slot, src_ap, off=0, first=True, ready=None):
        sems = self.sems
        dW = sems["dWA" if slot == "A" else "dWB"]
        buf = self.sb["wA" if slot == "A" else "wB"]
        guard = self.wguard[slot] if first else None
        a, bcols = int(src_ap.shape[1]), int(src_ap.shape[2])
        dest = buf[:, off:off + a * bcols].rearrange("p (a b) -> p a b", b=bcols)

        @self.block.sync
        def _(sy):
            if guard is not None:
                sy.wait_ge(guard[0].h, guard[1])
            if ready is not None:
                sy.wait_ge(ready[0].h, ready[1])
            dW.inc(sy.dma_start(out=dest, in_=src_ap), 16)
        return buf, dW.n

    def xres(self, dc, t0, n):
        c = self.c
        return self.sb["x_res"][:, dc * c["TC"] + t0: dc * c["TC"] + t0 + n]

    def emit_prologue(self):
        nc, c, block = self.nc, self.c, self.block
        sems, sb, dram = self.sems, self.sb, self.dram
        D, DC, TC, E = c["D"], c["DC"], c["TC"], c["E"]

        @block.scalar
        def _(s):
            sems["dIn"].inc(s.dma_start(out=sb["x_res"][:], in_=dram[self.in_name]), 16)
        @block.sync
        def _(s):
            if 0 in self.layers:
                sems["dSc"].inc(s.dma_start(out=sb["dvt"][:], in_=dram["dvv"]), 16)
                s.wait_ge(sems["dSc"].h, sems["dSc"].n)

        @block.gpsimd
        def _(g):
            g.iota(sb["ident"][:], [[1, 128]], base=0, channel_multiplier=-1,
                   allow_small_or_imprecise_dtypes=True)
            sems["sGP"].inc(g.iota(sb["wA"][0:8, 0:8 * 128], [[1, 8], [0, 128]],
                                   base=0, channel_multiplier=-1,
                                   allow_small_or_imprecise_dtypes=True))
        @block.vector
        def _(v):
            v.wait_ge(sems["sGP"].h, sems["sGP"].n)
            v.tensor_scalar(sb["ident"][:], sb["ident"][:], 0.0, None, ALU.is_equal)
            v.tensor_scalar(sb["selmat"][:], sb["wA"][0:8, 0:8 * 128], 0.0, None, ALU.is_equal)
            v.memset(sb["onesc"][:], 1.0)
            v.memset(sb["ones1"][:], 1.0)
            v.drain()
            sems["sDVE"].inc(v.tensor_copy(sb["identB"][:], sb["ident"][:]))
        self.wguard["A"] = (sems["sDVE"], sems["sDVE"].n)
        self.wguard["B"] = (sems["sDVE"], sems["sDVE"].n)

        ps = self.ps
        stage_jobs = []
        if 0 in self.layers:
            stage_jobs.append((dram["b_dec"], "bias_sb"))
            stage_jobs.append((dram["w_dec"][0:1, :], "w0col"))
            stage_jobs.append((dram["w_dec"][1:2, :], "w1col"))
        for srow, dname in stage_jobs:
            @block.sync
            def _(s, srow=srow):
                s.wait_ge(sems["sPE"].h, sems["sPE"].n)
                sems["dSc"].inc(s.dma_start(
                    out=sb["stg16"][0:DC, :],
                    in_=srow.rearrange("o (dc p) -> (o dc) p", p=128)), 16)
                s.wait_ge(sems["dSc"].h, sems["dSc"].n)
            @block.tensor
            def _(t):
                t.wait_ge(sems["dSc"].h, sems["dSc"].n)
                t.wait_ge(sems["sDVE"].h, sems["sDVE"].n)
                sems["sPE"].inc(t.transpose(ps[0][:, 0:DC], sb["stg16"][0:DC, :],
                                            sb["ident"][0:DC, 0:DC]))
            @block.vector
            def _(v, dname=dname):
                v.wait_ge(sems["sPE"].h, sems["sPE"].n)
                v.drain()
                sems["sDVE"].inc(v.tensor_copy(sb[dname][:, 0:DC], ps[0][:, 0:DC]))

        for li, dname in ((0, "wr0"), (1, "wr1")):
            if li not in self.layers:
                continue
            for dc in range(DC):
                @block.sync
                def _(s, li=li, dc=dc):
                    s.wait_ge(sems["sPE"].h, sems["sPE"].n)
                    sems["dSc"].inc(s.dma_start(
                        out=sb["stg16"][0:8, :],
                        in_=dram["wr0" if li == 0 else "wr1"][:, dc * 128:(dc + 1) * 128]), 16)
                    s.wait_ge(sems["dSc"].h, sems["dSc"].n)
                @block.tensor
                def _(t, dc=dc):
                    t.wait_ge(sems["dSc"].h, sems["dSc"].n)
                    if dc == 0:
                        t.wait_ge(sems["sDVE"].h, sems["sDVE"].n)
                    sems["sPE"].inc(t.transpose(ps[1][:, dc * 8:(dc + 1) * 8],
                                                sb["stg16"][0:8, :],
                                                sb["ident"][0:8, 0:8]))
            @block.vector
            def _(v, dname=dname):
                v.wait_ge(sems["sPE"].h, sems["sPE"].n)
                v.drain()
                sems["sDVE"].inc(v.tensor_copy(sb[dname + "_sb"][:], ps[1][:, 0:DC * 8]))
        self.bank_guard[0] = (sems["sDVE"], sems["sDVE"].n)
        self.bank_guard[1] = (sems["sDVE"], sems["sDVE"].n)

        @block.sync
        def _(s):
            if 1 in self.layers:
                s.wait_ge(sems["sPE"].h, sems["sPE"].n)
                sems["dSc"].inc(s.dma_start(out=sb["xtmp"][:, 0:128], in_=dram["gw"]), 16)
                s.wait_ge(sems["dSc"].h, sems["dSc"].n)
                sems["dSc"].inc(s.dma_start(out=sb["stg16"][0:1, 0:128], in_=dram["gb"]), 16)
                s.wait_ge(sems["dSc"].h, sems["dSc"].n)
        @block.vector
        def _(v):
            if 1 in self.layers:
                v.wait_ge(sems["dSc"].h, sems["dSc"].n)
                v.drain()
                sems["sDVE"].inc(v.tensor_copy(sb["gwT_sb"][:], sb["xtmp"][:, 0:128]))
        @block.tensor
        def _(t):
            if 1 in self.layers:
                t.wait_ge(sems["dSc"].h, sems["dSc"].n)
                sems["sPE"].inc(t.transpose(ps[2][:, 0:1], sb["stg16"][0:1, 0:128],
                                            sb["ident"][0:1, 0:1]))
        @block.vector
        def _(v):
            if 1 in self.layers:
                v.wait_ge(sems["sPE"].h, sems["sPE"].n)
                v.drain()
                sems["sDVE"].inc(v.tensor_scalar(sb["gbneg"][:], ps[2][:, 0:1],
                                                 -1.0, None, ALU.mult))
        self.bank_guard[2] = (sems["sDVE"], sems["sDVE"].n)

        if 0 in self.layers:
            self.emit_wdecay()
        self.emit_casts()

    def emit_casts(self):
        c, block = self.c, self.block
        sems, dram = self.sems, self.dram
        E = c["E"]
        CH = 1 << 19

        def pieces(g, cg, dst, srcname):
            if "!" in srcname:
                nm, e = srcname.split("!")
                srcap = dram[nm][int(e)]
            else:
                srcap = dram[srcname]
            n = 1
            for d in srcap.shape:
                n *= d
            flat_src = srcap.rearrange(
                " ".join(f"d{i}" for i in range(len(srcap.shape))) + " -> ("
                + " ".join(f"d{i}" for i in range(len(srcap.shape))) + ")")
            flat_dst = dst.rearrange(
                " ".join(f"d{i}" for i in range(len(dst.shape))) + " -> ("
                + " ".join(f"d{i}" for i in range(len(dst.shape))) + ")")
            o = 0
            while o < n:
                m = min(CH, n - o)
                cg.inc(g.dma_start(out=flat_dst[o:o + m], in_=flat_src[o:o + m]), 16)
                o += m

        @block.gpsimd
        def _(g):
            g.wait_ge(sems["dIn"].h, 16)
            g.wait_ge(sems["dSc"].h, sems["dSc"].n)
            g.wait_ge(sems["dGp"].h, sems["dGp"].n)
            if 0 in self.layers:
                cg = sems["cgws"]
                for nm in ("wsg", "wsu", "wsd"):
                    pieces(g, cg, dram["c_" + nm], nm)
                self.cast_ready["ws"] = (cg, cg.n)
                for e in range(E):
                    cge = sems[f"cgwe{e}"]
                    for nm in ("weg", "weu", "wed"):
                        pieces(g, cge, dram["c_" + nm][e], nm + f"!{e}")
                    self.cast_ready[f"we{e}"] = (cge, cge.n)
            if 1 in self.layers:
                cg = sems["cgqk"]
                for nm in ("qw", "kw", "vw", "ow"):
                    pieces(g, cg, dram["c_" + nm], nm)
                self.cast_ready["qkvo"] = (cg, cg.n)
                for e in range(E):
                    cge = sems[f"cgl1e{e}"]
                    for nm in ("gp", "up", "dp"):
                        pieces(g, cge, dram["c_" + nm][e], nm + f"!{e}")
                    self.cast_ready[f"l1e{e}"] = (cge, cge.n)

    def emit_wdecay(self):
        nc, c, block = self.nc, self.c, self.block
        sems, sb, dram = self.sems, self.sb, self.dram
        D, DC = c["D"], c["DC"]
        ps, pr = self.ps, self.pr

        @block.vector
        def _(v):
            v.wait_ge(sems["dSc"].h, sems["dSc"].n)
            sems["sDVE"].inc(v.tensor_scalar(sb["dvt"][:], sb["dvt"][:], 0.9, 1.0,
                                             ALU.max, ALU.min))
        @block.scalar
        def _(s):
            s.wait_ge(sems["sDVE"].h, sems["sDVE"].n)
            sems["sACT"].inc(s.activation(sb["dvt"][:], sb["dvt"][:], AF.Ln))
        @block.sync
        def _(s):
            s.wait_ge(sems["sACT"].h, sems["sACT"].n)
            sems["dSc"].inc(s.dma_start(out=sb["r_a"][0:1, 0:1], in_=sb["dvt"][0:1, 0:1]), 16)
            s.wait_ge(sems["dSc"].h, sems["dSc"].n)
            sems["dSc"].inc(s.dma_start(out=sb["r_b"][0:1, 0:1], in_=sb["dvt"][1:2, 0:1]), 16)
            s.wait_ge(sems["dSc"].h, sems["dSc"].n)
        @block.tensor
        def _(t):
            t.wait_ge(sems["dSc"].h, sems["dSc"].n)
            t.wait_ge(sems["sDVE"].h, sems["sDVE"].n)
            t.matmul(ps[0][:, 0:1], sb["ones1"][0:1, :], sb["r_a"][0:1, 0:1],
                     start=True, stop=True)
            sems["sPE"].inc(t.matmul(ps[1][:, 0:1], sb["ones1"][0:1, :],
                                     sb["r_b"][0:1, 0:1], start=True, stop=True))
        @block.vector
        def _(v):
            v.wait_ge(sems["sPE"].h, sems["sPE"].n)
            v.tensor_copy(sb["lnd0"][:], ps[0][:, 0:1])
            sems["sDVE"].inc(v.tensor_copy(sb["lnd1"][:], ps[1][:, 0:1]))

        with contextlib.ExitStack() as scope:
            vrow = scope.enter_context(nc.sbuf_tensor("wd_v", [1, D], F32))
            r0 = scope.enter_context(nc.sbuf_tensor("wd_r0", [1, D], F32))
            r1 = scope.enter_context(nc.sbuf_tensor("wd_r1", [1, D], F32))
            w1r = scope.enter_context(nc.sbuf_tensor("wd_w1r", [1, D], F32))
            mbig0 = scope.enter_context(nc.sbuf_tensor("wd_m", [128, 2 * D], BF16))
            wbfF = scope.enter_context(nc.sbuf_tensor("wd_bf", [128, D], BF16))
            ecol0 = scope.enter_context(nc.sbuf_tensor("wd_e0", [128, 1], F32))
            ecol1 = scope.enter_context(nc.sbuf_tensor("wd_e1", [128, 1], F32))
            cA = scope.enter_context(nc.sbuf_tensor("wd_ca", [128, 1], F32))
            c0T = scope.enter_context(nc.sbuf_tensor("wd_c0t", [1, 128], F32))
            c1T = scope.enter_context(nc.sbuf_tensor("wd_c1t", [1, 128], F32))

            @block.vector
            def _(v):
                v.drain()
                sems["sDVE"].inc(v.memset(mbig0[:], 1.0))
            @block.gpsimd
            def _(g):
                g.wait_ge(sems["sDVE"].h, sems["sDVE"].n)
                g.affine_select(mbig0[:], mbig0[:], [[1, 2 * D]], ALU.is_ge, 0.0,
                                base=-D, channel_multiplier=-1)
                sems["sGP"].inc(g.iota(ecol0[:], [[1, 1]], base=0, channel_multiplier=-1,
                                       allow_small_or_imprecise_dtypes=True))
            @block.sync
            def _(s):
                sems["dSc"].inc(s.dma_start(out=w1r[0:1, :], in_=dram["w_dec"][1:2, :]), 16)
                s.wait_ge(sems["dSc"].h, sems["dSc"].n)
            w1r_cnt = sems["dSc"].n
            @block.vector
            def _(v):
                v.wait_ge(sems["sGP"].h, sems["sGP"].n)
                v.drain()
                v.tensor_scalar(ecol1[:], ecol0[:], sb["lnd1"][:, 0:1], None, ALU.mult)
                v.drain()
                sems["sDVE"].inc(v.tensor_scalar(ecol0[:], ecol0[:], sb["lnd0"][:, 0:1],
                                                 None, ALU.mult))
            @block.scalar
            def _(s):
                s.wait_ge(sems["sDVE"].h, sems["sDVE"].n)
                s.drain()
                s.activation(ecol0[:], ecol0[:], AF.Exp)
                s.drain()
                sems["sACT"].inc(s.activation(ecol1[:], ecol1[:], AF.Exp))
            @block.tensor
            def _(t):
                t.wait_ge(sems["sACT"].h, sems["sACT"].n)
                sems["sPE"].inc(t.transpose(ps[0][0:1, 0:128], ecol1[:],
                                            sb["ident"][:]))
            @block.vector
            def _(v):
                v.wait_ge(sems["sPE"].h, sems["sPE"].n)
                v.drain()
                sems["sDVE"].inc(v.tensor_copy(c1T[0:1, :], ps[0][0:1, 0:128]))

            nwin = D // 512
            for kc in range(DC):
                @block.vector
                def _(v, kc=kc):
                    v.wait_ge(sems["sPE"].h, sems["sPE"].n)
                    v.drain()
                    sems["sDVE"].inc(v.tensor_scalar(cA[:], ecol0[:],
                                                     sb["w0col"][:, kc:kc + 1],
                                                     None, ALU.mult))
                @block.gpsimd
                def _(g, kc=kc):
                    g.wait_ge(sems["sDVE"].h, sems["sDVE"].n)
                    sems["sGP"].inc(g.iota(vrow[0:1, :], [[1, D]], base=-128 * kc,
                                           channel_multiplier=0,
                                           allow_small_or_imprecise_dtypes=True))
                @block.tensor
                def _(t, kc=kc):
                    t.wait_ge(sems["sDVE"].h, sems["sDVE"].n)
                    sems["sPE"].inc(t.transpose(ps[0][0:1, 0:128], cA[:],
                                                sb["ident"][:]))
                @block.vector
                def _(v, kc=kc):
                    v.wait_ge(sems["sPE"].h, sems["sPE"].n)
                    v.wait_ge(sems["sGP"].h, sems["sGP"].n)
                    v.drain()
                    v.tensor_copy(c0T[0:1, :], ps[0][0:1, 0:128])
                    v.tensor_scalar(vrow[0:1, :], vrow[0:1, :], 0.0, None, ALU.max)
                    v.drain()
                    v.tensor_scalar(r0[0:1, :], vrow[0:1, :], sb["lnd0"][0:1, 0:1],
                                    None, ALU.mult)
                    v.drain()
                    sems["sDVE"].inc(v.tensor_scalar(r1[0:1, :], vrow[0:1, :],
                                                     sb["lnd1"][0:1, 0:1], None, ALU.mult))
                @block.scalar
                def _(s, kc=kc):
                    s.wait_ge(sems["sDVE"].h, sems["sDVE"].n)
                    s.drain()
                    s.activation(r0[0:1, :], r0[0:1, :], AF.Exp)
                    s.drain()
                    sems["sACT"].inc(s.activation(r1[0:1, :], r1[0:1, :], AF.Exp))
                @block.vector
                def _(v, kc=kc):
                    v.wait_ge(sems["sACT"].h, sems["sACT"].n)
                    v.wait_ge(sems["dSc"].h, w1r_cnt)
                    v.drain()
                    sems["sDVE"].inc(v.tensor_tensor(r1[0:1, :], r1[0:1, :],
                                                     w1r[0:1, :], ALU.mult))
                pe_w = sems["sPE"].n
                dve_w = sems["sDVE"].n
                @block.tensor
                def _(t, kc=kc, pe_w=pe_w, dve_w=dve_w):
                    t.wait_ge(sems["sDVE"].h, dve_w)
                    for win in range(nwin):
                        if win >= 2:
                            t.wait_ge(sems["sDVE"].h, dve_w + win - 1)
                        bkw = 2 + win % 2
                        t.matmul(ps[bkw][:, 0:512], c0T[0:1, :],
                                 r0[0:1, win * 512:(win + 1) * 512],
                                 start=True, stop=False)
                        t.matmul(ps[bkw][:, 0:512], c1T[0:1, :],
                                 r1[0:1, win * 512:(win + 1) * 512],
                                 start=False, stop=True).then_inc(sems["sPE"].h, 1)
                sems["sPE"].n = pe_w + nwin
                dwd_prev = sems["dWd"].n
                @block.vector
                def _(v, kc=kc, pe_w=pe_w, dwd_prev=dwd_prev):
                    v.wait_ge(sems["dWd"].h, dwd_prev)
                    for win in range(nwin):
                        v.wait_ge(sems["sPE"].h, pe_w + win + 1)
                        v.drain()
                        v.tensor_tensor(
                            wbfF[:, win * 512:(win + 1) * 512],
                            ps[2 + win % 2][:, 0:512],
                            mbig0[:, D - 128 * kc + win * 512: D - 128 * kc + (win + 1) * 512],
                            ALU.mult).then_inc(sems["sDVE"].h, 1)
                sems["sDVE"].n = dve_w + nwin
                @block.gpsimd
                def _(sy, kc=kc):
                    sy.wait_ge(sems["sDVE"].h, sems["sDVE"].n)
                    sems["dWd"].inc(sy.dma_start(
                        out=dram["c_wdec"][:, :, kc * 256:(kc + 1) * 256].rearrange(
                            "dcp p j -> p dcp j"),
                        in_=wbfF[:].rearrange("p (dcp j) -> p dcp j", j=256)), 16)
            self.cast_ready["wdec"] = (sems["dWd"], sems["dWd"].n)
            self.scope_guards.append((sems["dWd"], sems["dWd"].n))
            self.scope_guards.append((sems["sDVE"], sems["sDVE"].n))

    def emit_stats_xn(self, tile, extra_guards=()):
        c, block = self.c, self.block
        sems, sb = self.sems, self.sb
        DC, TT, D = c["DC"], c["TT"], c["D"]
        t0 = tile * TT
        ps, pr = self.ps, self.pr
        xnb = sb["xnb"]
        xsq = sb["H"]

        @block.vector
        def _(v):
            v.wait_ge(sems["dIn"].h, 16)
            if self.xn_guard is not None:
                v.wait_ge(self.xn_guard[0].h, self.xn_guard[1])
            for g in extra_guards:
                v.wait_ge(g[0].h, g[1])
            v.drain()
            for dc in range(DC):
                v.tensor_copy(xnb[:, dc * TT:(dc + 1) * TT], self.xres(dc, t0, TT))
            for dc in range(DC - 1):
                v.tensor_tensor(xsq[:, dc * TT:(dc + 1) * TT],
                                self.xres(dc, t0, TT), self.xres(dc, t0, TT), ALU.mult)
            dc = DC - 1
            sems["sDVE"].inc(v.tensor_tensor(xsq[:, dc * TT:(dc + 1) * TT],
                                             self.xres(dc, t0, TT),
                                             self.xres(dc, t0, TT), ALU.mult))
        @block.tensor
        def _(t):
            t.wait_ge(sems["sDVE"].h, sems["sDVE"].n)
            if self.pr_guard is not None:
                t.wait_ge(self.pr_guard[0].h, self.pr_guard[1])
            for dc in range(DC):
                t.matmul(pr[0][0:1, 0:TT], sb["onesc"][:, 0:1],
                         xnb[:, dc * TT:(dc + 1) * TT],
                         start=(dc == 0), stop=(dc == DC - 1))
            for dc in range(DC):
                last = t.matmul(pr[1][0:1, 0:TT], sb["onesc"][:, 0:1],
                                xsq[:, dc * TT:(dc + 1) * TT],
                                start=(dc == 0), stop=(dc == DC - 1))
            sems["sPE"].inc(last)
        @block.vector
        def _(v):
            v.wait_ge(sems["sPE"].h, sems["sPE"].n)
            v.drain()
            v.tensor_scalar(sb["rowA"][0:1, 0:TT], pr[0][0:1, 0:TT], 1.0 / D, None, ALU.mult)
            v.tensor_scalar(sb["rowB"][0:1, 0:TT], pr[1][0:1, 0:TT], 1.0 / D, None, ALU.mult)
            v.drain()
            v.tensor_tensor(sb["rowC"][0:1, 0:TT], sb["rowA"][0:1, 0:TT],
                            sb["rowA"][0:1, 0:TT], ALU.mult)
            v.drain()
            v.tensor_tensor(sb["rowB"][0:1, 0:TT], sb["rowB"][0:1, 0:TT],
                            sb["rowC"][0:1, 0:TT], ALU.subtract)
            v.drain()
            sems["sDVE"].inc(v.tensor_scalar(sb["rowB"][0:1, 0:TT], sb["rowB"][0:1, 0:TT],
                                             EPS, None, ALU.add))
        self.pr_guard = (sems["sDVE"], sems["sDVE"].n)
        @block.scalar
        def _(s):
            s.wait_ge(sems["sDVE"].h, sems["sDVE"].n)
            s.drain()
            s.activation(sb["rowB"][0:1, 0:TT], sb["rowB"][0:1, 0:TT], AF.Ln)
            s.drain()
            sems["sACT"].inc(s.activation(sb["rowB"][0:1, 0:TT], sb["rowB"][0:1, 0:TT],
                                          AF.Exp, scale=-0.5))
        @block.tensor
        def _(t):
            t.wait_ge(sems["sACT"].h, sems["sACT"].n)
            bm = self.take_bank(t)
            br = self.take_bank(t)
            self._mb, self._rb = bm, br
            t.matmul(ps[bm][:, 0:TT], sb["ones1"][0:1, :], sb["rowA"][0:1, 0:TT],
                     start=True, stop=True)
            sems["sPE"].inc(t.matmul(ps[br][:, 0:TT], sb["ones1"][0:1, :],
                                     sb["rowB"][0:1, 0:TT], start=True, stop=True))
        @block.vector
        def _(v):
            v.wait_ge(sems["sPE"].h, sems["sPE"].n)
            v.drain()
            v.tensor_copy(sb["mr_b"][:, 0:TT], ps[self._mb][:, 0:TT])
            sems["sDVE"].inc(v.tensor_copy(sb["mr_b"][:, TT:2 * TT], ps[self._rb][:, 0:TT]))
        self.bank_guard[self._mb] = (sems["sDVE"], sems["sDVE"].n)
        self.bank_guard[self._rb] = (sems["sDVE"], sems["sDVE"].n)
        @block.vector
        def _(v):
            v.drain()
            for dc in range(DC):
                v.tensor_tensor(sb["xtmp"][:, 0:TT], self.xres(dc, t0, TT),
                                sb["mr_b"][:, 0:TT], ALU.subtract)
                v.drain()
                v.tensor_tensor(xnb[:, dc * TT:(dc + 1) * TT], sb["xtmp"][:, 0:TT],
                                sb["mr_b"][:, TT:2 * TT], ALU.mult)
                v.drain()
            sems["sDVE"].inc(v.tensor_copy(xnb[:, 0:1], xnb[:, 0:1]))
        self.xn_ready = sems["sDVE"].n

    def emit_l0_attn(self, tile):
        c, block = self.c, self.block
        sems, sb, dram = self.sems, self.sb, self.dram
        DC, TT, TC = c["DC"], c["TT"], c["TC"]
        t0 = tile * TT
        ps = self.ps
        wdec_ready = self.cast_ready["wdec"]

        for dcp in range(DC // 2):
            slot = self.slot()
            src = dram["c_wdec"][dcp].rearrange("p (kc j) -> p kc j", j=256)
            buf, wcnt = self.load_slab(slot, src, ready=wdec_ready)
            for sub in range(2):
                dc = dcp * 2 + sub
                @block.tensor
                def _(t, dc=dc, sub=sub, buf=buf, wcnt=wcnt, slot=slot):
                    dW = sems["dWA" if slot == "A" else "dWB"]
                    t.wait_ge(dW.h, wcnt)
                    t.wait_ge(sems["sDVE"].h, self.xn_ready)
                    bk = self.take_bank(t)
                    self._bk = bk
                    for kc in range(DC):
                        last = t.matmul(ps[bk][:, 0:TT],
                                        buf[:, kc * 256 + sub * 128: kc * 256 + sub * 128 + 128],
                                        sb["xnb"][:, kc * TT:(kc + 1) * TT],
                                        start=(kc == 0), stop=(kc == DC - 1))
                    sems["sPE"].inc(last)
                self.wguard[slot] = (sems["sPE"], sems["sPE"].n)
                @block.vector
                def _(v, dc=dc):
                    v.wait_ge(sems["sPE"].h, sems["sPE"].n)
                    v.drain()
                    sems["sDVE"].inc(v.scalar_tensor_tensor(
                        self.xres(dc, t0, TT), ps[self._bk][:, 0:TT],
                        sb["bias_sb"][:, dc:dc + 1], self.xres(dc, t0, TT),
                        ALU.add, ALU.add))
                self.bank_guard[self._bk] = (sems["sDVE"], sems["sDVE"].n)

    def emit_router(self, tile, layer):
        c, block = self.c, self.block
        sems, sb = self.sems, self.sb
        DC, TT, CPT = c["DC"], c["TT"], c["CPT"]
        ps = self.ps
        wr = sb["wr0_sb"] if layer == 0 else sb["wr1_sb"]

        @block.tensor
        def _(t):
            t.wait_ge(sems["sDVE"].h, self.xn_ready)
            rb = self.take_bank(t)
            self._rbank = rb
            for tsub in range(CPT):
                for dc in range(DC):
                    last = t.matmul(ps[rb][:, tsub * 8:(tsub + 1) * 8],
                                    sb["xnb"][:, dc * TT + tsub * 128: dc * TT + tsub * 128 + 128],
                                    wr[:, dc * 8:(dc + 1) * 8],
                                    start=(dc == 0), stop=(dc == DC - 1))
            sems["sPE"].inc(last)
        @block.vector
        def _(v):
            v.wait_ge(sems["sPE"].h, sems["sPE"].n)
            v.drain()
            sems["sDVE"].inc(v.tensor_copy(sb["rlog"][:, 0:CPT * 8], ps[self._rbank][:, 0:CPT * 8]))
        self.bank_guard[self._rbank] = (sems["sDVE"], sems["sDVE"].n)

        for tsub in range(CPT):
            sl = slice(tsub * 8, tsub * 8 + 8)
            if layer == 0:
                @block.vector
                def _(v, sl=sl):
                    v.drain()
                    sems["sDVE"].inc(v.reduce_max(sb["r_a"][:], sb["rlog"][:, sl],
                                                  AX.X, negate=True))
                @block.scalar
                def _(s, sl=sl):
                    s.wait_ge(sems["sDVE"].h, sems["sDVE"].n)
                    s.drain()
                    sems["sACT"].inc(s.activation(sb["escr"][:], sb["rlog"][:, sl],
                                                  AF.Exp, bias=sb["r_a"][:, 0:1]))
                @block.vector
                def _(v, tsub=tsub, sl=sl):
                    v.wait_ge(sems["sACT"].h, sems["sACT"].n)
                    v.drain()
                    v.tensor_reduce(sb["r_b"][:], sb["escr"][:], AX.X, ALU.add)
                    v.drain()
                    v.reciprocal(sb["r_c"][:], sb["r_b"][:])
                    v.drain()
                    v.tensor_scalar(sb["escr"][:], sb["escr"][:], sb["r_c"][:, 0:1],
                                    None, ALU.mult)
                    v.drain()
                    v.reduce_max(sb["r_a"][:], sb["escr"][:], AX.X)
                    v.drain()
                    v.tensor_scalar(sb["escr2"][:], sb["escr"][:], sb["r_a"][:, 0:1],
                                    None, ALU.is_equal)
                    v.drain()
                    v.scalar_tensor_tensor(sb["e8"][:], sb["escr2"][:], NEG_BIG,
                                           sb["escr"][:], ALU.mult, ALU.add)
                    v.drain()
                    v.reduce_max(sb["r_b"][:], sb["e8"][:], AX.X)
                    v.drain()
                    v.tensor_scalar(sb["e8"][:], sb["escr"][:], sb["r_b"][:, 0:1],
                                    None, ALU.is_ge)
                    v.drain()
                    sems["sDVE"].inc(v.tensor_tensor(sb["cb_tm"][:, sl], sb["escr"][:],
                                                     sb["e8"][:], ALU.mult))
            else:
                @block.vector
                def _(v, sl=sl):
                    v.drain()
                    v.reduce_max(sb["r_a"][:], sb["rlog"][:, sl], AX.X)
                    v.drain()
                    v.tensor_scalar(sb["escr2"][:], sb["rlog"][:, sl], sb["r_a"][:, 0:1],
                                    None, ALU.is_equal)
                    v.drain()
                    v.scalar_tensor_tensor(sb["e8"][:], sb["escr2"][:], NEG_BIG,
                                           sb["rlog"][:, sl], ALU.mult, ALU.add)
                    v.drain()
                    v.reduce_max(sb["r_b"][:], sb["e8"][:], AX.X)
                    v.drain()
                    v.tensor_scalar(sb["escr"][:], sb["rlog"][:, sl], sb["r_b"][:, 0:1],
                                    None, ALU.is_equal)
                    v.drain()
                    sems["sDVE"].inc(v.tensor_tensor(sb["r_c"][:], sb["r_a"][:],
                                                     sb["r_b"][:], ALU.subtract))
                @block.scalar
                def _(s):
                    s.wait_ge(sems["sDVE"].h, sems["sDVE"].n)
                    s.drain()
                    sems["sACT"].inc(s.activation(sb["r_d"][:], sb["r_c"][:], AF.Exp,
                                                  scale=-1.0))
                @block.vector
                def _(v, sl=sl):
                    v.wait_ge(sems["sACT"].h, sems["sACT"].n)
                    v.drain()
                    v.tensor_scalar(sb["r_d"][:], sb["r_d"][:], 1.0, None, ALU.add)
                    v.drain()
                    v.reciprocal(sb["r_c"][:], sb["r_d"][:])
                    v.drain()
                    v.tensor_scalar(sb["r_d"][:], sb["r_c"][:], -1.0, 1.0,
                                    ALU.mult, ALU.add)
                    v.drain()
                    v.tensor_scalar(sb["escr2"][:], sb["escr2"][:], sb["r_c"][:, 0:1],
                                    None, ALU.mult)
                    v.drain()
                    sems["sDVE"].inc(v.scalar_tensor_tensor(
                        sb["cb_tm"][:, sl], sb["escr"][:], sb["r_d"][:, 0:1],
                        sb["escr2"][:], ALU.mult, ALU.add))
        @block.tensor
        def _(t):
            t.wait_ge(sems["sDVE"].h, sems["sDVE"].n)
            cbk = self.take_bank(t)
            self._cbk = cbk
            for tsub in range(CPT):
                last = t.transpose(ps[cbk][0:8, tsub * 128:(tsub + 1) * 128],
                                   sb["cb_tm"][:, tsub * 8:(tsub + 1) * 8], sb["ident"][:])
            sems["sPE"].inc(last)
        @block.vector
        def _(v):
            v.wait_ge(sems["sPE"].h, sems["sPE"].n)
            v.drain()
            sems["sDVE"].inc(v.tensor_copy(sb["cb_fm"][0:8, 0:TT],
                                           ps[self._cbk][0:8, 0:CPT * 128]))
        self.bank_guard[self._cbk] = (sems["sDVE"], sems["sDVE"].n)
        self.cb_ready = sems["sDVE"].n

    def emit_cb_broadcast(self, e):
        sems, sb, block = self.sems, self.sb, self.block
        TT = self.c["TT"]
        ps = self.ps

        @block.tensor
        def _(t):
            t.wait_ge(sems["sDVE"].h, self.cb_ready)
            bk = self.take_bank(t)
            self._cbb = bk
            sems["sPE"].inc(t.matmul(ps[bk][:, 0:TT],
                                     sb["selmat"][0:8, e * 128:(e + 1) * 128],
                                     sb["cb_fm"][0:8, 0:TT], start=True, stop=True))
        @block.vector
        def _(v):
            v.wait_ge(sems["sPE"].h, sems["sPE"].n)
            v.drain()
            sems["sDVE"].inc(v.tensor_copy(sb["cb_b"][:, 0:TT], ps[self._cbb][:, 0:TT]))
        self.bank_guard[self._cbb] = (sems["sDVE"], sems["sDVE"].n)
        self.cbb_ready = sems["sDVE"].n

    def emit_gate_up_down(self, tile, gsrc, usrc, dsrc, nhc, with_cb, ready):
        c, block = self.c, self.block
        sems, sb = self.sems, self.sb
        DC, TT, TC = c["DC"], c["TT"], c["TC"]
        t0 = tile * TT
        ps = self.ps
        H = sb["H"]
        pe0, act0, dve0 = sems["sPE"].n, sems["sACT"].n, sems["sDVE"].n

        wcnt = {}
        slots = {}
        for hc in range(nhc):
            slot = self.slot()
            slots[hc] = slot
            gs = gsrc[hc].rearrange("p (dc j) -> p dc j", j=128)
            us = usrc[hc].rearrange("p (dc j) -> p dc j", j=128)
            self.load_slab(slot, gs, off=0, ready=ready if hc == 0 else None)
            _, wcnt[hc] = self.load_slab(slot, us, off=DC * 128, first=False)
            self.wguard[slot] = (sems["sPE"], pe0 + hc + 1)

        banks = {}
        @block.tensor
        def _(t):
            t.wait_ge(sems["sDVE"].h, self.xn_ready)
            for hc in range(nhc):
                slot = slots[hc]
                buf = sb["wA" if slot == "A" else "wB"]
                t.wait_ge(sems["dWA" if slot == "A" else "dWB"].h, wcnt[hc])
                bg = self.take_bank(t)
                bu = self.take_bank(t)
                banks[hc] = (bg, bu)
                for dc in range(DC):
                    t.matmul(ps[bg][:, 0:TT], buf[:, dc * 128:(dc + 1) * 128],
                             sb["xnb"][:, dc * TT:(dc + 1) * TT],
                             start=(dc == 0), stop=(dc == DC - 1))
                for dc in range(DC):
                    last = t.matmul(ps[bu][:, 0:TT],
                                    buf[:, DC * 128 + dc * 128: DC * 128 + (dc + 1) * 128],
                                    sb["xnb"][:, dc * TT:(dc + 1) * TT],
                                    start=(dc == 0), stop=(dc == DC - 1))
                last.then_inc(sems["sPE"].h, 1)
                self.bank_guard[bg] = (sems["sACT"], act0 + hc + 1)
                self.bank_guard[bu] = (sems["sDVE"], dve0 + 2 * (hc + 1))
        sems["sPE"].n = pe0 + nhc

        @block.scalar
        def _(s):
            for hc in range(nhc):
                s.wait_ge(sems["sPE"].h, pe0 + hc + 1)
                s.drain()
                s.activation(H[:, hc * TT:(hc + 1) * TT], ps[banks[hc][0]][:, 0:TT],
                             AF.Silu).then_inc(sems["sACT"].h, 1)
        sems["sACT"].n = act0 + nhc

        @block.vector
        def _(v):
            if with_cb:
                v.wait_ge(sems["sDVE"].h, self.cbb_ready)
            for hc in range(nhc):
                v.wait_ge(sems["sPE"].h, pe0 + hc + 1)
                v.wait_ge(sems["sACT"].h, act0 + hc + 1)
                v.drain()
                if with_cb:
                    v.tensor_tensor(sb["ue"][:, 0:TT], ps[banks[hc][1]][:, 0:TT],
                                    sb["cb_b"][:, 0:TT], ALU.mult
                                    ).then_inc(sems["sDVE"].h, 1)
                else:
                    v.tensor_copy(sb["ue"][:, 0:TT], ps[banks[hc][1]][:, 0:TT]
                                  ).then_inc(sems["sDVE"].h, 1)
                v.drain()
                v.tensor_tensor(H[:, hc * TT:(hc + 1) * TT],
                                H[:, hc * TT:(hc + 1) * TT],
                                sb["ue"][:, 0:TT], ALU.mult).then_inc(sems["sDVE"].h, 1)
        sems["sDVE"].n = dve0 + 2 * nhc
        self.h_ready = sems["sDVE"].n

        pe1, dve1 = sems["sPE"].n, sems["sDVE"].n
        dwcnt = {}
        dslots = {}
        for dc in range(DC):
            slot = self.slot()
            dslots[dc] = slot
            dsl = dsrc[dc].rearrange("p (hc j) -> p hc j", j=128)
            _, dwcnt[dc] = self.load_slab(slot, dsl)
            self.wguard[slot] = (sems["sPE"], pe1 + dc + 1)
        dbanks = {}
        @block.tensor
        def _(t):
            t.wait_ge(sems["sDVE"].h, self.h_ready)
            for dc in range(DC):
                slot = dslots[dc]
                buf = sb["wA" if slot == "A" else "wB"]
                t.wait_ge(sems["dWA" if slot == "A" else "dWB"].h, dwcnt[dc])
                bk = self.take_bank(t)
                dbanks[dc] = bk
                for hc in range(nhc):
                    last = t.matmul(ps[bk][:, 0:TT], buf[:, hc * 128:(hc + 1) * 128],
                                    H[:, hc * TT:(hc + 1) * TT],
                                    start=(hc == 0), stop=(hc == nhc - 1))
                last.then_inc(sems["sPE"].h, 1)
                self.bank_guard[bk] = (sems["sDVE"], dve1 + dc + 1)
        sems["sPE"].n = pe1 + DC
        @block.vector
        def _(v):
            for dc in range(DC):
                v.wait_ge(sems["sPE"].h, pe1 + dc + 1)
                v.drain()
                v.scalar_tensor_tensor(
                    self.xres(dc, t0, TT), ps[dbanks[dc]][:, 0:TT], 0.0,
                    self.xres(dc, t0, TT), ALU.add, ALU.add).then_inc(sems["sDVE"].h, 1)
        sems["sDVE"].n = dve1 + DC

    def emit_l0_tile(self, tile, guards=()):
        c, dram = self.c, self.dram
        self.emit_stats_xn(tile, extra_guards=guards)
        self.emit_router(tile, layer=0)
        self.emit_gate_up_down(tile, dram["c_wsg"], dram["c_wsu"], dram["c_wsd"],
                               c["SHC"], with_cb=False, ready=self.cast_ready["ws"])
        for e in range(c["E"]):
            self.emit_cb_broadcast(e)
            self.emit_gate_up_down(tile, dram["c_weg"][e], dram["c_weu"][e],
                                   dram["c_wed"][e], c["EHC"], with_cb=True,
                                   ready=self.cast_ready[f"we{e}"])
        self.emit_l0_attn(tile)
        self.xn_guard = (self.sems["sPE"], self.sems["sPE"].n)

    def emit_kvq_tile(self, tile):
        c, block = self.c, self.block
        sems, sb, dram = self.sems, self.sb, self.dram
        DC, TT, TC, NH, CPT, D = c["DC"], c["TT"], c["TC"], c["NH"], c["CPT"], c["D"]
        ps = self.ps
        qk_ready = self.cast_ready["qkvo"]
        nh_half = DC // 2
        nhb = 512 // 128

        units = []
        for h in range(NH):
            units.append(("K", h))
            units.append(("Q", h))
        for eb in range(D // 512):
            for tsub in range(CPT):
                units.append(("V", (eb, tsub)))

        pe0 = sems["sPE"].n
        wcnt = {}
        uslots = {}
        vslabs = {}
        ui = 0
        for kind, arg in units:
            if kind in ("K", "Q"):
                slot = self.slot()
                uslots[ui] = slot
                src = dram["c_kw" if kind == "K" else "c_qw"][arg].rearrange(
                    "p (dc j) -> p dc j", j=128)
                _, wcnt[ui] = self.load_slab(
                    slot, src, ready=qk_ready if (ui == 0 and tile == 0) else None)
                self.wguard[slot] = (sems["sPE"], pe0 + ui + 1)
            else:
                eb, tsub = arg
                if tsub == 0:
                    slot = self.slot()
                    srcA = dram["c_vw"][eb, 0].rearrange("p (di e) -> p di e", e=512)
                    _, wA_c = self.load_slab(slot, srcA)
                    slotB = self.slot()
                    srcB = dram["c_vw"][eb, 1].rearrange("p (di e) -> p di e", e=512)
                    _, wB_c = self.load_slab(slotB, srcB)
                    vslabs[eb] = (slot, slotB, wA_c, wB_c)
                    self.wguard[slot] = (sems["sPE"], pe0 + ui + CPT)
                    self.wguard[slotB] = (sems["sPE"], pe0 + ui + CPT)
                uslots[ui] = vslabs[eb]
            ui += 1

        @block.tensor
        def _(t):
            t.wait_ge(sems["sDVE"].h, self.xn_ready)
            ui = 0
            for kind, arg in units:
                bk = self.take_bank(t)
                self._kv_banks = getattr(self, "_kv_banks", {})
                self._kv_banks[ui] = bk
                if kind in ("K", "Q"):
                    slot = uslots[ui]
                    buf = sb["wA" if slot == "A" else "wB"]
                    t.wait_ge(sems["dWA" if slot == "A" else "dWB"].h, wcnt[ui])
                    for dc in range(DC):
                        last = t.matmul(ps[bk][:, 0:TT], buf[:, dc * 128:(dc + 1) * 128],
                                        sb["xnb"][:, dc * TT:(dc + 1) * TT],
                                        start=(dc == 0), stop=(dc == DC - 1))
                else:
                    eb, tsub = arg
                    slot, slotB, wA_c, wB_c = uslots[ui]
                    t.wait_ge(sems["dWA" if slot == "A" else "dWB"].h, wA_c)
                    t.wait_ge(sems["dWA" if slotB == "A" else "dWB"].h, wB_c)
                    for dc in range(DC):
                        sl = slot if dc < nh_half else slotB
                        buf = sb["wA" if sl == "A" else "wB"]
                        di = dc if dc < nh_half else dc - nh_half
                        last = t.matmul(ps[bk][:, 0:512],
                                        sb["xnb"][:, dc * TT + tsub * 128: dc * TT + tsub * 128 + 128],
                                        buf[:, di * 512:(di + 1) * 512],
                                        start=(dc == 0), stop=(dc == DC - 1))
                last.then_inc(sems["sPE"].h, 1)
                self.bank_guard[bk] = (sems["sDVE"], sems["sDVE"].n + ui + 1)
                ui += 1
        n_units = len(units)
        pe_end = pe0 + n_units
        sems["sPE"].n = pe_end

        dve0 = sems["sDVE"].n
        ktn = {}
        ktsem_cnt = {}
        for ui in range(n_units):
            i = (self._kvt_rr if hasattr(self, "_kvt_rr") else 0)
            name = "ktmpA" if (ui % 2 == 0) else "ktmpB"
            ktn[ui] = name
        kt_base = {"ktmpA": sems["dKtA"].n, "ktmpB": sems["dKtB"].n}
        kt_cnt = {"ktmpA": 0, "ktmpB": 0}
        @block.vector
        def _(v):
            for ui, (kind, arg) in enumerate(units):
                ncols = TT if kind in ("K", "Q") else 512
                ktsem = sems["dKtA" if ui % 2 == 0 else "dKtB"]
                v.wait_ge(ktsem.h, kt_base[ktn[ui]] + 16 * kt_cnt[ktn[ui]])
                kt_cnt[ktn[ui]] += 1
                v.wait_ge(sems["sPE"].h, pe0 + ui + 1)
                v.drain()
                v.tensor_copy(sb[ktn[ui]][:, 0:ncols],
                              ps[self._kv_banks[ui]][:, 0:ncols]).then_inc(sems["sDVE"].h, 1)
        sems["sDVE"].n = dve0 + n_units

        @block.gpsimd
        def _(g):
            for ui, (kind, arg) in enumerate(units):
                ktsem = sems["dKtA" if ui % 2 == 0 else "dKtB"]
                g.wait_ge(sems["sDVE"].h, dve0 + ui + 1)
                if kind == "K":
                    dst = dram["kvloc"][arg * 128:(arg + 1) * 128, tile * TT:(tile + 1) * TT]
                    ktsem.inc(g.dma_start(out=dst, in_=sb[ktn[ui]][:, 0:TT]), 16)
                elif kind == "Q":
                    dst = dram["qloc"][arg * 128:(arg + 1) * 128, tile * TT:(tile + 1) * TT]
                    ktsem.inc(g.dma_start(out=dst, in_=sb[ktn[ui]][:, 0:TT]), 16)
                else:
                    eb, tsub = arg
                    h0 = eb * nhb
                    cloc = tile * CPT + tsub
                    dst = bass.AP(
                        dram["kvloc"].tensor,
                        (NH * 128 + h0 * 128) * TC + cloc * 128,
                        [[TC, 128], [128 * TC, nhb], [1, 128]])
                    ktsem.inc(g.dma_start(
                        out=dst,
                        in_=sb[ktn[ui]][:, 0:512].rearrange("p (hh hd) -> p hh hd", hd=128)), 16)
        self.xn_guard = (sems["sPE"], sems["sPE"].n)

    def emit_collective(self):
        sems, dram, block = self.sems, self.dram, self.block

        @block.gpsimd
        def _(g):
            g.wait_ge(sems["dKtA"].h, sems["dKtA"].n)
            g.wait_ge(sems["dKtB"].h, sems["dKtB"].n)
            sems["sCC"].inc(g.collective_compute(
                "AllGather", ALU.bypass, replica_groups=GROUPS,
                ins=[dram["kvloc"]], outs=[dram["kvg"]]))

    def head_sched(self, parity):
        tiles = EVEN_TILES if parity == 0 else ODD_TILES
        CPT, TT = self.c["CPT"], self.c["TT"]
        out = []
        for s in range(4):
            gt = tiles[s]
            q0 = gt * TT
            lst = []
            for G in range(CPT * (gt + 1)):
                d = 128 * G - q0
                lst.append((self.bufcol[G], (TT - d) if d >= 0 else None))
            out.append(lst)
        return out

    def emit_attention(self):
        c, block, nc = self.c, self.block, self.nc
        sems, sb, dram = self.sems, self.sb, self.dram
        DC, TT, TC, NH, CPT, D = c["DC"], c["TT"], c["TC"], c["NH"], c["CPT"], c["D"]
        KCH = c["KCH"]
        ps, pr = self.ps, self.pr
        nhh = NH // 2
        sc = [self.head_sched(0), self.head_sched(1)]
        inv_sqrt_hd = 1.0 / math.sqrt(c["HD"])

        Ks = [sb["KsA"], sb["KsB"]]
        Vs = [sb["VsA"], sb["VsB"]]
        Qs = [sb["QsA"], sb["QsB"]]
        dKs = [sems["dKsA"], sems["dKsB"]]
        dVs = [sems["dVsA"], sems["dVsB"]]
        dQs = [sems["dQsA"], sems["dQsB"]]

        @block.vector
        def _(v):
            for g in self.scope_guards2:
                v.wait_ge(g[0].h, g[1])
            v.drain()
            sems["sDVE"].inc(v.memset(sb["Mbig"][:], 1.0))
        @block.gpsimd
        def _(g):
            g.wait_ge(sems["sDVE"].h, sems["sDVE"].n)
            sems["sGP"].inc(g.affine_select(sb["Mbig"][:], sb["Mbig"][:], [[1, 3 * TT]],
                                            ALU.is_ge, 0.0, base=-TT, channel_multiplier=-1))
        mbig_ready = (sems["sGP"], sems["sGP"].n)

        pe0, act0, dve0 = sems["sPE"].n, sems["sACT"].n, sems["sDVE"].n

        def nmax_s(s):
            return max(len(sc[0][s]), len(sc[1][s]))

        def bases(par, s, h):
            n_max = nmax_s(s)
            pe, act, dve = pe0, act0, dve0
            for s2 in range(s):
                nm2 = nmax_s(s2)
                pe += NH * (3 * nm2 + 2) + 2 * DC
                act += NH * (1 + nm2)
                dve += NH * (CPT + 6) + 2 * DC
            pe += h * (3 * n_max + 2) + (DC if h > nhh - 1 else 0)
            act += h * (1 + n_max)
            dve += h * (CPT + 6) + (DC if h > nhh - 1 else 0)
            return pe, act, dve, n_max

        def pe_head_end(s, h):
            pe, _, _, n_max = bases(0, s, h)
            return pe + 3 * n_max + 2

        last_use = {}
        load_cnt = {}
        oslab = {}
        first_load = True

        for s in range(4):
            for h in range(NH):
                sl = h % 2
                @block.sync
                def _(sy, s=s, h=h, sl=sl, first=first_load):
                    if first:
                        for g in self.scope_guards2:
                            sy.wait_ge(g[0].h, g[1])
                        sy.wait_ge(sems["sCC"].h, 1)
                        sy.wait_ge(sems["sGP"].h, mbig_ready[1])
                    lu = last_use.get(("K", sl))
                    if lu is not None:
                        sy.wait_ge(sems["sPE"].h, lu)
                    dKs[sl].inc(sy.dma_start(
                        out=Ks[sl][:, 0:TC], in_=dram["kvg"][h * 128:(h + 1) * 128, :]), 16)
                    dKs[sl].inc(sy.dma_start(
                        out=Ks[sl][:, TC:2 * TC],
                        in_=dram["kvg"][2 * D + h * 128: 2 * D + (h + 1) * 128, :]), 16)
                    dVs[sl].inc(sy.dma_start(
                        out=Vs[sl][:, 0:TC],
                        in_=dram["kvg"][D + h * 128: D + (h + 1) * 128, :]), 16)
                    dVs[sl].inc(sy.dma_start(
                        out=Vs[sl][:, TC:2 * TC],
                        in_=dram["kvg"][3 * D + h * 128: 3 * D + (h + 1) * 128, :]), 16)
                    dQs[sl].inc(sy.dma_start(
                        out=Qs[sl][:, 0:TT],
                        in_=dram["qloc"][h * 128:(h + 1) * 128, s * TT:(s + 1) * TT]), 16)
                first_load = False
                load_cnt[("K", s, h)] = dKs[sl].n
                load_cnt[("V", s, h)] = dVs[sl].n
                load_cnt[("Q", s, h)] = dQs[sl].n
                last_use[("K", sl)] = pe_head_end(s, h)
                if h == nhh - 1 or h == NH - 1:
                    half = 0 if h == nhh - 1 else 1
                    nq = DC // 4
                    pe_done_half = pe_head_end(s, h)
                    for q in range(nq):
                        slot = self.slot()
                        osrc = dram["c_ow"][half, q].rearrange("p (kc j) -> p kc j", j=512)
                        gspec = self.wguard[slot]
                        @block.sync
                        def _(sy, slot=slot, osrc=osrc, gspec=gspec):
                            if gspec is not None:
                                sy.wait_ge(gspec[0].h, gspec[1])
                            dW = sems["dWA" if slot == "A" else "dWB"]
                            a, b = int(osrc.shape[1]), int(osrc.shape[2])
                            buf = sb["wA" if slot == "A" else "wB"]
                            dest = buf[:, 0:a * b].rearrange("p (a b) -> p a b", b=b)
                            dW.inc(sy.dma_start(out=dest, in_=osrc), 16)
                        dWs = sems["dWA" if slot == "A" else "dWB"]
                        oslab[(s, half, q)] = (slot, sb["wA" if slot == "A" else "wB"], dWs.n)
                        self.wguard[slot] = (sems["sPE"], pe_done_half + 4 * (q + 1))

        def slot_end(s):
            pe, act, dve, _ = bases(0, s, NH - 1)
            nm = max(len(sc[0][s]), len(sc[1][s]))
            return (pe + (3 * nm + 2) + DC, act + (1 + nm), dve + (CPT + 6) + DC)

        for s in range(4):
            @block.tensor
            def _(t, s=s):
                def body(par):
                    for h in range(NH):
                        pe_b, act_b, dve_b, n_max = bases(par, s, h)
                        n = len(sc[par][s])
                        t.wait_ge(dQs[h % 2].h, load_cnt[("Q", s, h)])
                        t.wait_ge(sems["sACT"].h, act_b)
                        t.matmul(ps[3][:, 0:TT], sb["gwT_sb"][:], Qs[h % 2][:, 0:TT],
                                 start=True, stop=True, skip_group_check=True
                                 ).then_inc(sems["sPE"].h, 1)
                        t.wait_ge(dKs[h % 2].h, load_cnt[("K", s, h)])
                        t.wait_ge(dVs[h % 2].h, load_cnt[("V", s, h)])

                        def score(j):
                            bc, off = sc[par][s][j]
                            if j >= 2:
                                t.wait_ge(sems["sACT"].h, act_b + 2 + (j - 2))
                            else:
                                t.wait_ge(sems["sACT"].h, act_b)
                            t.matmul(ps[j % 2][:, 0:TT],
                                     Ks[h % 2][:, bc * 128:(bc + 1) * 128],
                                     Qs[h % 2][:, 0:TT], start=True, stop=True,
                                     skip_group_check=True).then_inc(sems["sPE"].h, 1)

                        def avsum(j):
                            bc, off = sc[par][s][j]
                            t.wait_ge(sems["sACT"].h,
                                      act_b + 2 + j if j < n - 1 else act_b + 1 + n_max)
                            if off is not None:
                                jd = j - (n - CPT)
                                t.wait_ge(sems["sDVE"].h, dve_b + 3 + jd)
                            if j == 0:
                                t.wait_ge(sems["sDVE"].h, dve_b)
                            t.matmul(ps[2][:, 0:TT],
                                     Vs[h % 2][:, bc * 128:(bc + 1) * 128],
                                     sb["PT0" if j % 2 == 0 else "PT1"][:, 0:TT],
                                     start=(j == 0), stop=(j == n - 1),
                                     skip_group_check=True).then_inc(sems["sPE"].h, 1)
                            t.matmul(pr[0][0:1, 0:TT], sb["onesc"][:, 0:1],
                                     sb["PT0" if j % 2 == 0 else "PT1"][:, 0:TT],
                                     start=(j == 0), stop=(j == n - 1),
                                     skip_group_check=True).then_inc(
                                sems["sPE"].h, 1 + (3 * (n_max - n) if j == n - 1 else 0))

                        score(0)
                        for j in range(1, n):
                            score(j)
                            avsum(j - 1)
                        avsum(n - 1)
                        t.wait_ge(sems["sDVE"].h, dve_b + 3 + CPT)
                        t.matmul(ps[4][:, 0:TT], sb["ones1"][0:1, :],
                                 sb["rowA"][0:1, 0:TT], start=True, stop=True,
                                 skip_group_check=True).then_inc(sems["sPE"].h, 1)
                        if h == nhh - 1 or h == NH - 1:
                            half = 0 if h == nhh - 1 else 1
                            pe_done = pe_b + 3 * n_max + 2
                            dve_done = dve_b + CPT + 6
                            for dc in range(DC):
                                slot, buf, wcnt = oslab[(s, half, dc // 4)]
                                dsub = dc % 4
                                t.wait_ge(sems["dWA" if slot == "A" else "dWB"].h, wcnt)
                                t.wait_ge(sems["sDVE"].h, dve_done)
                                bk = 5 if dc % 2 == 0 else 4
                                if dc >= 2:
                                    t.wait_ge(sems["sDVE"].h, dve_done + dc - 1)
                                for kc in range(nhh):
                                    last = t.matmul(
                                        ps[bk][:, 0:TT],
                                        buf[:, kc * 512 + dsub * 128: kc * 512 + dsub * 128 + 128],
                                        sb["out8"][:, kc * TT:(kc + 1) * TT],
                                        start=(kc == 0), stop=(kc == nhh - 1),
                                        skip_group_check=True)
                                last.then_inc(sems["sPE"].h, 1)
                with t.register(f"prt_t{s}") as prt:
                    t.cc_rank_ld(prt, replica_groups=GROUPS)
                    with t.If_eq(prt, 0):
                        body(0)
                    with t.Else():
                        body(1)

            @block.scalar
            def _(sx, s=s):
                def body(par):
                    for h in range(NH):
                        pe_b, act_b, dve_b, n_max = bases(par, s, h)
                        n = len(sc[par][s])
                        sx.wait_ge(sems["sPE"].h, pe_b + 1)
                        sx.wait_ge(sems["sDVE"].h, dve_b)
                        sx.drain()
                        sx.activation(sb["gatebuf"][:, 0:TT], ps[3][:, 0:TT], AF.Exp,
                                      scale=-1.0, bias=sb["gbneg"][:, 0:1]
                                      ).then_inc(sems["sACT"].h, 1)
                        for j in range(n):
                            pos = 2 if j == 0 else 3 * j
                            sx.wait_ge(sems["sPE"].h, pe_b + pos)
                            sx.drain()
                            a = sx.activation(sb["PT0" if j % 2 == 0 else "PT1"][:, 0:TT],
                                              ps[j % 2][:, 0:TT], AF.Exp, scale=inv_sqrt_hd)
                            a.then_inc(sems["sACT"].h,
                                       1 + (n_max - n) if j == n - 1 else 1)
                with sx.register(f"prt_s{s}") as prt:
                    sx.cc_rank_ld(prt, replica_groups=GROUPS)
                    with sx.If_eq(prt, 0):
                        body(0)
                    with sx.Else():
                        body(1)

            @block.vector
            def _(v, s=s):
                def body(par):
                    for h in range(NH):
                        pe_b, act_b, dve_b, n_max = bases(par, s, h)
                        n = len(sc[par][s])
                        v.wait_ge(sems["sACT"].h, act_b + 1)
                        v.drain()
                        v.tensor_scalar(sb["gatebuf"][:, 0:TT], sb["gatebuf"][:, 0:TT],
                                        1.0, None, ALU.add).then_inc(sems["sDVE"].h, 1)
                        v.drain()
                        v.reciprocal(sb["gatebuf"][:, 0:TT], sb["gatebuf"][:, 0:TT]
                                     ).then_inc(sems["sDVE"].h, 1)
                        for jd in range(CPT):
                            j = n - CPT + jd
                            bc, off = sc[par][s][j]
                            v.wait_ge(sems["sACT"].h,
                                      act_b + 2 + j if j < n - 1 else act_b + 1 + n_max)
                            v.drain()
                            v.tensor_tensor(sb["PT0" if j % 2 == 0 else "PT1"][:, 0:TT],
                                            sb["PT0" if j % 2 == 0 else "PT1"][:, 0:TT],
                                            sb["Mbig"][:, off:off + TT], ALU.mult
                                            ).then_inc(sems["sDVE"].h, 1)
                        v.wait_ge(sems["sPE"].h, pe_b + 3 * n_max + 1)
                        v.drain()
                        v.reciprocal(sb["rowA"][0:1, 0:TT], pr[0][0:1, 0:TT]
                                     ).then_inc(sems["sDVE"].h, 1)
                        v.wait_ge(sems["sPE"].h, pe_b + 3 * n_max + 2)
                        v.drain()
                        v.tensor_copy(sb["rb_sb"][:, 0:TT], ps[4][:, 0:TT]
                                      ).then_inc(sems["sDVE"].h, 1)
                        if h >= nhh:
                            pass
                        v.drain()
                        v.tensor_tensor(sb["out8"][:, (h % nhh) * TT:(h % nhh + 1) * TT],
                                        ps[2][:, 0:TT], sb["rb_sb"][:, 0:TT], ALU.mult
                                        ).then_inc(sems["sDVE"].h, 1)
                        v.drain()
                        v.tensor_tensor(sb["out8"][:, (h % nhh) * TT:(h % nhh + 1) * TT],
                                        sb["out8"][:, (h % nhh) * TT:(h % nhh + 1) * TT],
                                        sb["gatebuf"][:, 0:TT], ALU.mult
                                        ).then_inc(sems["sDVE"].h, 1)
                        if h == nhh - 1 or h == NH - 1:
                            pe_done = pe_b + 3 * n_max + 2
                            t0 = s * TT
                            for dc in range(DC):
                                v.wait_ge(sems["sPE"].h, pe_done + dc + 1)
                                v.drain()
                                v.scalar_tensor_tensor(
                                    self.xres(dc, t0, TT),
                                    ps[5 if dc % 2 == 0 else 4][:, 0:TT], 0.0,
                                    self.xres(dc, t0, TT), ALU.add, ALU.add
                                    ).then_inc(sems["sDVE"].h, 1)
                with v.register(f"prt_v{s}") as prt:
                    v.cc_rank_ld(prt, replica_groups=GROUPS)
                    with v.If_eq(prt, 0):
                        body(0)
                    with v.Else():
                        body(1)

            pe_e, act_e, dve_e = slot_end(s)
            sems["sPE"].n = pe_e
            sems["sACT"].n = act_e
            sems["sDVE"].n = dve_e
        for i in range(6):
            self.bank_guard[i] = (sems["sDVE"], sems["sDVE"].n)
        self.pr_guard = (sems["sDVE"], sems["sDVE"].n)

    def emit_l1_ffn_tile(self, tile):
        c, dram = self.c, self.dram
        self.emit_stats_xn(tile)
        self.emit_router(tile, layer=1)
        for e in range(c["E"]):
            self.emit_cb_broadcast(e)
            self.emit_gate_up_down(tile, dram["c_gp"][e], dram["c_up"][e],
                                   dram["c_dp"][e], c["EHC"], with_cb=True,
                                   ready=self.cast_ready[f"l1e{e}"])
        self.xn_guard = (self.sems["sPE"], self.sems["sPE"].n)

    def emit_epilogue(self):
        sems, sb, dram, block = self.sems, self.sb, self.dram, self.block
        TC = self.c["TC"]

        @block.scalar
        def _(s):
            s.wait_ge(sems["sDVE"].h, sems["sDVE"].n)
            sems["dOut"].inc(s.dma_start(out=dram[self.out_name], in_=sb["x_res"][:]), 16)
            s.wait_ge(sems["dOut"].h, sems["dOut"].n)


def build_program(cfg, layers=(0, 1), in_name="x_in", out_name="out"):
    nc = bass.Bass("TRN2", target_bir_lowering=False)
    b = B(nc, cfg, layers, in_name, out_name)
    b.declare()
    c = b.c
    TT, DC, SHC, EHC = c["TT"], c["DC"], c["SHC"], c["EHC"]
    with b.es:
        b.alloc_sems()
        b.alloc_psum()
        b.alloc_persistent()
        with nc.Block() as block:
            b.block = block
            b.emit_prologue()
            with contextlib.ExitStack() as s1:
                def S1(name, shape, dt):
                    b.sb[name] = s1.enter_context(nc.sbuf_tensor(name, shape, dt))
                hchunks = max(DC, EHC, SHC if 0 in layers else 0)
                S1("xnb", [128, DC * TT], BF16)
                S1("H", [128, hchunks * TT], BF16)
                S1("mr_b", [128, 2 * TT], F32)
                S1("rowB", [1, TT], F32)
                S1("rowC", [1, TT], F32)
                S1("ktmpA2", [128, 512], BF16)
                S1("ktmpB2", [128, 512], BF16)
                S1("ue", [128, TT], BF16)
                b.sb["ktmpA"] = b.sb["ktmpA2"]
                b.sb["ktmpB"] = b.sb["ktmpB2"]
                if 0 in layers:
                    for tile in range(4):
                        b.emit_l0_tile(tile, guards=tuple(b.scope_guards) if tile == 0 else ())
                if 1 in layers:
                    g0 = tuple(b.scope_guards) if 0 not in layers else ()
                    for tile in range(4):
                        b.emit_stats_xn(tile, extra_guards=g0 if tile == 0 else ())
                        b.emit_kvq_tile(tile)
                    b.emit_collective()
                    for tile in range(4):
                        b.emit_l1_ffn_tile(tile)
                b.scope_guards2 = [(b.sems["sPE"], b.sems["sPE"].n),
                                   (b.sems["sDVE"], b.sems["sDVE"].n),
                                   (b.sems["sACT"], b.sems["sACT"].n)]
            if 1 in layers:
                with contextlib.ExitStack() as s2:
                    def S2(name, shape, dt):
                        b.sb[name] = s2.enter_context(nc.sbuf_tensor(name, shape, dt))
                    TC, NH = c["TC"], c["NH"]
                    S2("KsA", [128, 2 * TC], BF16)
                    S2("KsB", [128, 2 * TC], BF16)
                    S2("VsA", [128, 2 * TC], BF16)
                    S2("VsB", [128, 2 * TC], BF16)
                    S2("QsA", [128, TT], BF16)
                    S2("QsB", [128, TT], BF16)
                    S2("PT0", [128, TT], BF16)
                    S2("PT1", [128, TT], BF16)
                    S2("out8", [128, (NH // 2) * TT], BF16)
                    S2("rb_sb", [128, TT], F32)
                    S2("Mbig", [128, 3 * TT], BF16)
                    b.sb["gatebuf"] = b.sb["xtmp"]
                    b.emit_attention()
            b.emit_epilogue()
    return nc


def _slabify(a, inC, outC):
    return np.ascontiguousarray(
        a.reshape(inC, 128, outC, 128).transpose(2, 1, 0, 3).reshape(outC, 128, inC * 128))


def _shard_inputs(cfg, inputs):
    c = derive(cfg)
    TT, TC, D = c["TT"], c["TC"], c["D"]
    DC, EHC, SHC, NH, E = c["DC"], c["EHC"], c["SHC"], c["NH"], c["E"]
    nhh = NH // 2
    rows = snake_rows(TT)
    x = np.asarray(inputs["L0_attn_x"])
    maps = []
    common0 = {}
    common1 = {}
    common0["w_dec"] = np.ascontiguousarray(inputs["L0_attn_weight"])
    common0["b_dec"] = np.ascontiguousarray(inputs["L0_attn_bias"]).reshape(1, -1)
    common0["dvv"] = np.ascontiguousarray(inputs["L0_attn_decay_value"])
    common0["wr0"] = np.ascontiguousarray(np.asarray(inputs["L0_ffn_W_router"]).T)
    common0["wsg"] = _slabify(np.asarray(inputs["L0_ffn_Ws_gate"]), DC, SHC)
    common0["wsu"] = _slabify(np.asarray(inputs["L0_ffn_Ws_up"]), DC, SHC)
    common0["wsd"] = _slabify(np.asarray(inputs["L0_ffn_Ws_down"]), SHC, DC)
    common0["weg"] = np.stack([_slabify(np.asarray(inputs["L0_ffn_We_gate"][e]), DC, EHC)
                               for e in range(E)])
    common0["weu"] = np.stack([_slabify(np.asarray(inputs["L0_ffn_We_up"][e]), DC, EHC)
                               for e in range(E)])
    common0["wed"] = np.stack([_slabify(np.asarray(inputs["L0_ffn_We_down"][e]), EHC, DC)
                               for e in range(E)])
    qwT = np.asarray(inputs["L1_attn_q_w"]).T
    kwT = np.asarray(inputs["L1_attn_k_w"]).T
    vwT = np.asarray(inputs["L1_attn_v_w"]).T
    owT = np.asarray(inputs["L1_attn_o_w"]).T
    common1["qw"] = _slabify(qwT, DC, NH)
    common1["kw"] = _slabify(kwT, DC, NH)
    common1["vw"] = np.ascontiguousarray(
        vwT.reshape(2, DC // 2, 128, D // 512, 512).transpose(3, 0, 2, 1, 4)
        .reshape(D // 512, 2, 128, (DC // 2) * 512))
    common1["ow"] = np.ascontiguousarray(
        owT.reshape(2, nhh, 128, DC // 4, 512).transpose(0, 3, 2, 1, 4)
        .reshape(2, DC // 4, 128, nhh * 512))
    common1["gw"] = np.ascontiguousarray(np.asarray(inputs["L1_attn_gate_w"]).T)
    common1["gb"] = np.ascontiguousarray(inputs["L1_attn_gate_b"]).reshape(1, -1)
    common1["wr1"] = np.ascontiguousarray(np.asarray(inputs["L1_ffn_gate_w"]))
    gpT = np.asarray(inputs["L1_ffn_gate_proj"]).transpose(0, 2, 1)
    upT = np.asarray(inputs["L1_ffn_up_proj"]).transpose(0, 2, 1)
    dpT = np.asarray(inputs["L1_ffn_down_proj"]).transpose(0, 2, 1)
    common1["gp"] = np.stack([_slabify(gpT[e], DC, EHC) for e in range(E)])
    common1["up"] = np.stack([_slabify(upT[e], DC, EHC) for e in range(E)])
    common1["dp"] = np.stack([_slabify(dpT[e], EHC, DC) for e in range(E)])
    for core in range(8):
        bat, par = core // 2, core % 2
        xc = x[bat][rows[par], :].T
        xr = np.ascontiguousarray(
            xc.reshape(DC, 128, TC).transpose(1, 0, 2).reshape(128, DC * TC))
        maps.append((xr, dict(common0), dict(common1)))
    return maps, rows


def _gather_output(cfg, results, rows, out_name="out"):
    c = derive(cfg)
    D, TC, TT = c["D"], c["TC"], c["TT"]
    B = 4
    S = c["S"]
    DC = c["DC"]
    out = np.empty((B, S, D), dtype=np.float32)
    for core in range(8):
        bat, par = core // 2, core % 2
        oc = np.asarray(results[core][out_name])
        od = oc.reshape(128, DC, TC).transpose(1, 0, 2).reshape(D, TC)
        out[bat][rows[par], :] = od.T
    return out


def kernel(**inputs):
    cfg = TINY_CFG if os.environ.get("MK_TINY") else FULL_CFG
    mode = os.environ.get("MK_MODE", "fused")
    maps, rows = _shard_inputs(cfg, inputs)

    if mode == "fused":
        nc = build_program(cfg, layers=(0, 1))
        in_maps = []
        for xc, c0, c1 in maps:
            m = {"x_in": xc}
            m.update(c0)
            m.update(c1)
            in_maps.append(m)
        res = run_bass_kernel_spmd(nc, in_maps, core_ids=list(range(8)))
        return _gather_output(cfg, res.results, rows)

    nc0 = build_program(cfg, layers=(0,), in_name="x_in", out_name="out")
    in_maps0 = []
    for xc, c0, c1 in maps:
        m = {"x_in": xc}
        m.update(c0)
        in_maps0.append(m)
    res0 = run_bass_kernel_spmd(nc0, in_maps0, core_ids=list(range(8)))
    nc1 = build_program(cfg, layers=(1,), in_name="x_in", out_name="out")
    in_maps1 = []
    for i, (xc, c0, c1) in enumerate(maps):
        m = {"x_in": np.ascontiguousarray(np.asarray(res0.results[i]["out"]))}
        m.update(c1)
        in_maps1.append(m)
    res1 = run_bass_kernel_spmd(nc1, in_maps1, core_ids=list(range(8)))
    return _gather_output(cfg, res1.results, rows)


# version 08
# speedup vs baseline: 1.1379x; 1.0674x over previous
import os
import math
import contextlib
import numpy as np
import ml_dtypes
import concourse.bass as bass
import concourse.mybir as mybir
from concourse.bass_utils import run_bass_kernel_spmd

F32 = mybir.dt.float32
BF16 = mybir.dt.bfloat16
AF = mybir.ActivationFunctionType
ALU = mybir.AluOpType
AX = mybir.AxisListType

GROUPS = [[0, 1], [2, 3], [4, 5], [6, 7]]
EVEN_TILES = [0, 3, 4, 7]
ODD_TILES = [1, 2, 5, 6]
NEG_BIG = -1.0e30
EPS = 1e-5

FULL_CFG = dict(D=2048, DE=1408, DS=2816, E=8, HD=128, NH=16, TT=512)
TINY_CFG = dict(D=512, DE=256, DS=512, E=8, HD=128, NH=4, TT=128)
MIDI_CFG = dict(D=1024, DE=512, DS=1024, E=8, HD=128, NH=8, TT=256)


def derive(cfg):
    c = dict(cfg)
    c["DC"] = c["D"] // 128
    c["EHC"] = c["DE"] // 128
    c["SHC"] = c["DS"] // 128
    c["TC"] = 4 * c["TT"]
    c["S"] = 8 * c["TT"]
    c["CPT"] = c["TT"] // 128
    c["KCH"] = c["S"] // 128
    assert c["NH"] * c["HD"] == c["D"]
    return c


def snake_rows(TT):
    rows = {}
    for par, tiles in ((0, EVEN_TILES), (1, ODD_TILES)):
        r = []
        for t in tiles:
            r.extend(range(t * TT, (t + 1) * TT))
        rows[par] = np.array(r, dtype=np.int64)
    return rows


def bufcol_map(CPT):
    m = {}
    for par, tiles in ((0, EVEN_TILES), (1, ODD_TILES)):
        for i, t in enumerate(tiles):
            for j in range(CPT):
                m[t * CPT + j] = par * 4 * CPT + i * CPT + j
    return m


class Sem:
    def __init__(self, handle, name):
        self.h = handle
        self.name = name
        self.n = 0

    def inc(self, instr, k=1):
        instr.then_inc(self.h, k)
        self.n += k
        return self.n


class B:

    def __init__(self, nc, cfg, layers=(0, 1), in_name="x_in", out_name="out"):
        self.nc = nc
        self.c = derive(cfg)
        self.layers = layers
        self.in_name = in_name
        self.out_name = out_name
        self.es = contextlib.ExitStack()
        self.dram = {}
        self.sb = {}
        self.sems = {}
        self.cast_ready = {}
        self.bufcol = bufcol_map(self.c["CPT"])
        self.wguard = {"A": None, "B": None}
        self.next_slot = "A"
        self.xn_guard = None
        self.scope_guards = []
        self.h_ready = 0
        self.kvq_extra_guard = None

    def declare(self):
        nc, c = self.nc, self.c
        D, DE, DS, E = c["D"], c["DE"], c["DS"], c["E"]
        TC, HD, NH = c["TC"], c["HD"], c["NH"]

        def din(name, shape):
            self.dram[name] = nc.dram_tensor(name, shape, F32, kind="ExternalInput").ap()

        def dint(name, shape, dt=BF16):
            self.dram[name] = nc.dram_tensor(name, shape, dt).ap()

        DC, EHC, SHC, NH2 = c["DC"], c["EHC"], c["SHC"], c["NH"]
        nhh = NH2 // 2
        din(self.in_name, [128, c["DC"] * TC])
        shp = {
            "wsg": [SHC, 128, DC * 128], "wsu": [SHC, 128, DC * 128],
            "wsd": [DC, 128, SHC * 128],
            "weg": [E, EHC, 128, DC * 128], "weu": [E, EHC, 128, DC * 128],
            "wed": [E, DC, 128, EHC * 128],
            "qw": [NH2, 128, DC * 128], "kw": [NH2, 128, DC * 128],
            "vw": [D // 512, 2, 128, (DC // 2) * 512],
            "ow": [2, DC // 4, 128, nhh * 512],
            "gp": [E, EHC, 128, DC * 128], "up": [E, EHC, 128, DC * 128],
            "dp": [E, DC, 128, EHC * 128],
        }
        if 0 in self.layers:
            din("w_dec", [2, D]); din("b_dec", [1, D]); din("dvv", [2, 1])
            din("wr0", [E, D])
            for nm in ("wsg", "wsu", "wsd", "weg", "weu", "wed"):
                din(nm, shp[nm]); dint("c_" + nm, shp[nm])
            dint("c_wdec", [c["DC"] // 2, 128, c["DC"] * 256])
        if 1 in self.layers:
            din("gw", [HD, HD]); din("gb", [1, HD]); din("wr1", [E, D])
            for nm in ("qw", "kw", "vw", "ow", "gp", "up", "dp"):
                din(nm, shp[nm]); dint("c_" + nm, shp[nm])
            dint("kvloc", [2 * D, TC])
            dint("kvg", [4 * D, TC])
            dint("qloc", [D, TC])
        self.dram[self.out_name] = nc.dram_tensor(self.out_name, [128, c["DC"] * TC],
                                                  F32, kind="ExternalOutput").ap()

    def alloc_sems(self):
        names = ["dIn", "dWA", "dWB", "dWd", "dSc", "dGp", "dKtA", "dKtB",
                 "sPE", "sDVE", "sACT", "sGP", "sCC", "dOut", "sAH",
                 "dKsA", "dKsB", "dVsA", "dVsB", "dQsA", "dQsB"]
        if 0 in self.layers:
            names += ["cgws"] + [f"cgwe{e}" for e in range(self.c["E"])]
        if 1 in self.layers:
            names += ["cgqk"] + [f"cgl1e{e}" for e in range(self.c["E"])]
        for n in names:
            self.sems[n] = Sem(self.es.enter_context(self.nc.semaphore(n)), n)

    def alloc_psum(self):
        self.ps = [self.es.enter_context(self.nc.psum_tensor(f"ps{i}", [128, 512], F32))
                   for i in range(6)]
        self.pr = [self.es.enter_context(self.nc.psum_tensor(f"pr{i}", [1, 512], F32))
                   for i in range(2)]
        self.bank_guard = [None] * 6
        self.pr_guard = None
        self._rr = 0

    def alloc_persistent(self):
        nc, c = self.nc, self.c
        TT, DC = c["TT"], c["DC"]

        def S(name, shape, dt):
            self.sb[name] = self.es.enter_context(nc.sbuf_tensor(name, shape, dt))

        S("x_res", [128, DC * c["TC"]], F32)
        S("wA", [128, max(4096, 2 * DC * 128)], BF16)
        S("wB", [128, max(4096, 2 * DC * 128)], BF16)
        S("ident", [128, 128], F32)
        S("onesc", [128, 1], BF16)
        S("ones1", [1, 128], F32)
        S("selmat", [8, 8 * 128], BF16)
        S("wr0_sb", [128, DC * 8], BF16)
        S("wr1_sb", [128, DC * 8], BF16)
        S("gwT_sb", [128, 128], BF16)
        S("gbneg", [128, 1], F32)
        S("bias_sb", [128, DC], F32)
        S("rlog", [128, 4 * 8], F32)
        S("escr", [128, 8], F32)
        S("escr2", [128, 8], F32)
        S("e8", [128, 8], F32)
        S("r_a", [128, 1], F32)
        S("r_b", [128, 1], F32)
        S("r_c", [128, 1], F32)
        S("r_d", [128, 1], F32)
        S("cb_tm", [128, 4 * 8], F32)
        S("cb_fm", [8, TT], BF16)
        S("cb_b", [128, TT], BF16)
        S("xtmp", [128, TT], F32)
        S("rowA", [1, TT], F32)
        S("lnd0", [128, 1], F32)
        S("lnd1", [128, 1], F32)
        S("w0col", [128, DC], F32)
        S("w1col", [128, DC], F32)
        S("stg16", [16, 128], F32)
        S("identB", [128, 128], BF16)
        S("dvt", [2, 1], F32)

    def take_bank(self, t):
        i = self._rr
        self._rr = (self._rr + 1) % 6
        g = self.bank_guard[i]
        if g is not None:
            t.wait_ge(g[0].h, g[1])
        return i

    def slot(self):
        s = self.next_slot
        self.next_slot = "B" if s == "A" else "A"
        return s

    def load_slab(self, slot, src_ap, off=0, first=True, ready=None):
        sems = self.sems
        dW = sems["dWA" if slot == "A" else "dWB"]
        buf = self.sb["wA" if slot == "A" else "wB"]
        guard = self.wguard[slot] if first else None
        a, bcols = int(src_ap.shape[1]), int(src_ap.shape[2])
        dest = buf[:, off:off + a * bcols].rearrange("p (a b) -> p a b", b=bcols)

        @self.block.sync
        def _(sy):
            if guard is not None:
                sy.wait_ge(guard[0].h, guard[1])
            if ready is not None:
                sy.wait_ge(ready[0].h, ready[1])
            dW.inc(sy.dma_start(out=dest, in_=src_ap), 16)
        return buf, dW.n

    def xres(self, dc, t0, n):
        c = self.c
        return self.sb["x_res"][:, dc * c["TC"] + t0: dc * c["TC"] + t0 + n]

    def emit_prologue(self):
        nc, c, block = self.nc, self.c, self.block
        sems, sb, dram = self.sems, self.sb, self.dram
        D, DC, TC, E = c["D"], c["DC"], c["TC"], c["E"]

        @block.scalar
        def _(s):
            sems["dIn"].inc(s.dma_start(out=sb["x_res"][:], in_=dram[self.in_name]), 16)
        @block.sync
        def _(s):
            if 0 in self.layers:
                sems["dSc"].inc(s.dma_start(out=sb["dvt"][:], in_=dram["dvv"]), 16)
                s.wait_ge(sems["dSc"].h, sems["dSc"].n)

        @block.gpsimd
        def _(g):
            g.iota(sb["ident"][:], [[1, 128]], base=0, channel_multiplier=-1,
                   allow_small_or_imprecise_dtypes=True)
            sems["sGP"].inc(g.iota(sb["wA"][0:8, 0:8 * 128], [[1, 8], [0, 128]],
                                   base=0, channel_multiplier=-1,
                                   allow_small_or_imprecise_dtypes=True))
        @block.vector
        def _(v):
            v.wait_ge(sems["sGP"].h, sems["sGP"].n)
            v.tensor_scalar(sb["ident"][:], sb["ident"][:], 0.0, None, ALU.is_equal)
            v.tensor_scalar(sb["selmat"][:], sb["wA"][0:8, 0:8 * 128], 0.0, None, ALU.is_equal)
            v.memset(sb["onesc"][:], 1.0)
            v.memset(sb["ones1"][:], 1.0)
            v.drain()
            sems["sDVE"].inc(v.tensor_copy(sb["identB"][:], sb["ident"][:]))
        self.wguard["A"] = (sems["sDVE"], sems["sDVE"].n)
        self.wguard["B"] = (sems["sDVE"], sems["sDVE"].n)

        ps = self.ps
        stage_jobs = []
        if 0 in self.layers:
            stage_jobs.append((dram["b_dec"], "bias_sb"))
            stage_jobs.append((dram["w_dec"][0:1, :], "w0col"))
            stage_jobs.append((dram["w_dec"][1:2, :], "w1col"))
        for srow, dname in stage_jobs:
            @block.sync
            def _(s, srow=srow):
                s.wait_ge(sems["sPE"].h, sems["sPE"].n)
                sems["dSc"].inc(s.dma_start(
                    out=sb["stg16"][0:DC, :],
                    in_=srow.rearrange("o (dc p) -> (o dc) p", p=128)), 16)
                s.wait_ge(sems["dSc"].h, sems["dSc"].n)
            @block.tensor
            def _(t):
                t.wait_ge(sems["dSc"].h, sems["dSc"].n)
                t.wait_ge(sems["sDVE"].h, sems["sDVE"].n)
                sems["sPE"].inc(t.transpose(ps[0][:, 0:DC], sb["stg16"][0:DC, :],
                                            sb["ident"][0:DC, 0:DC]))
            @block.vector
            def _(v, dname=dname):
                v.wait_ge(sems["sPE"].h, sems["sPE"].n)
                v.drain()
                sems["sDVE"].inc(v.tensor_copy(sb[dname][:, 0:DC], ps[0][:, 0:DC]))

        for li, dname in ((0, "wr0"), (1, "wr1")):
            if li not in self.layers:
                continue
            for dc in range(DC):
                @block.sync
                def _(s, li=li, dc=dc):
                    s.wait_ge(sems["sPE"].h, sems["sPE"].n)
                    sems["dSc"].inc(s.dma_start(
                        out=sb["stg16"][0:8, :],
                        in_=dram["wr0" if li == 0 else "wr1"][:, dc * 128:(dc + 1) * 128]), 16)
                    s.wait_ge(sems["dSc"].h, sems["dSc"].n)
                @block.tensor
                def _(t, dc=dc):
                    t.wait_ge(sems["dSc"].h, sems["dSc"].n)
                    if dc == 0:
                        t.wait_ge(sems["sDVE"].h, sems["sDVE"].n)
                    sems["sPE"].inc(t.transpose(ps[1][:, dc * 8:(dc + 1) * 8],
                                                sb["stg16"][0:8, :],
                                                sb["ident"][0:8, 0:8]))
            @block.vector
            def _(v, dname=dname):
                v.wait_ge(sems["sPE"].h, sems["sPE"].n)
                v.drain()
                sems["sDVE"].inc(v.tensor_copy(sb[dname + "_sb"][:], ps[1][:, 0:DC * 8]))
        self.bank_guard[0] = (sems["sDVE"], sems["sDVE"].n)
        self.bank_guard[1] = (sems["sDVE"], sems["sDVE"].n)

        @block.sync
        def _(s):
            if 1 in self.layers:
                s.wait_ge(sems["sPE"].h, sems["sPE"].n)
                sems["dSc"].inc(s.dma_start(out=sb["xtmp"][:, 0:128], in_=dram["gw"]), 16)
                s.wait_ge(sems["dSc"].h, sems["dSc"].n)
                sems["dSc"].inc(s.dma_start(out=sb["stg16"][0:1, 0:128], in_=dram["gb"]), 16)
                s.wait_ge(sems["dSc"].h, sems["dSc"].n)
        @block.vector
        def _(v):
            if 1 in self.layers:
                v.wait_ge(sems["dSc"].h, sems["dSc"].n)
                v.drain()
                sems["sDVE"].inc(v.tensor_copy(sb["gwT_sb"][:], sb["xtmp"][:, 0:128]))
        @block.tensor
        def _(t):
            if 1 in self.layers:
                t.wait_ge(sems["dSc"].h, sems["dSc"].n)
                sems["sPE"].inc(t.transpose(ps[2][:, 0:1], sb["stg16"][0:1, 0:128],
                                            sb["ident"][0:1, 0:1]))
        @block.vector
        def _(v):
            if 1 in self.layers:
                v.wait_ge(sems["sPE"].h, sems["sPE"].n)
                v.drain()
                sems["sDVE"].inc(v.tensor_scalar(sb["gbneg"][:], ps[2][:, 0:1],
                                                 -1.0, None, ALU.mult))
        self.bank_guard[2] = (sems["sDVE"], sems["sDVE"].n)

        if 0 in self.layers:
            self.emit_wdecay()
        self.emit_casts()

    def emit_casts(self):
        c, block = self.c, self.block
        sems, dram = self.sems, self.dram
        E = c["E"]
        CH = 1 << 19

        def pieces(g, cg, dst, srcname):
            if "!" in srcname:
                nm, e = srcname.split("!")
                srcap = dram[nm][int(e)]
            else:
                srcap = dram[srcname]
            n = 1
            for d in srcap.shape:
                n *= d
            flat_src = srcap.rearrange(
                " ".join(f"d{i}" for i in range(len(srcap.shape))) + " -> ("
                + " ".join(f"d{i}" for i in range(len(srcap.shape))) + ")")
            flat_dst = dst.rearrange(
                " ".join(f"d{i}" for i in range(len(dst.shape))) + " -> ("
                + " ".join(f"d{i}" for i in range(len(dst.shape))) + ")")
            o = 0
            while o < n:
                m = min(CH, n - o)
                cg.inc(g.dma_start(out=flat_dst[o:o + m], in_=flat_src[o:o + m]), 16)
                o += m

        @block.gpsimd
        def _(g):
            g.wait_ge(sems["dIn"].h, 16)
            g.wait_ge(sems["dSc"].h, sems["dSc"].n)
            g.wait_ge(sems["dGp"].h, sems["dGp"].n)
            if 0 in self.layers:
                cg = sems["cgws"]
                for nm in ("wsg", "wsu", "wsd"):
                    pieces(g, cg, dram["c_" + nm], nm)
                self.cast_ready["ws"] = (cg, cg.n)
                for e in range(E):
                    cge = sems[f"cgwe{e}"]
                    for nm in ("weg", "weu", "wed"):
                        pieces(g, cge, dram["c_" + nm][e], nm + f"!{e}")
                    self.cast_ready[f"we{e}"] = (cge, cge.n)
            if 1 in self.layers:
                cg = sems["cgqk"]
                for nm in ("qw", "kw", "vw", "ow"):
                    pieces(g, cg, dram["c_" + nm], nm)
                self.cast_ready["qkvo"] = (cg, cg.n)
                for e in range(E):
                    cge = sems[f"cgl1e{e}"]
                    for nm in ("gp", "up", "dp"):
                        pieces(g, cge, dram["c_" + nm][e], nm + f"!{e}")
                    self.cast_ready[f"l1e{e}"] = (cge, cge.n)

    def emit_wdecay(self):
        nc, c, block = self.nc, self.c, self.block
        sems, sb, dram = self.sems, self.sb, self.dram
        D, DC = c["D"], c["DC"]
        ps, pr = self.ps, self.pr

        @block.vector
        def _(v):
            v.wait_ge(sems["dSc"].h, sems["dSc"].n)
            sems["sDVE"].inc(v.tensor_scalar(sb["dvt"][:], sb["dvt"][:], 0.9, 1.0,
                                             ALU.max, ALU.min))
        @block.scalar
        def _(s):
            s.wait_ge(sems["sDVE"].h, sems["sDVE"].n)
            sems["sACT"].inc(s.activation(sb["dvt"][:], sb["dvt"][:], AF.Ln))
        @block.sync
        def _(s):
            s.wait_ge(sems["sACT"].h, sems["sACT"].n)
            sems["dSc"].inc(s.dma_start(out=sb["r_a"][0:1, 0:1], in_=sb["dvt"][0:1, 0:1]), 16)
            s.wait_ge(sems["dSc"].h, sems["dSc"].n)
            sems["dSc"].inc(s.dma_start(out=sb["r_b"][0:1, 0:1], in_=sb["dvt"][1:2, 0:1]), 16)
            s.wait_ge(sems["dSc"].h, sems["dSc"].n)
        @block.tensor
        def _(t):
            t.wait_ge(sems["dSc"].h, sems["dSc"].n)
            t.wait_ge(sems["sDVE"].h, sems["sDVE"].n)
            t.matmul(ps[0][:, 0:1], sb["ones1"][0:1, :], sb["r_a"][0:1, 0:1],
                     start=True, stop=True)
            sems["sPE"].inc(t.matmul(ps[1][:, 0:1], sb["ones1"][0:1, :],
                                     sb["r_b"][0:1, 0:1], start=True, stop=True))
        @block.vector
        def _(v):
            v.wait_ge(sems["sPE"].h, sems["sPE"].n)
            v.tensor_copy(sb["lnd0"][:], ps[0][:, 0:1])
            sems["sDVE"].inc(v.tensor_copy(sb["lnd1"][:], ps[1][:, 0:1]))

        with contextlib.ExitStack() as scope:
            vrow = scope.enter_context(nc.sbuf_tensor("wd_v", [1, D], F32))
            r0 = scope.enter_context(nc.sbuf_tensor("wd_r0", [1, D], F32))
            r1 = scope.enter_context(nc.sbuf_tensor("wd_r1", [1, D], F32))
            w1r = scope.enter_context(nc.sbuf_tensor("wd_w1r", [1, D], F32))
            mbig0 = scope.enter_context(nc.sbuf_tensor("wd_m", [128, 2 * D], BF16))
            wbfF = scope.enter_context(nc.sbuf_tensor("wd_bf", [128, D], BF16))
            ecol0 = scope.enter_context(nc.sbuf_tensor("wd_e0", [128, 1], F32))
            ecol1 = scope.enter_context(nc.sbuf_tensor("wd_e1", [128, 1], F32))
            cA = scope.enter_context(nc.sbuf_tensor("wd_ca", [128, 1], F32))
            c0T = scope.enter_context(nc.sbuf_tensor("wd_c0t", [1, 128], F32))
            c1T = scope.enter_context(nc.sbuf_tensor("wd_c1t", [1, 128], F32))

            @block.vector
            def _(v):
                v.drain()
                sems["sDVE"].inc(v.memset(mbig0[:], 1.0))
            @block.gpsimd
            def _(g):
                g.wait_ge(sems["sDVE"].h, sems["sDVE"].n)
                g.affine_select(mbig0[:], mbig0[:], [[1, 2 * D]], ALU.is_ge, 0.0,
                                base=-D, channel_multiplier=-1)
                sems["sGP"].inc(g.iota(ecol0[:], [[1, 1]], base=0, channel_multiplier=-1,
                                       allow_small_or_imprecise_dtypes=True))
            @block.sync
            def _(s):
                sems["dSc"].inc(s.dma_start(out=w1r[0:1, :], in_=dram["w_dec"][1:2, :]), 16)
                s.wait_ge(sems["dSc"].h, sems["dSc"].n)
            w1r_cnt = sems["dSc"].n
            @block.vector
            def _(v):
                v.wait_ge(sems["sGP"].h, sems["sGP"].n)
                v.drain()
                v.tensor_scalar(ecol1[:], ecol0[:], sb["lnd1"][:, 0:1], None, ALU.mult)
                v.drain()
                sems["sDVE"].inc(v.tensor_scalar(ecol0[:], ecol0[:], sb["lnd0"][:, 0:1],
                                                 None, ALU.mult))
            @block.scalar
            def _(s):
                s.wait_ge(sems["sDVE"].h, sems["sDVE"].n)
                s.drain()
                s.activation(ecol0[:], ecol0[:], AF.Exp)
                s.drain()
                sems["sACT"].inc(s.activation(ecol1[:], ecol1[:], AF.Exp))
            @block.tensor
            def _(t):
                t.wait_ge(sems["sACT"].h, sems["sACT"].n)
                sems["sPE"].inc(t.transpose(ps[0][0:1, 0:128], ecol1[:],
                                            sb["ident"][:]))
            @block.vector
            def _(v):
                v.wait_ge(sems["sPE"].h, sems["sPE"].n)
                v.drain()
                sems["sDVE"].inc(v.tensor_copy(c1T[0:1, :], ps[0][0:1, 0:128]))

            nwin = D // 512
            for kc in range(DC):
                @block.vector
                def _(v, kc=kc):
                    v.wait_ge(sems["sPE"].h, sems["sPE"].n)
                    v.drain()
                    sems["sDVE"].inc(v.tensor_scalar(cA[:], ecol0[:],
                                                     sb["w0col"][:, kc:kc + 1],
                                                     None, ALU.mult))
                @block.gpsimd
                def _(g, kc=kc):
                    g.wait_ge(sems["sDVE"].h, sems["sDVE"].n)
                    sems["sGP"].inc(g.iota(vrow[0:1, :], [[1, D]], base=-128 * kc,
                                           channel_multiplier=0,
                                           allow_small_or_imprecise_dtypes=True))
                @block.tensor
                def _(t, kc=kc):
                    t.wait_ge(sems["sDVE"].h, sems["sDVE"].n)
                    sems["sPE"].inc(t.transpose(ps[0][0:1, 0:128], cA[:],
                                                sb["ident"][:]))
                @block.vector
                def _(v, kc=kc):
                    v.wait_ge(sems["sPE"].h, sems["sPE"].n)
                    v.wait_ge(sems["sGP"].h, sems["sGP"].n)
                    v.drain()
                    v.tensor_copy(c0T[0:1, :], ps[0][0:1, 0:128])
                    v.tensor_scalar(vrow[0:1, :], vrow[0:1, :], 0.0, None, ALU.max)
                    v.drain()
                    v.tensor_scalar(r0[0:1, :], vrow[0:1, :], sb["lnd0"][0:1, 0:1],
                                    None, ALU.mult)
                    v.drain()
                    sems["sDVE"].inc(v.tensor_scalar(r1[0:1, :], vrow[0:1, :],
                                                     sb["lnd1"][0:1, 0:1], None, ALU.mult))
                @block.scalar
                def _(s, kc=kc):
                    s.wait_ge(sems["sDVE"].h, sems["sDVE"].n)
                    s.drain()
                    s.activation(r0[0:1, :], r0[0:1, :], AF.Exp)
                    s.drain()
                    sems["sACT"].inc(s.activation(r1[0:1, :], r1[0:1, :], AF.Exp))
                @block.vector
                def _(v, kc=kc):
                    v.wait_ge(sems["sACT"].h, sems["sACT"].n)
                    v.wait_ge(sems["dSc"].h, w1r_cnt)
                    v.drain()
                    sems["sDVE"].inc(v.tensor_tensor(r1[0:1, :], r1[0:1, :],
                                                     w1r[0:1, :], ALU.mult))
                pe_w = sems["sPE"].n
                dve_w = sems["sDVE"].n
                @block.tensor
                def _(t, kc=kc, pe_w=pe_w, dve_w=dve_w):
                    t.wait_ge(sems["sDVE"].h, dve_w)
                    for win in range(nwin):
                        if win >= 2:
                            t.wait_ge(sems["sDVE"].h, dve_w + win - 1)
                        bkw = 2 + win % 2
                        t.matmul(ps[bkw][:, 0:512], c0T[0:1, :],
                                 r0[0:1, win * 512:(win + 1) * 512],
                                 start=True, stop=False)
                        t.matmul(ps[bkw][:, 0:512], c1T[0:1, :],
                                 r1[0:1, win * 512:(win + 1) * 512],
                                 start=False, stop=True).then_inc(sems["sPE"].h, 1)
                sems["sPE"].n = pe_w + nwin
                dwd_prev = sems["dWd"].n
                @block.vector
                def _(v, kc=kc, pe_w=pe_w, dwd_prev=dwd_prev):
                    v.wait_ge(sems["dWd"].h, dwd_prev)
                    for win in range(nwin):
                        v.wait_ge(sems["sPE"].h, pe_w + win + 1)
                        v.drain()
                        v.tensor_tensor(
                            wbfF[:, win * 512:(win + 1) * 512],
                            ps[2 + win % 2][:, 0:512],
                            mbig0[:, D - 128 * kc + win * 512: D - 128 * kc + (win + 1) * 512],
                            ALU.mult).then_inc(sems["sDVE"].h, 1)
                sems["sDVE"].n = dve_w + nwin
                @block.gpsimd
                def _(sy, kc=kc):
                    sy.wait_ge(sems["sDVE"].h, sems["sDVE"].n)
                    sems["dWd"].inc(sy.dma_start(
                        out=dram["c_wdec"][:, :, kc * 256:(kc + 1) * 256].rearrange(
                            "dcp p j -> p dcp j"),
                        in_=wbfF[:].rearrange("p (dcp j) -> p dcp j", j=256)), 16)
            self.cast_ready["wdec"] = (sems["dWd"], sems["dWd"].n)
            self.scope_guards.append((sems["dWd"], sems["dWd"].n))
            self.scope_guards.append((sems["sDVE"], sems["sDVE"].n))

    def emit_stats_xn(self, tile, extra_guards=()):
        c, block = self.c, self.block
        sems, sb = self.sems, self.sb
        DC, TT, D = c["DC"], c["TT"], c["D"]
        t0 = tile * TT
        ps, pr = self.ps, self.pr
        xnb = sb["xnb"]
        xsq = sb["H"]

        @block.vector
        def _(v):
            v.wait_ge(sems["dIn"].h, 16)
            if self.xn_guard is not None:
                v.wait_ge(self.xn_guard[0].h, self.xn_guard[1])
            for g in extra_guards:
                v.wait_ge(g[0].h, g[1])
            v.drain()
            for dc in range(DC):
                v.tensor_copy(xnb[:, dc * TT:(dc + 1) * TT], self.xres(dc, t0, TT))
            for dc in range(DC - 1):
                v.tensor_tensor(xsq[:, dc * TT:(dc + 1) * TT],
                                self.xres(dc, t0, TT), self.xres(dc, t0, TT), ALU.mult)
            dc = DC - 1
            sems["sDVE"].inc(v.tensor_tensor(xsq[:, dc * TT:(dc + 1) * TT],
                                             self.xres(dc, t0, TT),
                                             self.xres(dc, t0, TT), ALU.mult))
        @block.tensor
        def _(t):
            t.wait_ge(sems["sDVE"].h, sems["sDVE"].n)
            if self.pr_guard is not None:
                t.wait_ge(self.pr_guard[0].h, self.pr_guard[1])
            for dc in range(DC):
                t.matmul(pr[0][0:1, 0:TT], sb["onesc"][:, 0:1],
                         xnb[:, dc * TT:(dc + 1) * TT],
                         start=(dc == 0), stop=(dc == DC - 1))
            for dc in range(DC):
                last = t.matmul(pr[1][0:1, 0:TT], sb["onesc"][:, 0:1],
                                xsq[:, dc * TT:(dc + 1) * TT],
                                start=(dc == 0), stop=(dc == DC - 1))
            sems["sPE"].inc(last)
        @block.vector
        def _(v):
            v.wait_ge(sems["sPE"].h, sems["sPE"].n)
            v.drain()
            v.tensor_scalar(sb["rowA"][0:1, 0:TT], pr[0][0:1, 0:TT], 1.0 / D, None, ALU.mult)
            v.tensor_scalar(sb["rowB"][0:1, 0:TT], pr[1][0:1, 0:TT], 1.0 / D, None, ALU.mult)
            v.drain()
            v.tensor_tensor(sb["rowC"][0:1, 0:TT], sb["rowA"][0:1, 0:TT],
                            sb["rowA"][0:1, 0:TT], ALU.mult)
            v.drain()
            v.tensor_tensor(sb["rowB"][0:1, 0:TT], sb["rowB"][0:1, 0:TT],
                            sb["rowC"][0:1, 0:TT], ALU.subtract)
            v.drain()
            sems["sDVE"].inc(v.tensor_scalar(sb["rowB"][0:1, 0:TT], sb["rowB"][0:1, 0:TT],
                                             EPS, None, ALU.add))
        self.pr_guard = (sems["sDVE"], sems["sDVE"].n)
        @block.scalar
        def _(s):
            s.wait_ge(sems["sDVE"].h, sems["sDVE"].n)
            s.drain()
            s.activation(sb["rowB"][0:1, 0:TT], sb["rowB"][0:1, 0:TT], AF.Ln)
            s.drain()
            sems["sACT"].inc(s.activation(sb["rowB"][0:1, 0:TT], sb["rowB"][0:1, 0:TT],
                                          AF.Exp, scale=-0.5))
        @block.tensor
        def _(t):
            t.wait_ge(sems["sACT"].h, sems["sACT"].n)
            bm = self.take_bank(t)
            br = self.take_bank(t)
            self._mb, self._rb = bm, br
            t.matmul(ps[bm][:, 0:TT], sb["ones1"][0:1, :], sb["rowA"][0:1, 0:TT],
                     start=True, stop=True)
            sems["sPE"].inc(t.matmul(ps[br][:, 0:TT], sb["ones1"][0:1, :],
                                     sb["rowB"][0:1, 0:TT], start=True, stop=True))
        @block.vector
        def _(v):
            v.wait_ge(sems["sPE"].h, sems["sPE"].n)
            v.drain()
            v.tensor_copy(sb["mr_b"][:, 0:TT], ps[self._mb][:, 0:TT])
            sems["sDVE"].inc(v.tensor_copy(sb["mr_b"][:, TT:2 * TT], ps[self._rb][:, 0:TT]))
        self.bank_guard[self._mb] = (sems["sDVE"], sems["sDVE"].n)
        self.bank_guard[self._rb] = (sems["sDVE"], sems["sDVE"].n)
        self.xn_chunk = {}
        @block.vector
        def _(v):
            v.drain()
            for dc in range(DC):
                v.tensor_tensor(sb["xtmp"][:, 0:TT], self.xres(dc, t0, TT),
                                sb["mr_b"][:, 0:TT], ALU.subtract)
                v.drain()
                v.tensor_tensor(xnb[:, dc * TT:(dc + 1) * TT], sb["xtmp"][:, 0:TT],
                                sb["mr_b"][:, TT:2 * TT], ALU.mult
                                ).then_inc(sems["sDVE"].h, 1)
                sems["sDVE"].n += 0
                self.xn_chunk[dc] = sems["sDVE"].n + dc + 1
                v.drain()
        sems["sDVE"].n += DC
        self.xn_ready = sems["sDVE"].n

    def emit_l0_attn(self, tile):
        c, block = self.c, self.block
        sems, sb, dram = self.sems, self.sb, self.dram
        DC, TT, TC = c["DC"], c["TT"], c["TC"]
        t0 = tile * TT
        ps = self.ps
        wdec_ready = self.cast_ready["wdec"]

        for dcp in range(DC // 2):
            slot = self.slot()
            src = dram["c_wdec"][dcp].rearrange("p (kc j) -> p kc j", j=256)
            buf, wcnt = self.load_slab(slot, src, ready=wdec_ready)
            for sub in range(2):
                dc = dcp * 2 + sub
                @block.tensor
                def _(t, dc=dc, sub=sub, buf=buf, wcnt=wcnt, slot=slot):
                    dW = sems["dWA" if slot == "A" else "dWB"]
                    t.wait_ge(dW.h, wcnt)
                    t.wait_ge(sems["sDVE"].h, self.xn_ready)
                    bk = self.take_bank(t)
                    self._bk = bk
                    for kc in range(DC):
                        last = t.matmul(ps[bk][:, 0:TT],
                                        buf[:, kc * 256 + sub * 128: kc * 256 + sub * 128 + 128],
                                        sb["xnb"][:, kc * TT:(kc + 1) * TT],
                                        start=(kc == 0), stop=(kc == DC - 1))
                    sems["sPE"].inc(last)
                self.wguard[slot] = (sems["sPE"], sems["sPE"].n)
                @block.vector
                def _(v, dc=dc):
                    v.wait_ge(sems["sPE"].h, sems["sPE"].n)
                    v.drain()
                    sems["sDVE"].inc(v.scalar_tensor_tensor(
                        self.xres(dc, t0, TT), ps[self._bk][:, 0:TT],
                        sb["bias_sb"][:, dc:dc + 1], self.xres(dc, t0, TT),
                        ALU.add, ALU.add))
                self.bank_guard[self._bk] = (sems["sDVE"], sems["sDVE"].n)

    def emit_router(self, tile, layer):
        c, block = self.c, self.block
        sems, sb = self.sems, self.sb
        DC, TT, CPT = c["DC"], c["TT"], c["CPT"]
        ps = self.ps
        wr = sb["wr0_sb"] if layer == 0 else sb["wr1_sb"]

        @block.tensor
        def _(t):
            rb = self.take_bank(t)
            self._rbank = rb
            for tsub in range(CPT):
                for dc in range(DC):
                    if tsub == 0:
                        t.wait_ge(sems["sDVE"].h, self.xn_chunk[dc])
                    last = t.matmul(ps[rb][:, tsub * 8:(tsub + 1) * 8],
                                    sb["xnb"][:, dc * TT + tsub * 128: dc * TT + tsub * 128 + 128],
                                    wr[:, dc * 8:(dc + 1) * 8],
                                    start=(dc == 0), stop=(dc == DC - 1))
            sems["sPE"].inc(last)
        @block.vector
        def _(v):
            v.wait_ge(sems["sPE"].h, sems["sPE"].n)
            v.drain()
            sems["sDVE"].inc(v.tensor_copy(sb["rlog"][:, 0:CPT * 8], ps[self._rbank][:, 0:CPT * 8]))
        self.bank_guard[self._rbank] = (sems["sDVE"], sems["sDVE"].n)

        for tsub in range(CPT):
            sl = slice(tsub * 8, tsub * 8 + 8)
            if layer == 0:
                @block.vector
                def _(v, sl=sl):
                    v.drain()
                    sems["sDVE"].inc(v.reduce_max(sb["r_a"][:], sb["rlog"][:, sl],
                                                  AX.X, negate=True))
                @block.scalar
                def _(s, sl=sl):
                    s.wait_ge(sems["sDVE"].h, sems["sDVE"].n)
                    s.drain()
                    sems["sACT"].inc(s.activation(sb["escr"][:], sb["rlog"][:, sl],
                                                  AF.Exp, bias=sb["r_a"][:, 0:1]))
                @block.vector
                def _(v, tsub=tsub, sl=sl):
                    v.wait_ge(sems["sACT"].h, sems["sACT"].n)
                    v.drain()
                    v.tensor_reduce(sb["r_b"][:], sb["escr"][:], AX.X, ALU.add)
                    v.drain()
                    v.reciprocal(sb["r_c"][:], sb["r_b"][:])
                    v.drain()
                    v.tensor_scalar(sb["escr"][:], sb["escr"][:], sb["r_c"][:, 0:1],
                                    None, ALU.mult)
                    v.drain()
                    v.reduce_max(sb["r_a"][:], sb["escr"][:], AX.X)
                    v.drain()
                    v.tensor_scalar(sb["escr2"][:], sb["escr"][:], sb["r_a"][:, 0:1],
                                    None, ALU.is_equal)
                    v.drain()
                    v.scalar_tensor_tensor(sb["e8"][:], sb["escr2"][:], NEG_BIG,
                                           sb["escr"][:], ALU.mult, ALU.add)
                    v.drain()
                    v.reduce_max(sb["r_b"][:], sb["e8"][:], AX.X)
                    v.drain()
                    v.tensor_scalar(sb["e8"][:], sb["escr"][:], sb["r_b"][:, 0:1],
                                    None, ALU.is_ge)
                    v.drain()
                    sems["sDVE"].inc(v.tensor_tensor(sb["cb_tm"][:, sl], sb["escr"][:],
                                                     sb["e8"][:], ALU.mult))
            else:
                @block.vector
                def _(v, sl=sl):
                    v.drain()
                    v.reduce_max(sb["r_a"][:], sb["rlog"][:, sl], AX.X)
                    v.drain()
                    v.tensor_scalar(sb["escr2"][:], sb["rlog"][:, sl], sb["r_a"][:, 0:1],
                                    None, ALU.is_equal)
                    v.drain()
                    v.scalar_tensor_tensor(sb["e8"][:], sb["escr2"][:], NEG_BIG,
                                           sb["rlog"][:, sl], ALU.mult, ALU.add)
                    v.drain()
                    v.reduce_max(sb["r_b"][:], sb["e8"][:], AX.X)
                    v.drain()
                    v.tensor_scalar(sb["escr"][:], sb["rlog"][:, sl], sb["r_b"][:, 0:1],
                                    None, ALU.is_equal)
                    v.drain()
                    sems["sDVE"].inc(v.tensor_tensor(sb["r_c"][:], sb["r_a"][:],
                                                     sb["r_b"][:], ALU.subtract))
                @block.scalar
                def _(s):
                    s.wait_ge(sems["sDVE"].h, sems["sDVE"].n)
                    s.drain()
                    sems["sACT"].inc(s.activation(sb["r_d"][:], sb["r_c"][:], AF.Exp,
                                                  scale=-1.0))
                @block.vector
                def _(v, sl=sl):
                    v.wait_ge(sems["sACT"].h, sems["sACT"].n)
                    v.drain()
                    v.tensor_scalar(sb["r_d"][:], sb["r_d"][:], 1.0, None, ALU.add)
                    v.drain()
                    v.reciprocal(sb["r_c"][:], sb["r_d"][:])
                    v.drain()
                    v.tensor_scalar(sb["r_d"][:], sb["r_c"][:], -1.0, 1.0,
                                    ALU.mult, ALU.add)
                    v.drain()
                    v.tensor_scalar(sb["escr2"][:], sb["escr2"][:], sb["r_c"][:, 0:1],
                                    None, ALU.mult)
                    v.drain()
                    sems["sDVE"].inc(v.scalar_tensor_tensor(
                        sb["cb_tm"][:, sl], sb["escr"][:], sb["r_d"][:, 0:1],
                        sb["escr2"][:], ALU.mult, ALU.add))
        @block.tensor
        def _(t):
            t.wait_ge(sems["sDVE"].h, sems["sDVE"].n)
            cbk = self.take_bank(t)
            self._cbk = cbk
            for tsub in range(CPT):
                last = t.transpose(ps[cbk][0:8, tsub * 128:(tsub + 1) * 128],
                                   sb["cb_tm"][:, tsub * 8:(tsub + 1) * 8], sb["ident"][:])
            sems["sPE"].inc(last)
        @block.vector
        def _(v):
            v.wait_ge(sems["sPE"].h, sems["sPE"].n)
            v.drain()
            sems["sDVE"].inc(v.tensor_copy(sb["cb_fm"][0:8, 0:TT],
                                           ps[self._cbk][0:8, 0:CPT * 128]))
        self.bank_guard[self._cbk] = (sems["sDVE"], sems["sDVE"].n)
        self.cb_ready = sems["sDVE"].n

    def emit_cb_broadcast(self, e):
        sems, sb, block = self.sems, self.sb, self.block
        TT = self.c["TT"]
        ps = self.ps

        @block.tensor
        def _(t):
            t.wait_ge(sems["sDVE"].h, self.cb_ready)
            bk = self.take_bank(t)
            self._cbb = bk
            sems["sPE"].inc(t.matmul(ps[bk][:, 0:TT],
                                     sb["selmat"][0:8, e * 128:(e + 1) * 128],
                                     sb["cb_fm"][0:8, 0:TT], start=True, stop=True))
        @block.vector
        def _(v):
            v.wait_ge(sems["sPE"].h, sems["sPE"].n)
            v.drain()
            sems["sDVE"].inc(v.tensor_copy(sb["cb_b"][:, 0:TT], ps[self._cbb][:, 0:TT]))
        self.bank_guard[self._cbb] = (sems["sDVE"], sems["sDVE"].n)
        self.cbb_ready = sems["sDVE"].n

    def emit_gate_up_down(self, tile, gsrc, usrc, dsrc, nhc, with_cb, ready):
        c, block = self.c, self.block
        sems, sb = self.sems, self.sb
        DC, TT, TC = c["DC"], c["TT"], c["TC"]
        t0 = tile * TT
        ps = self.ps
        H = sb["H"]
        pe0, act0, dve0 = sems["sPE"].n, sems["sACT"].n, sems["sDVE"].n

        wcnt = {}
        slots = {}
        for hc in range(nhc):
            slot = self.slot()
            slots[hc] = slot
            gs = gsrc[hc].rearrange("p (dc j) -> p dc j", j=128)
            us = usrc[hc].rearrange("p (dc j) -> p dc j", j=128)
            self.load_slab(slot, gs, off=0, ready=ready if hc == 0 else None)
            _, wcnt[hc] = self.load_slab(slot, us, off=DC * 128, first=False)
            self.wguard[slot] = (sems["sPE"], pe0 + hc + 1)

        banks = {}
        @block.tensor
        def _(t):
            for hc in range(nhc):
                slot = slots[hc]
                buf = sb["wA" if slot == "A" else "wB"]
                t.wait_ge(sems["dWA" if slot == "A" else "dWB"].h, wcnt[hc])
                bg = self.take_bank(t)
                bu = self.take_bank(t)
                banks[hc] = (bg, bu)
                for dc in range(DC):
                    if hc == 0:
                        t.wait_ge(sems["sDVE"].h, self.xn_chunk[dc])
                    t.matmul(ps[bg][:, 0:TT], buf[:, dc * 128:(dc + 1) * 128],
                             sb["xnb"][:, dc * TT:(dc + 1) * TT],
                             start=(dc == 0), stop=(dc == DC - 1))
                for dc in range(DC):
                    last = t.matmul(ps[bu][:, 0:TT],
                                    buf[:, DC * 128 + dc * 128: DC * 128 + (dc + 1) * 128],
                                    sb["xnb"][:, dc * TT:(dc + 1) * TT],
                                    start=(dc == 0), stop=(dc == DC - 1))
                last.then_inc(sems["sPE"].h, 1)
                self.bank_guard[bg] = (sems["sACT"], act0 + hc + 1)
                self.bank_guard[bu] = (sems["sDVE"], dve0 + 2 * (hc + 1))
        sems["sPE"].n = pe0 + nhc

        @block.scalar
        def _(s):
            for hc in range(nhc):
                s.wait_ge(sems["sPE"].h, pe0 + hc + 1)
                s.drain()
                s.activation(H[:, hc * TT:(hc + 1) * TT], ps[banks[hc][0]][:, 0:TT],
                             AF.Silu).then_inc(sems["sACT"].h, 1)
        sems["sACT"].n = act0 + nhc

        @block.vector
        def _(v):
            if with_cb:
                v.wait_ge(sems["sDVE"].h, self.cbb_ready)
            for hc in range(nhc):
                v.wait_ge(sems["sPE"].h, pe0 + hc + 1)
                v.wait_ge(sems["sACT"].h, act0 + hc + 1)
                v.drain()
                if with_cb:
                    v.tensor_tensor(sb["ue"][:, 0:TT], ps[banks[hc][1]][:, 0:TT],
                                    sb["cb_b"][:, 0:TT], ALU.mult
                                    ).then_inc(sems["sDVE"].h, 1)
                else:
                    v.tensor_copy(sb["ue"][:, 0:TT], ps[banks[hc][1]][:, 0:TT]
                                  ).then_inc(sems["sDVE"].h, 1)
                v.drain()
                v.tensor_tensor(H[:, hc * TT:(hc + 1) * TT],
                                H[:, hc * TT:(hc + 1) * TT],
                                sb["ue"][:, 0:TT], ALU.mult).then_inc(sems["sDVE"].h, 1)
        sems["sDVE"].n = dve0 + 2 * nhc
        self.h_ready = sems["sDVE"].n

        pe1, dve1 = sems["sPE"].n, sems["sDVE"].n
        dwcnt = {}
        dslots = {}
        for dc in range(DC):
            slot = self.slot()
            dslots[dc] = slot
            dsl = dsrc[dc].rearrange("p (hc j) -> p hc j", j=128)
            _, dwcnt[dc] = self.load_slab(slot, dsl)
            self.wguard[slot] = (sems["sPE"], pe1 + dc + 1)
        dbanks = {}
        @block.tensor
        def _(t):
            t.wait_ge(sems["sDVE"].h, self.h_ready)
            for dc in range(DC):
                slot = dslots[dc]
                buf = sb["wA" if slot == "A" else "wB"]
                t.wait_ge(sems["dWA" if slot == "A" else "dWB"].h, dwcnt[dc])
                bk = self.take_bank(t)
                dbanks[dc] = bk
                for hc in range(nhc):
                    last = t.matmul(ps[bk][:, 0:TT], buf[:, hc * 128:(hc + 1) * 128],
                                    H[:, hc * TT:(hc + 1) * TT],
                                    start=(hc == 0), stop=(hc == nhc - 1))
                last.then_inc(sems["sPE"].h, 1)
                self.bank_guard[bk] = (sems["sDVE"], dve1 + dc + 1)
        sems["sPE"].n = pe1 + DC
        @block.vector
        def _(v):
            for dc in range(DC):
                v.wait_ge(sems["sPE"].h, pe1 + dc + 1)
                v.drain()
                v.scalar_tensor_tensor(
                    self.xres(dc, t0, TT), ps[dbanks[dc]][:, 0:TT], 0.0,
                    self.xres(dc, t0, TT), ALU.add, ALU.add).then_inc(sems["sDVE"].h, 1)
        sems["sDVE"].n = dve1 + DC

    def emit_l0_tile(self, tile, guards=()):
        c, dram = self.c, self.dram
        self.emit_stats_xn(tile, extra_guards=guards)
        self.emit_router(tile, layer=0)
        if tile == 0:
            self.emit_l0_attn(tile)
        self.emit_gate_up_down(tile, dram["c_wsg"], dram["c_wsu"], dram["c_wsd"],
                               c["SHC"], with_cb=False, ready=self.cast_ready["ws"])
        for e in range(c["E"]):
            self.emit_cb_broadcast(e)
            self.emit_gate_up_down(tile, dram["c_weg"][e], dram["c_weu"][e],
                                   dram["c_wed"][e], c["EHC"], with_cb=True,
                                   ready=self.cast_ready[f"we{e}"])
        if tile != 0:
            self.emit_l0_attn(tile)
        self.xn_guard = (self.sems["sPE"], self.sems["sPE"].n)

    def emit_kvq_tile(self, tile):
        c, block = self.c, self.block
        sems, sb, dram = self.sems, self.sb, self.dram
        DC, TT, TC, NH, CPT, D = c["DC"], c["TT"], c["TC"], c["NH"], c["CPT"], c["D"]
        ps = self.ps
        qk_ready = self.cast_ready["qkvo"]
        nh_half = DC // 2
        nhb = 512 // 128

        units = []
        for h in range(NH):
            units.append(("K", h))
            units.append(("Q", h))
        for eb in range(D // 512):
            for tsub in range(CPT):
                units.append(("V", (eb, tsub)))

        pe0 = sems["sPE"].n
        wcnt = {}
        uslots = {}
        vslabs = {}
        ui = 0
        for kind, arg in units:
            if kind in ("K", "Q"):
                slot = self.slot()
                uslots[ui] = slot
                src = dram["c_kw" if kind == "K" else "c_qw"][arg].rearrange(
                    "p (dc j) -> p dc j", j=128)
                _, wcnt[ui] = self.load_slab(
                    slot, src, ready=qk_ready if (ui == 0 and tile == 0) else None)
                self.wguard[slot] = (sems["sPE"], pe0 + ui + 1)
            else:
                eb, tsub = arg
                if tsub == 0:
                    slot = self.slot()
                    srcA = dram["c_vw"][eb, 0].rearrange("p (di e) -> p di e", e=512)
                    _, wA_c = self.load_slab(slot, srcA)
                    slotB = self.slot()
                    srcB = dram["c_vw"][eb, 1].rearrange("p (di e) -> p di e", e=512)
                    _, wB_c = self.load_slab(slotB, srcB)
                    vslabs[eb] = (slot, slotB, wA_c, wB_c)
                    self.wguard[slot] = (sems["sPE"], pe0 + ui + CPT)
                    self.wguard[slotB] = (sems["sPE"], pe0 + ui + CPT)
                uslots[ui] = vslabs[eb]
            ui += 1

        @block.tensor
        def _(t):
            ui = 0
            for kind, arg in units:
                bk = self.take_bank(t)
                self._kv_banks = getattr(self, "_kv_banks", {})
                self._kv_banks[ui] = bk
                if kind in ("K", "Q"):
                    slot = uslots[ui]
                    buf = sb["wA" if slot == "A" else "wB"]
                    t.wait_ge(sems["dWA" if slot == "A" else "dWB"].h, wcnt[ui])
                    for dc in range(DC):
                        if ui == 0:
                            t.wait_ge(sems["sDVE"].h, self.xn_chunk[dc])
                        last = t.matmul(ps[bk][:, 0:TT], buf[:, dc * 128:(dc + 1) * 128],
                                        sb["xnb"][:, dc * TT:(dc + 1) * TT],
                                        start=(dc == 0), stop=(dc == DC - 1))
                else:
                    eb, tsub = arg
                    slot, slotB, wA_c, wB_c = uslots[ui]
                    t.wait_ge(sems["dWA" if slot == "A" else "dWB"].h, wA_c)
                    t.wait_ge(sems["dWA" if slotB == "A" else "dWB"].h, wB_c)
                    for dc in range(DC):
                        sl = slot if dc < nh_half else slotB
                        buf = sb["wA" if sl == "A" else "wB"]
                        di = dc if dc < nh_half else dc - nh_half
                        last = t.matmul(ps[bk][:, 0:512],
                                        sb["xnb"][:, dc * TT + tsub * 128: dc * TT + tsub * 128 + 128],
                                        buf[:, di * 512:(di + 1) * 512],
                                        start=(dc == 0), stop=(dc == DC - 1))
                last.then_inc(sems["sPE"].h, 1)
                self.bank_guard[bk] = (sems["sDVE"], sems["sDVE"].n + ui + 1)
                ui += 1
        n_units = len(units)
        pe_end = pe0 + n_units
        sems["sPE"].n = pe_end

        dve0 = sems["sDVE"].n
        ktn = {}
        ktsem_cnt = {}
        for ui in range(n_units):
            i = (self._kvt_rr if hasattr(self, "_kvt_rr") else 0)
            name = "ktmpA" if (ui % 2 == 0) else "ktmpB"
            ktn[ui] = name
        kt_base = {"ktmpA": sems["dKtA"].n, "ktmpB": sems["dKtB"].n}
        kt_cnt = {"ktmpA": 0, "ktmpB": 0}
        @block.vector
        def _(v):
            for ui, (kind, arg) in enumerate(units):
                ncols = TT if kind in ("K", "Q") else 512
                ktsem = sems["dKtA" if ui % 2 == 0 else "dKtB"]
                v.wait_ge(ktsem.h, kt_base[ktn[ui]] + 16 * kt_cnt[ktn[ui]])
                kt_cnt[ktn[ui]] += 1
                v.wait_ge(sems["sPE"].h, pe0 + ui + 1)
                v.drain()
                v.tensor_copy(sb[ktn[ui]][:, 0:ncols],
                              ps[self._kv_banks[ui]][:, 0:ncols]).then_inc(sems["sDVE"].h, 1)
        sems["sDVE"].n = dve0 + n_units

        @block.gpsimd
        def _(g):
            for ui, (kind, arg) in enumerate(units):
                ktsem = sems["dKtA" if ui % 2 == 0 else "dKtB"]
                g.wait_ge(sems["sDVE"].h, dve0 + ui + 1)
                if kind == "K":
                    dst = dram["kvloc"][arg * 128:(arg + 1) * 128, tile * TT:(tile + 1) * TT]
                    ktsem.inc(g.dma_start(out=dst, in_=sb[ktn[ui]][:, 0:TT]), 16)
                elif kind == "Q":
                    dst = dram["qloc"][arg * 128:(arg + 1) * 128, tile * TT:(tile + 1) * TT]
                    ktsem.inc(g.dma_start(out=dst, in_=sb[ktn[ui]][:, 0:TT]), 16)
                else:
                    eb, tsub = arg
                    h0 = eb * nhb
                    cloc = tile * CPT + tsub
                    dst = bass.AP(
                        dram["kvloc"].tensor,
                        (NH * 128 + h0 * 128) * TC + cloc * 128,
                        [[TC, 128], [128 * TC, nhb], [1, 128]])
                    ktsem.inc(g.dma_start(
                        out=dst,
                        in_=sb[ktn[ui]][:, 0:512].rearrange("p (hh hd) -> p hh hd", hd=128)), 16)
        self.xn_guard = (sems["sPE"], sems["sPE"].n)

    def emit_collective(self):
        sems, dram, block = self.sems, self.dram, self.block

        @block.gpsimd
        def _(g):
            g.wait_ge(sems["dKtA"].h, sems["dKtA"].n)
            g.wait_ge(sems["dKtB"].h, sems["dKtB"].n)
            sems["sCC"].inc(g.collective_compute(
                "AllGather", ALU.bypass, replica_groups=GROUPS,
                ins=[dram["kvloc"]], outs=[dram["kvg"]]))

    def head_sched(self, parity):
        tiles = EVEN_TILES if parity == 0 else ODD_TILES
        CPT, TT = self.c["CPT"], self.c["TT"]
        out = []
        for s in range(4):
            gt = tiles[s]
            q0 = gt * TT
            lst = []
            for G in range(CPT * (gt + 1)):
                d = 128 * G - q0
                lst.append((self.bufcol[G], (TT - d) if d >= 0 else None))
            out.append(lst)
        return out

    def emit_attention(self):
        c, block, nc = self.c, self.block, self.nc
        sems, sb, dram = self.sems, self.sb, self.dram
        DC, TT, TC, NH, CPT, D = c["DC"], c["TT"], c["TC"], c["NH"], c["CPT"], c["D"]
        KCH = c["KCH"]
        ps, pr = self.ps, self.pr
        nhh = NH // 2
        sc = [self.head_sched(0), self.head_sched(1)]
        inv_sqrt_hd = 1.0 / math.sqrt(c["HD"])

        SBANKS = [0, 1, 5]
        Ks = [sb["KsA"], sb["KsB"]]
        Vs = [sb["VsA"], sb["VsB"]]
        Qs = [sb["QsA"], sb["QsB"]]
        dKs = [sems["dKsA"], sems["dKsB"]]
        dVs = [sems["dVsA"], sems["dVsB"]]
        dQs = [sems["dQsA"], sems["dQsB"]]

        @block.vector
        def _(v):
            for g in self.scope_guards2:
                v.wait_ge(g[0].h, g[1])
            v.drain()
            sems["sDVE"].inc(v.memset(sb["Mbig"][:], 1.0))
        @block.gpsimd
        def _(g):
            g.wait_ge(sems["sDVE"].h, sems["sDVE"].n)
            sems["sGP"].inc(g.affine_select(sb["Mbig"][:], sb["Mbig"][:], [[1, 3 * TT]],
                                            ALU.is_ge, 0.0, base=-TT, channel_multiplier=-1))
        mbig_ready = (sems["sGP"], sems["sGP"].n)

        pe0, act0, dve0 = sems["sPE"].n, sems["sACT"].n, sems["sDVE"].n

        def nmax_s(s):
            return max(len(sc[0][s]), len(sc[1][s]))

        def bases(par, s, h):
            n_max = nmax_s(s)
            pe, act, dve = pe0, act0, dve0
            for s2 in range(s):
                nm2 = nmax_s(s2)
                pe += NH * (3 * nm2 + 2) + 2 * DC
                act += NH * (1 + nm2)
                dve += NH * (CPT + 6) + 2 * DC
            pe += h * (3 * n_max + 2) + (DC if h > nhh - 1 else 0)
            act += h * (1 + n_max)
            dve += h * (CPT + 6) + (DC if h > nhh - 1 else 0)
            return pe, act, dve, n_max

        def pe_head_end(s, h):
            pe, _, _, n_max = bases(0, s, h)
            return pe + 3 * n_max + 2

        last_use = {}
        load_cnt = {}
        oslab = {}
        first_load = True

        for s in range(4):
            for h in range(NH):
                sl = h % 2
                @block.sync
                def _(sy, s=s, h=h, sl=sl, first=first_load):
                    if first:
                        for g in self.scope_guards2:
                            sy.wait_ge(g[0].h, g[1])
                        sy.wait_ge(sems["sCC"].h, 1)
                        sy.wait_ge(sems["sGP"].h, mbig_ready[1])
                    lu = last_use.get(("K", sl))
                    if lu is not None:
                        sy.wait_ge(sems["sPE"].h, lu)
                    dKs[sl].inc(sy.dma_start(
                        out=Ks[sl][:, 0:TC], in_=dram["kvg"][h * 128:(h + 1) * 128, :]), 16)
                    dKs[sl].inc(sy.dma_start(
                        out=Ks[sl][:, TC:2 * TC],
                        in_=dram["kvg"][2 * D + h * 128: 2 * D + (h + 1) * 128, :]), 16)
                    dVs[sl].inc(sy.dma_start(
                        out=Vs[sl][:, 0:TC],
                        in_=dram["kvg"][D + h * 128: D + (h + 1) * 128, :]), 16)
                    dVs[sl].inc(sy.dma_start(
                        out=Vs[sl][:, TC:2 * TC],
                        in_=dram["kvg"][3 * D + h * 128: 3 * D + (h + 1) * 128, :]), 16)
                    dQs[sl].inc(sy.dma_start(
                        out=Qs[sl][:, 0:TT],
                        in_=dram["qloc"][h * 128:(h + 1) * 128, s * TT:(s + 1) * TT]), 16)
                first_load = False
                load_cnt[("K", s, h)] = dKs[sl].n
                load_cnt[("V", s, h)] = dVs[sl].n
                load_cnt[("Q", s, h)] = dQs[sl].n
                last_use[("K", sl)] = pe_head_end(s, h)
                if h == nhh - 1 or h == NH - 1:
                    half = 0 if h == nhh - 1 else 1
                    nq = DC // 4
                    pe_done_half = pe_head_end(s, h)
                    for q in range(nq):
                        slot = self.slot()
                        osrc = dram["c_ow"][half, q].rearrange("p (kc j) -> p kc j", j=512)
                        gspec = self.wguard[slot]
                        @block.sync
                        def _(sy, slot=slot, osrc=osrc, gspec=gspec):
                            if gspec is not None:
                                sy.wait_ge(gspec[0].h, gspec[1])
                            dW = sems["dWA" if slot == "A" else "dWB"]
                            a, b = int(osrc.shape[1]), int(osrc.shape[2])
                            buf = sb["wA" if slot == "A" else "wB"]
                            dest = buf[:, 0:a * b].rearrange("p (a b) -> p a b", b=b)
                            dW.inc(sy.dma_start(out=dest, in_=osrc), 16)
                        dWs = sems["dWA" if slot == "A" else "dWB"]
                        oslab[(s, half, q)] = (slot, sb["wA" if slot == "A" else "wB"], dWs.n)
                        self.wguard[slot] = (sems["sPE"], pe_done_half + 4 * (q + 1))

        def slot_end(s):
            pe, act, dve, _ = bases(0, s, NH - 1)
            nm = max(len(sc[0][s]), len(sc[1][s]))
            return (pe + (3 * nm + 2) + DC, act + (1 + nm), dve + (CPT + 6) + DC)

        for s in range(4):
            @block.tensor
            def _(t, s=s):
                def body(par):
                    for h in range(NH):
                        pe_b, act_b, dve_b, n_max = bases(par, s, h)
                        n = len(sc[par][s])
                        t.wait_ge(dQs[h % 2].h, load_cnt[("Q", s, h)])
                        t.wait_ge(sems["sACT"].h, act_b)
                        t.matmul(ps[3][:, 0:TT], sb["gwT_sb"][:], Qs[h % 2][:, 0:TT],
                                 start=True, stop=True, skip_group_check=True
                                 ).then_inc(sems["sPE"].h, 1)
                        t.wait_ge(dKs[h % 2].h, load_cnt[("K", s, h)])
                        t.wait_ge(dVs[h % 2].h, load_cnt[("V", s, h)])

                        def score(j):
                            bc, off = sc[par][s][j]
                            if j >= 3:
                                t.wait_ge(sems["sACT"].h, act_b + 2 + (j - 3))
                            else:
                                t.wait_ge(sems["sACT"].h, act_b)
                            if j == 2:
                                t.wait_ge(sems["sDVE"].h, dve_b)
                            t.matmul(ps[SBANKS[j % 3]][:, 0:TT],
                                     Ks[h % 2][:, bc * 128:(bc + 1) * 128],
                                     Qs[h % 2][:, 0:TT], start=True, stop=True,
                                     skip_group_check=True).then_inc(sems["sPE"].h, 1)

                        def avsum(j):
                            bc, off = sc[par][s][j]
                            t.wait_ge(sems["sACT"].h,
                                      act_b + 2 + j if j < n - 1 else act_b + 1 + n_max)
                            if off is not None:
                                jd = j - (n - CPT)
                                t.wait_ge(sems["sDVE"].h, dve_b + 3 + jd)
                            if j == 0:
                                t.wait_ge(sems["sDVE"].h, dve_b)
                            t.matmul(ps[2][:, 0:TT],
                                     Vs[h % 2][:, bc * 128:(bc + 1) * 128],
                                     sb["PT" + str(j % 3)][:, 0:TT],
                                     start=(j == 0), stop=(j == n - 1),
                                     skip_group_check=True).then_inc(sems["sPE"].h, 1)
                            t.matmul(pr[0][0:1, 0:TT], sb["onesc"][:, 0:1],
                                     sb["PT" + str(j % 3)][:, 0:TT],
                                     start=(j == 0), stop=(j == n - 1),
                                     skip_group_check=True).then_inc(
                                sems["sPE"].h, 1 + (3 * (n_max - n) if j == n - 1 else 0))

                        score(0)
                        if n > 1:
                            score(1)
                        for j in range(2, n):
                            score(j)
                            avsum(j - 2)
                        if n > 1:
                            avsum(n - 2)
                        avsum(n - 1)
                        t.wait_ge(sems["sDVE"].h, dve_b + 3 + CPT)
                        t.matmul(ps[4][:, 0:TT], sb["ones1"][0:1, :],
                                 sb["rowA"][0:1, 0:TT], start=True, stop=True,
                                 skip_group_check=True).then_inc(sems["sPE"].h, 1)
                        if h == nhh - 1 or h == NH - 1:
                            half = 0 if h == nhh - 1 else 1
                            pe_done = pe_b + 3 * n_max + 2
                            dve_done = dve_b + CPT + 6
                            for dc in range(DC):
                                slot, buf, wcnt = oslab[(s, half, dc // 4)]
                                dsub = dc % 4
                                t.wait_ge(sems["dWA" if slot == "A" else "dWB"].h, wcnt)
                                t.wait_ge(sems["sDVE"].h, dve_done)
                                bk = 5 if dc % 2 == 0 else 4
                                if dc >= 2:
                                    t.wait_ge(sems["sDVE"].h, dve_done + dc - 1)
                                for kc in range(nhh):
                                    last = t.matmul(
                                        ps[bk][:, 0:TT],
                                        buf[:, kc * 512 + dsub * 128: kc * 512 + dsub * 128 + 128],
                                        sb["out8"][:, kc * TT:(kc + 1) * TT],
                                        start=(kc == 0), stop=(kc == nhh - 1),
                                        skip_group_check=True)
                                last.then_inc(sems["sPE"].h, 1)
                with t.register(f"prt_t{s}") as prt:
                    t.cc_rank_ld(prt, replica_groups=GROUPS)
                    with t.If_eq(prt, 0):
                        body(0)
                    with t.Else():
                        body(1)

            @block.scalar
            def _(sx, s=s):
                def body(par):
                    for h in range(NH):
                        pe_b, act_b, dve_b, n_max = bases(par, s, h)
                        n = len(sc[par][s])
                        sx.wait_ge(sems["sPE"].h, pe_b + 1)
                        sx.wait_ge(sems["sDVE"].h, dve_b)
                        sx.drain()
                        sx.activation(sb["gatebuf"][:, 0:TT], ps[3][:, 0:TT], AF.Exp,
                                      scale=-1.0, bias=sb["gbneg"][:, 0:1]
                                      ).then_inc(sems["sACT"].h, 1)
                        for j in range(n):
                            pos = 2 if j == 0 else (3 if j == 1 else 3 * j - 2)
                            sx.wait_ge(sems["sPE"].h, pe_b + pos)
                            sx.drain()
                            a = sx.activation(sb["PT" + str(j % 3)][:, 0:TT],
                                              ps[SBANKS[j % 3]][:, 0:TT], AF.Exp,
                                              scale=inv_sqrt_hd)
                            a.then_inc(sems["sACT"].h,
                                       1 + (n_max - n) if j == n - 1 else 1)
                with sx.register(f"prt_s{s}") as prt:
                    sx.cc_rank_ld(prt, replica_groups=GROUPS)
                    with sx.If_eq(prt, 0):
                        body(0)
                    with sx.Else():
                        body(1)

            @block.vector
            def _(v, s=s):
                def body(par):
                    for h in range(NH):
                        pe_b, act_b, dve_b, n_max = bases(par, s, h)
                        n = len(sc[par][s])
                        v.wait_ge(sems["sACT"].h, act_b + 1)
                        v.drain()
                        v.tensor_scalar(sb["gatebuf"][:, 0:TT], sb["gatebuf"][:, 0:TT],
                                        1.0, None, ALU.add).then_inc(sems["sDVE"].h, 1)
                        v.drain()
                        v.reciprocal(sb["gatebuf"][:, 0:TT], sb["gatebuf"][:, 0:TT]
                                     ).then_inc(sems["sDVE"].h, 1)
                        for jd in range(CPT):
                            j = n - CPT + jd
                            bc, off = sc[par][s][j]
                            v.wait_ge(sems["sACT"].h,
                                      act_b + 2 + j if j < n - 1 else act_b + 1 + n_max)
                            v.drain()
                            v.tensor_tensor(sb["PT" + str(j % 3)][:, 0:TT],
                                            sb["PT" + str(j % 3)][:, 0:TT],
                                            sb["Mbig"][:, off:off + TT], ALU.mult
                                            ).then_inc(sems["sDVE"].h, 1)
                        v.wait_ge(sems["sPE"].h, pe_b + 3 * n_max + 1)
                        v.drain()
                        v.reciprocal(sb["rowA"][0:1, 0:TT], pr[0][0:1, 0:TT]
                                     ).then_inc(sems["sDVE"].h, 1)
                        v.wait_ge(sems["sPE"].h, pe_b + 3 * n_max + 2)
                        v.drain()
                        v.tensor_copy(sb["rb_sb"][:, 0:TT], ps[4][:, 0:TT]
                                      ).then_inc(sems["sDVE"].h, 1)
                        if h >= nhh:
                            pass
                        v.drain()
                        v.tensor_tensor(sb["out8"][:, (h % nhh) * TT:(h % nhh + 1) * TT],
                                        ps[2][:, 0:TT], sb["rb_sb"][:, 0:TT], ALU.mult
                                        ).then_inc(sems["sDVE"].h, 1)
                        v.drain()
                        v.tensor_tensor(sb["out8"][:, (h % nhh) * TT:(h % nhh + 1) * TT],
                                        sb["out8"][:, (h % nhh) * TT:(h % nhh + 1) * TT],
                                        sb["gatebuf"][:, 0:TT], ALU.mult
                                        ).then_inc(sems["sDVE"].h, 1)
                        if h == nhh - 1 or h == NH - 1:
                            pe_done = pe_b + 3 * n_max + 2
                            t0 = s * TT
                            for dc in range(DC):
                                v.wait_ge(sems["sPE"].h, pe_done + dc + 1)
                                v.drain()
                                v.scalar_tensor_tensor(
                                    self.xres(dc, t0, TT),
                                    ps[5 if dc % 2 == 0 else 4][:, 0:TT], 0.0,
                                    self.xres(dc, t0, TT), ALU.add, ALU.add
                                    ).then_inc(sems["sDVE"].h, 1)
                with v.register(f"prt_v{s}") as prt:
                    v.cc_rank_ld(prt, replica_groups=GROUPS)
                    with v.If_eq(prt, 0):
                        body(0)
                    with v.Else():
                        body(1)

            pe_e, act_e, dve_e = slot_end(s)
            sems["sPE"].n = pe_e
            sems["sACT"].n = act_e
            sems["sDVE"].n = dve_e
        for i in range(6):
            self.bank_guard[i] = (sems["sDVE"], sems["sDVE"].n)
        self.pr_guard = (sems["sDVE"], sems["sDVE"].n)

    def emit_l1_ffn_tile(self, tile):
        c, dram = self.c, self.dram
        self.emit_stats_xn(tile)
        self.emit_router(tile, layer=1)
        for e in range(c["E"]):
            self.emit_cb_broadcast(e)
            self.emit_gate_up_down(tile, dram["c_gp"][e], dram["c_up"][e],
                                   dram["c_dp"][e], c["EHC"], with_cb=True,
                                   ready=self.cast_ready[f"l1e{e}"])
        self.xn_guard = (self.sems["sPE"], self.sems["sPE"].n)

    def emit_epilogue(self):
        sems, sb, dram, block = self.sems, self.sb, self.dram, self.block
        TC = self.c["TC"]

        @block.scalar
        def _(s):
            s.wait_ge(sems["sDVE"].h, sems["sDVE"].n)
            sems["dOut"].inc(s.dma_start(out=dram[self.out_name], in_=sb["x_res"][:]), 16)
            s.wait_ge(sems["dOut"].h, sems["dOut"].n)


def build_program(cfg, layers=(0, 1), in_name="x_in", out_name="out"):
    nc = bass.Bass("TRN2", target_bir_lowering=False)
    b = B(nc, cfg, layers, in_name, out_name)
    b.declare()
    c = b.c
    TT, DC, SHC, EHC = c["TT"], c["DC"], c["SHC"], c["EHC"]
    with b.es:
        b.alloc_sems()
        b.alloc_psum()
        b.alloc_persistent()
        with nc.Block() as block:
            b.block = block
            b.emit_prologue()
            with contextlib.ExitStack() as s1:
                def S1(name, shape, dt):
                    b.sb[name] = s1.enter_context(nc.sbuf_tensor(name, shape, dt))
                hchunks = max(DC, EHC, SHC if 0 in layers else 0)
                S1("xnb", [128, DC * TT], BF16)
                S1("H", [128, hchunks * TT], BF16)
                S1("mr_b", [128, 2 * TT], F32)
                S1("rowB", [1, TT], F32)
                S1("rowC", [1, TT], F32)
                S1("ktmpA2", [128, 512], BF16)
                S1("ktmpB2", [128, 512], BF16)
                S1("ue", [128, TT], BF16)
                b.sb["ktmpA"] = b.sb["ktmpA2"]
                b.sb["ktmpB"] = b.sb["ktmpB2"]
                if 0 in layers:
                    for tile in range(4):
                        b.emit_l0_tile(tile, guards=tuple(b.scope_guards) if tile == 0 else ())
                if 1 in layers:
                    g0 = tuple(b.scope_guards) if 0 not in layers else ()
                    for tile in range(4):
                        b.emit_stats_xn(tile, extra_guards=g0 if tile == 0 else ())
                        b.emit_kvq_tile(tile)
                    b.emit_collective()
                    for tile in range(4):
                        b.emit_l1_ffn_tile(tile)
                b.scope_guards2 = [(b.sems["sPE"], b.sems["sPE"].n),
                                   (b.sems["sDVE"], b.sems["sDVE"].n),
                                   (b.sems["sACT"], b.sems["sACT"].n)]
            if 1 in layers:
                with contextlib.ExitStack() as s2:
                    def S2(name, shape, dt):
                        b.sb[name] = s2.enter_context(nc.sbuf_tensor(name, shape, dt))
                    TC, NH = c["TC"], c["NH"]
                    S2("KsA", [128, 2 * TC], BF16)
                    S2("KsB", [128, 2 * TC], BF16)
                    S2("VsA", [128, 2 * TC], BF16)
                    S2("VsB", [128, 2 * TC], BF16)
                    S2("QsA", [128, TT], BF16)
                    S2("QsB", [128, TT], BF16)
                    S2("PT0", [128, TT], BF16)
                    S2("PT1", [128, TT], BF16)
                    S2("PT2", [128, TT], BF16)
                    S2("out8", [128, (NH // 2) * TT], BF16)
                    S2("rb_sb", [128, TT], F32)
                    S2("Mbig", [128, 3 * TT], BF16)
                    b.sb["gatebuf"] = b.sb["xtmp"]
                    b.emit_attention()
            b.emit_epilogue()
    return nc


def _slabify(a, inC, outC):
    return np.ascontiguousarray(
        a.reshape(inC, 128, outC, 128).transpose(2, 1, 0, 3).reshape(outC, 128, inC * 128))


def _shard_inputs(cfg, inputs):
    c = derive(cfg)
    TT, TC, D = c["TT"], c["TC"], c["D"]
    DC, EHC, SHC, NH, E = c["DC"], c["EHC"], c["SHC"], c["NH"], c["E"]
    nhh = NH // 2
    rows = snake_rows(TT)
    x = np.asarray(inputs["L0_attn_x"])
    maps = []
    common0 = {}
    common1 = {}
    common0["w_dec"] = np.ascontiguousarray(inputs["L0_attn_weight"])
    common0["b_dec"] = np.ascontiguousarray(inputs["L0_attn_bias"]).reshape(1, -1)
    common0["dvv"] = np.ascontiguousarray(inputs["L0_attn_decay_value"])
    common0["wr0"] = np.ascontiguousarray(np.asarray(inputs["L0_ffn_W_router"]).T)
    common0["wsg"] = _slabify(np.asarray(inputs["L0_ffn_Ws_gate"]), DC, SHC)
    common0["wsu"] = _slabify(np.asarray(inputs["L0_ffn_Ws_up"]), DC, SHC)
    common0["wsd"] = _slabify(np.asarray(inputs["L0_ffn_Ws_down"]), SHC, DC)
    common0["weg"] = np.stack([_slabify(np.asarray(inputs["L0_ffn_We_gate"][e]), DC, EHC)
                               for e in range(E)])
    common0["weu"] = np.stack([_slabify(np.asarray(inputs["L0_ffn_We_up"][e]), DC, EHC)
                               for e in range(E)])
    common0["wed"] = np.stack([_slabify(np.asarray(inputs["L0_ffn_We_down"][e]), EHC, DC)
                               for e in range(E)])
    qwT = np.asarray(inputs["L1_attn_q_w"]).T
    kwT = np.asarray(inputs["L1_attn_k_w"]).T
    vwT = np.asarray(inputs["L1_attn_v_w"]).T
    owT = np.asarray(inputs["L1_attn_o_w"]).T
    common1["qw"] = _slabify(qwT, DC, NH)
    common1["kw"] = _slabify(kwT, DC, NH)
    common1["vw"] = np.ascontiguousarray(
        vwT.reshape(2, DC // 2, 128, D // 512, 512).transpose(3, 0, 2, 1, 4)
        .reshape(D // 512, 2, 128, (DC // 2) * 512))
    common1["ow"] = np.ascontiguousarray(
        owT.reshape(2, nhh, 128, DC // 4, 512).transpose(0, 3, 2, 1, 4)
        .reshape(2, DC // 4, 128, nhh * 512))
    common1["gw"] = np.ascontiguousarray(np.asarray(inputs["L1_attn_gate_w"]).T)
    common1["gb"] = np.ascontiguousarray(inputs["L1_attn_gate_b"]).reshape(1, -1)
    common1["wr1"] = np.ascontiguousarray(np.asarray(inputs["L1_ffn_gate_w"]))
    gpT = np.asarray(inputs["L1_ffn_gate_proj"]).transpose(0, 2, 1)
    upT = np.asarray(inputs["L1_ffn_up_proj"]).transpose(0, 2, 1)
    dpT = np.asarray(inputs["L1_ffn_down_proj"]).transpose(0, 2, 1)
    common1["gp"] = np.stack([_slabify(gpT[e], DC, EHC) for e in range(E)])
    common1["up"] = np.stack([_slabify(upT[e], DC, EHC) for e in range(E)])
    common1["dp"] = np.stack([_slabify(dpT[e], EHC, DC) for e in range(E)])
    for core in range(8):
        bat, par = core // 2, core % 2
        xc = x[bat][rows[par], :].T
        xr = np.ascontiguousarray(
            xc.reshape(DC, 128, TC).transpose(1, 0, 2).reshape(128, DC * TC))
        maps.append((xr, dict(common0), dict(common1)))
    return maps, rows


def _gather_output(cfg, results, rows, out_name="out"):
    c = derive(cfg)
    D, TC, TT = c["D"], c["TC"], c["TT"]
    B = 4
    S = c["S"]
    DC = c["DC"]
    out = np.empty((B, S, D), dtype=np.float32)
    for core in range(8):
        bat, par = core // 2, core % 2
        oc = np.asarray(results[core][out_name])
        od = oc.reshape(128, DC, TC).transpose(1, 0, 2).reshape(D, TC)
        out[bat][rows[par], :] = od.T
    return out


def kernel(**inputs):
    cfg = TINY_CFG if os.environ.get("MK_TINY") else FULL_CFG
    mode = os.environ.get("MK_MODE", "fused")
    maps, rows = _shard_inputs(cfg, inputs)

    if mode == "fused":
        nc = build_program(cfg, layers=(0, 1))
        in_maps = []
        for xc, c0, c1 in maps:
            m = {"x_in": xc}
            m.update(c0)
            m.update(c1)
            in_maps.append(m)
        res = run_bass_kernel_spmd(nc, in_maps, core_ids=list(range(8)))
        return _gather_output(cfg, res.results, rows)

    nc0 = build_program(cfg, layers=(0,), in_name="x_in", out_name="out")
    in_maps0 = []
    for xc, c0, c1 in maps:
        m = {"x_in": xc}
        m.update(c0)
        in_maps0.append(m)
    res0 = run_bass_kernel_spmd(nc0, in_maps0, core_ids=list(range(8)))
    nc1 = build_program(cfg, layers=(1,), in_name="x_in", out_name="out")
    in_maps1 = []
    for i, (xc, c0, c1) in enumerate(maps):
        m = {"x_in": np.ascontiguousarray(np.asarray(res0.results[i]["out"]))}
        m.update(c1)
        in_maps1.append(m)
    res1 = run_bass_kernel_spmd(nc1, in_maps1, core_ids=list(range(8)))
    return _gather_output(cfg, res1.results, rows)


# version 09 (final)
# speedup vs baseline: 1.2301x; 1.0810x over previous
import os
import math
import contextlib
import numpy as np
import ml_dtypes
import concourse.bass as bass
import concourse.mybir as mybir
from concourse.bass_utils import run_bass_kernel_spmd

F32 = mybir.dt.float32
BF16 = mybir.dt.bfloat16
AF = mybir.ActivationFunctionType
ALU = mybir.AluOpType
AX = mybir.AxisListType

GROUPS = [[0, 1], [2, 3], [4, 5], [6, 7]]
EVEN_TILES = [0, 3, 4, 7]
ODD_TILES = [1, 2, 5, 6]
NEG_BIG = -1.0e30
EPS = 1e-5

FULL_CFG = dict(D=2048, DE=1408, DS=2816, E=8, HD=128, NH=16, TT=512)
TINY_CFG = dict(D=512, DE=256, DS=512, E=8, HD=128, NH=4, TT=128)
MIDI_CFG = dict(D=1024, DE=512, DS=1024, E=8, HD=128, NH=8, TT=256)


def derive(cfg):
    c = dict(cfg)
    c["DC"] = c["D"] // 128
    c["EHC"] = c["DE"] // 128
    c["SHC"] = c["DS"] // 128
    c["TC"] = 4 * c["TT"]
    c["S"] = 8 * c["TT"]
    c["CPT"] = c["TT"] // 128
    c["KCH"] = c["S"] // 128
    assert c["NH"] * c["HD"] == c["D"]
    return c


def snake_rows(TT):
    rows = {}
    for par, tiles in ((0, EVEN_TILES), (1, ODD_TILES)):
        r = []
        for t in tiles:
            r.extend(range(t * TT, (t + 1) * TT))
        rows[par] = np.array(r, dtype=np.int64)
    return rows


def bufcol_map(CPT):
    m = {}
    for par, tiles in ((0, EVEN_TILES), (1, ODD_TILES)):
        for i, t in enumerate(tiles):
            for j in range(CPT):
                m[t * CPT + j] = par * 4 * CPT + i * CPT + j
    return m


class Sem:
    def __init__(self, handle, name):
        self.h = handle
        self.name = name
        self.n = 0

    def inc(self, instr, k=1):
        instr.then_inc(self.h, k)
        self.n += k
        return self.n


class B:

    def __init__(self, nc, cfg, layers=(0, 1), in_name="x_in", out_name="out"):
        self.nc = nc
        self.c = derive(cfg)
        self.layers = layers
        self.in_name = in_name
        self.out_name = out_name
        self.es = contextlib.ExitStack()
        self.dram = {}
        self.sb = {}
        self.sems = {}
        self.cast_ready = {}
        self.bufcol = bufcol_map(self.c["CPT"])
        self.wguard = {"A": None, "B": None}
        self.next_slot = "A"
        self.xn_guard = None
        self.scope_guards = []
        self.h_ready = 0
        self.kvq_extra_guard = None

    def declare(self):
        nc, c = self.nc, self.c
        D, DE, DS, E = c["D"], c["DE"], c["DS"], c["E"]
        TC, HD, NH = c["TC"], c["HD"], c["NH"]

        def din(name, shape):
            self.dram[name] = nc.dram_tensor(name, shape, F32, kind="ExternalInput").ap()

        def dint(name, shape, dt=BF16):
            self.dram[name] = nc.dram_tensor(name, shape, dt).ap()

        DC, EHC, SHC, NH2 = c["DC"], c["EHC"], c["SHC"], c["NH"]
        nhh = NH2 // 2
        din(self.in_name, [128, c["DC"] * TC])
        shp = {
            "wsg": [SHC, 128, DC * 128], "wsu": [SHC, 128, DC * 128],
            "wsd": [DC, 128, SHC * 128],
            "weg": [E, EHC, 128, DC * 128], "weu": [E, EHC, 128, DC * 128],
            "wed": [E, DC, 128, EHC * 128],
            "qw": [NH2, 128, DC * 128], "kw": [NH2, 128, DC * 128],
            "vw": [D // 512, 2, 128, (DC // 2) * 512],
            "ow": [2, DC // 4, 128, nhh * 512],
            "gp": [E, EHC, 128, DC * 128], "up": [E, EHC, 128, DC * 128],
            "dp": [E, DC, 128, EHC * 128],
        }
        if 0 in self.layers:
            din("w_dec", [2, D]); din("b_dec", [1, D]); din("dvv", [2, 1])
            din("wr0", [E, D])
            for nm in ("wsg", "wsu", "wsd", "weg", "weu", "wed"):
                din(nm, shp[nm]); dint("c_" + nm, shp[nm])
            dint("c_wdec", [c["DC"] // 2, 128, c["DC"] * 256])
        if 1 in self.layers:
            din("gw", [HD, HD]); din("gb", [1, HD]); din("wr1", [E, D])
            for nm in ("qw", "kw", "vw", "ow", "gp", "up", "dp"):
                din(nm, shp[nm]); dint("c_" + nm, shp[nm])
            dint("kvloc", [2 * D, TC])
            dint("kvg", [4 * D, TC])
            dint("qloc", [D, TC])
        self.dram[self.out_name] = nc.dram_tensor(self.out_name, [128, c["DC"] * TC],
                                                  F32, kind="ExternalOutput").ap()

    def alloc_sems(self):
        names = ["dIn", "dWA", "dWB", "dWd", "dSc", "dGp", "dKtA", "dKtB",
                 "sPE", "sDVE", "sACT", "sGP", "sCC", "dOut", "sAH",
                 "dKsA", "dKsB", "dVsA", "dVsB", "dQsA", "dQsB"]
        if 0 in self.layers:
            names += ["cgws"] + [f"cgwe{e}" for e in range(self.c["E"])]
        if 1 in self.layers:
            names += ["cgqk"] + [f"cgl1e{e}" for e in range(self.c["E"])]
        for n in names:
            self.sems[n] = Sem(self.es.enter_context(self.nc.semaphore(n)), n)

    def alloc_psum(self):
        self.ps = [self.es.enter_context(self.nc.psum_tensor(f"ps{i}", [128, 512], F32))
                   for i in range(6)]
        self.pr = [self.es.enter_context(self.nc.psum_tensor(f"pr{i}", [1, 512], F32))
                   for i in range(2)]
        self.bank_guard = [None] * 6
        self.pr_guard = None
        self._rr = 0

    def alloc_persistent(self):
        nc, c = self.nc, self.c
        TT, DC = c["TT"], c["DC"]

        def S(name, shape, dt):
            self.sb[name] = self.es.enter_context(nc.sbuf_tensor(name, shape, dt))

        S("x_res", [128, DC * c["TC"]], F32)
        S("wA", [128, max(4096, 2 * DC * 128)], BF16)
        S("wB", [128, max(4096, 2 * DC * 128)], BF16)
        S("ident", [128, 128], F32)
        S("onesc", [128, 1], BF16)
        S("ones1", [1, 128], F32)
        S("selmat", [8, 8 * 128], BF16)
        S("wr0_sb", [128, DC * 8], BF16)
        S("wr1_sb", [128, DC * 8], BF16)
        S("gwT_sb", [128, 128], BF16)
        S("gbneg", [128, 1], F32)
        S("bias_sb", [128, DC], F32)
        S("rlog", [128, 4 * 8], F32)
        S("escr", [128, 8], F32)
        S("escr2", [128, 8], F32)
        S("e8", [128, 8], F32)
        S("r_a", [128, 1], F32)
        S("r_b", [128, 1], F32)
        S("r_c", [128, 1], F32)
        S("r_d", [128, 1], F32)
        S("cb_tm", [128, 4 * 8], F32)
        S("cb_fm", [8, TT], BF16)
        S("cb_b", [128, TT], BF16)
        S("xtmp", [128, TT], F32)
        S("rowA", [1, TT], F32)
        S("lnd0", [128, 1], F32)
        S("lnd1", [128, 1], F32)
        S("w0col", [128, DC], F32)
        S("w1col", [128, DC], F32)
        S("stg16", [16, 128], F32)
        S("identB", [128, 128], BF16)
        S("dvt", [2, 1], F32)

    def take_bank(self, t):
        i = self._rr
        self._rr = (self._rr + 1) % 6
        g = self.bank_guard[i]
        if g is not None:
            t.wait_ge(g[0].h, g[1])
        return i

    def slot(self):
        s = self.next_slot
        self.next_slot = "B" if s == "A" else "A"
        return s

    def load_slab(self, slot, src_ap, off=0, first=True, ready=None):
        sems = self.sems
        dW = sems["dWA" if slot == "A" else "dWB"]
        buf = self.sb["wA" if slot == "A" else "wB"]
        guard = self.wguard[slot] if first else None
        a, bcols = int(src_ap.shape[1]), int(src_ap.shape[2])
        dest = buf[:, off:off + a * bcols].rearrange("p (a b) -> p a b", b=bcols)

        @self.block.sync
        def _(sy):
            if guard is not None:
                sy.wait_ge(guard[0].h, guard[1])
            if ready is not None:
                sy.wait_ge(ready[0].h, ready[1])
            dW.inc(sy.dma_start(out=dest, in_=src_ap), 16)
        return buf, dW.n

    def xres(self, dc, t0, n):
        c = self.c
        return self.sb["x_res"][:, dc * c["TC"] + t0: dc * c["TC"] + t0 + n]

    def emit_prologue(self):
        nc, c, block = self.nc, self.c, self.block
        sems, sb, dram = self.sems, self.sb, self.dram
        D, DC, TC, E = c["D"], c["DC"], c["TC"], c["E"]

        @block.scalar
        def _(s):
            sems["dIn"].inc(s.dma_start(out=sb["x_res"][:], in_=dram[self.in_name]), 16)
        @block.sync
        def _(s):
            if 0 in self.layers:
                sems["dSc"].inc(s.dma_start(out=sb["dvt"][:], in_=dram["dvv"]), 16)
                s.wait_ge(sems["dSc"].h, sems["dSc"].n)

        @block.gpsimd
        def _(g):
            g.iota(sb["ident"][:], [[1, 128]], base=0, channel_multiplier=-1,
                   allow_small_or_imprecise_dtypes=True)
            sems["sGP"].inc(g.iota(sb["wA"][0:8, 0:8 * 128], [[1, 8], [0, 128]],
                                   base=0, channel_multiplier=-1,
                                   allow_small_or_imprecise_dtypes=True))
        @block.vector
        def _(v):
            v.wait_ge(sems["sGP"].h, sems["sGP"].n)
            v.tensor_scalar(sb["ident"][:], sb["ident"][:], 0.0, None, ALU.is_equal)
            v.tensor_scalar(sb["selmat"][:], sb["wA"][0:8, 0:8 * 128], 0.0, None, ALU.is_equal)
            v.memset(sb["onesc"][:], 1.0)
            v.memset(sb["ones1"][:], 1.0)
            v.drain()
            sems["sDVE"].inc(v.tensor_copy(sb["identB"][:], sb["ident"][:]))
        self.wguard["A"] = (sems["sDVE"], sems["sDVE"].n)
        self.wguard["B"] = (sems["sDVE"], sems["sDVE"].n)

        ps = self.ps
        stage_jobs = []
        if 0 in self.layers:
            stage_jobs.append((dram["b_dec"], "bias_sb"))
            stage_jobs.append((dram["w_dec"][0:1, :], "w0col"))
            stage_jobs.append((dram["w_dec"][1:2, :], "w1col"))
        for srow, dname in stage_jobs:
            @block.sync
            def _(s, srow=srow):
                s.wait_ge(sems["sPE"].h, sems["sPE"].n)
                sems["dSc"].inc(s.dma_start(
                    out=sb["stg16"][0:DC, :],
                    in_=srow.rearrange("o (dc p) -> (o dc) p", p=128)), 16)
                s.wait_ge(sems["dSc"].h, sems["dSc"].n)
            @block.tensor
            def _(t):
                t.wait_ge(sems["dSc"].h, sems["dSc"].n)
                t.wait_ge(sems["sDVE"].h, sems["sDVE"].n)
                sems["sPE"].inc(t.transpose(ps[0][:, 0:DC], sb["stg16"][0:DC, :],
                                            sb["ident"][0:DC, 0:DC]))
            @block.vector
            def _(v, dname=dname):
                v.wait_ge(sems["sPE"].h, sems["sPE"].n)
                v.drain()
                sems["sDVE"].inc(v.tensor_copy(sb[dname][:, 0:DC], ps[0][:, 0:DC]))

        for li, dname in ((0, "wr0"), (1, "wr1")):
            if li not in self.layers:
                continue
            for dc in range(DC):
                @block.sync
                def _(s, li=li, dc=dc):
                    s.wait_ge(sems["sPE"].h, sems["sPE"].n)
                    sems["dSc"].inc(s.dma_start(
                        out=sb["stg16"][0:8, :],
                        in_=dram["wr0" if li == 0 else "wr1"][:, dc * 128:(dc + 1) * 128]), 16)
                    s.wait_ge(sems["dSc"].h, sems["dSc"].n)
                @block.tensor
                def _(t, dc=dc):
                    t.wait_ge(sems["dSc"].h, sems["dSc"].n)
                    if dc == 0:
                        t.wait_ge(sems["sDVE"].h, sems["sDVE"].n)
                    sems["sPE"].inc(t.transpose(ps[1][:, dc * 8:(dc + 1) * 8],
                                                sb["stg16"][0:8, :],
                                                sb["ident"][0:8, 0:8]))
            @block.vector
            def _(v, dname=dname):
                v.wait_ge(sems["sPE"].h, sems["sPE"].n)
                v.drain()
                sems["sDVE"].inc(v.tensor_copy(sb[dname + "_sb"][:], ps[1][:, 0:DC * 8]))
        self.bank_guard[0] = (sems["sDVE"], sems["sDVE"].n)
        self.bank_guard[1] = (sems["sDVE"], sems["sDVE"].n)

        @block.sync
        def _(s):
            if 1 in self.layers:
                s.wait_ge(sems["sPE"].h, sems["sPE"].n)
                sems["dSc"].inc(s.dma_start(out=sb["xtmp"][:, 0:128], in_=dram["gw"]), 16)
                s.wait_ge(sems["dSc"].h, sems["dSc"].n)
                sems["dSc"].inc(s.dma_start(out=sb["stg16"][0:1, 0:128], in_=dram["gb"]), 16)
                s.wait_ge(sems["dSc"].h, sems["dSc"].n)
        @block.vector
        def _(v):
            if 1 in self.layers:
                v.wait_ge(sems["dSc"].h, sems["dSc"].n)
                v.drain()
                sems["sDVE"].inc(v.tensor_copy(sb["gwT_sb"][:], sb["xtmp"][:, 0:128]))
        @block.tensor
        def _(t):
            if 1 in self.layers:
                t.wait_ge(sems["dSc"].h, sems["dSc"].n)
                sems["sPE"].inc(t.transpose(ps[2][:, 0:1], sb["stg16"][0:1, 0:128],
                                            sb["ident"][0:1, 0:1]))
        @block.vector
        def _(v):
            if 1 in self.layers:
                v.wait_ge(sems["sPE"].h, sems["sPE"].n)
                v.drain()
                sems["sDVE"].inc(v.tensor_scalar(sb["gbneg"][:], ps[2][:, 0:1],
                                                 -1.0, None, ALU.mult))
        self.bank_guard[2] = (sems["sDVE"], sems["sDVE"].n)

        if 0 in self.layers:
            self.emit_wdecay()
        self.emit_casts()

    def emit_casts(self):
        c, block = self.c, self.block
        sems, dram = self.sems, self.dram
        E = c["E"]
        CH = 1 << 19

        def pieces(g, cg, dst, srcname):
            if "!" in srcname:
                nm, e = srcname.split("!")
                srcap = dram[nm][int(e)]
            else:
                srcap = dram[srcname]
            n = 1
            for d in srcap.shape:
                n *= d
            flat_src = srcap.rearrange(
                " ".join(f"d{i}" for i in range(len(srcap.shape))) + " -> ("
                + " ".join(f"d{i}" for i in range(len(srcap.shape))) + ")")
            flat_dst = dst.rearrange(
                " ".join(f"d{i}" for i in range(len(dst.shape))) + " -> ("
                + " ".join(f"d{i}" for i in range(len(dst.shape))) + ")")
            o = 0
            while o < n:
                m = min(CH, n - o)
                cg.inc(g.dma_start(out=flat_dst[o:o + m], in_=flat_src[o:o + m]), 16)
                g.wait_ge(cg.h, cg.n)
                o += m

        @block.gpsimd
        def _(g):
            g.wait_ge(sems["dIn"].h, 16)
            g.wait_ge(sems["dSc"].h, sems["dSc"].n)
            g.wait_ge(sems["dGp"].h, sems["dGp"].n)
            if 0 in self.layers:
                cg = sems["cgws"]
                for nm in ("wsg", "wsu", "wsd"):
                    pieces(g, cg, dram["c_" + nm], nm)
                self.cast_ready["ws"] = (cg, cg.n)
                for e in range(E):
                    cge = sems[f"cgwe{e}"]
                    for nm in ("weg", "weu", "wed"):
                        pieces(g, cge, dram["c_" + nm][e], nm + f"!{e}")
                    self.cast_ready[f"we{e}"] = (cge, cge.n)
            if 1 in self.layers:
                cg = sems["cgqk"]
                for nm in ("qw", "kw", "vw", "ow"):
                    pieces(g, cg, dram["c_" + nm], nm)
                self.cast_ready["qkvo"] = (cg, cg.n)
                for e in range(E):
                    cge = sems[f"cgl1e{e}"]
                    for nm in ("gp", "up", "dp"):
                        pieces(g, cge, dram["c_" + nm][e], nm + f"!{e}")
                    self.cast_ready[f"l1e{e}"] = (cge, cge.n)

    def emit_wdecay(self):
        nc, c, block = self.nc, self.c, self.block
        sems, sb, dram = self.sems, self.sb, self.dram
        D, DC = c["D"], c["DC"]
        ps, pr = self.ps, self.pr

        @block.vector
        def _(v):
            v.wait_ge(sems["dSc"].h, sems["dSc"].n)
            sems["sDVE"].inc(v.tensor_scalar(sb["dvt"][:], sb["dvt"][:], 0.9, 1.0,
                                             ALU.max, ALU.min))
        @block.scalar
        def _(s):
            s.wait_ge(sems["sDVE"].h, sems["sDVE"].n)
            sems["sACT"].inc(s.activation(sb["dvt"][:], sb["dvt"][:], AF.Ln))
        @block.sync
        def _(s):
            s.wait_ge(sems["sACT"].h, sems["sACT"].n)
            sems["dSc"].inc(s.dma_start(out=sb["r_a"][0:1, 0:1], in_=sb["dvt"][0:1, 0:1]), 16)
            s.wait_ge(sems["dSc"].h, sems["dSc"].n)
            sems["dSc"].inc(s.dma_start(out=sb["r_b"][0:1, 0:1], in_=sb["dvt"][1:2, 0:1]), 16)
            s.wait_ge(sems["dSc"].h, sems["dSc"].n)
        @block.tensor
        def _(t):
            t.wait_ge(sems["dSc"].h, sems["dSc"].n)
            t.wait_ge(sems["sDVE"].h, sems["sDVE"].n)
            t.matmul(ps[0][:, 0:1], sb["ones1"][0:1, :], sb["r_a"][0:1, 0:1],
                     start=True, stop=True)
            sems["sPE"].inc(t.matmul(ps[1][:, 0:1], sb["ones1"][0:1, :],
                                     sb["r_b"][0:1, 0:1], start=True, stop=True))
        @block.vector
        def _(v):
            v.wait_ge(sems["sPE"].h, sems["sPE"].n)
            v.tensor_copy(sb["lnd0"][:], ps[0][:, 0:1])
            sems["sDVE"].inc(v.tensor_copy(sb["lnd1"][:], ps[1][:, 0:1]))

        with contextlib.ExitStack() as scope:
            vrow = scope.enter_context(nc.sbuf_tensor("wd_v", [1, D], F32))
            r0 = scope.enter_context(nc.sbuf_tensor("wd_r0", [1, D], F32))
            r1 = scope.enter_context(nc.sbuf_tensor("wd_r1", [1, D], F32))
            w1r = scope.enter_context(nc.sbuf_tensor("wd_w1r", [1, D], F32))
            mbig0 = scope.enter_context(nc.sbuf_tensor("wd_m", [128, 2 * D], BF16))
            wbfF = scope.enter_context(nc.sbuf_tensor("wd_bf", [128, D], BF16))
            ecol0 = scope.enter_context(nc.sbuf_tensor("wd_e0", [128, 1], F32))
            ecol1 = scope.enter_context(nc.sbuf_tensor("wd_e1", [128, 1], F32))
            cA = scope.enter_context(nc.sbuf_tensor("wd_ca", [128, 1], F32))
            c0T = scope.enter_context(nc.sbuf_tensor("wd_c0t", [1, 128], F32))
            c1T = scope.enter_context(nc.sbuf_tensor("wd_c1t", [1, 128], F32))

            @block.vector
            def _(v):
                v.drain()
                sems["sDVE"].inc(v.memset(mbig0[:], 1.0))
            @block.gpsimd
            def _(g):
                g.wait_ge(sems["sDVE"].h, sems["sDVE"].n)
                g.affine_select(mbig0[:], mbig0[:], [[1, 2 * D]], ALU.is_ge, 0.0,
                                base=-D, channel_multiplier=-1)
                sems["sGP"].inc(g.iota(ecol0[:], [[1, 1]], base=0, channel_multiplier=-1,
                                       allow_small_or_imprecise_dtypes=True))
            @block.sync
            def _(s):
                sems["dSc"].inc(s.dma_start(out=w1r[0:1, :], in_=dram["w_dec"][1:2, :]), 16)
                s.wait_ge(sems["dSc"].h, sems["dSc"].n)
            w1r_cnt = sems["dSc"].n
            @block.vector
            def _(v):
                v.wait_ge(sems["sGP"].h, sems["sGP"].n)
                v.drain()
                v.tensor_scalar(ecol1[:], ecol0[:], sb["lnd1"][:, 0:1], None, ALU.mult)
                v.drain()
                sems["sDVE"].inc(v.tensor_scalar(ecol0[:], ecol0[:], sb["lnd0"][:, 0:1],
                                                 None, ALU.mult))
            @block.scalar
            def _(s):
                s.wait_ge(sems["sDVE"].h, sems["sDVE"].n)
                s.drain()
                s.activation(ecol0[:], ecol0[:], AF.Exp)
                s.drain()
                sems["sACT"].inc(s.activation(ecol1[:], ecol1[:], AF.Exp))
            @block.tensor
            def _(t):
                t.wait_ge(sems["sACT"].h, sems["sACT"].n)
                sems["sPE"].inc(t.transpose(ps[0][0:1, 0:128], ecol1[:],
                                            sb["ident"][:]))
            @block.vector
            def _(v):
                v.wait_ge(sems["sPE"].h, sems["sPE"].n)
                v.drain()
                sems["sDVE"].inc(v.tensor_copy(c1T[0:1, :], ps[0][0:1, 0:128]))

            nwin = D // 512
            for kc in range(DC):
                @block.vector
                def _(v, kc=kc):
                    v.wait_ge(sems["sPE"].h, sems["sPE"].n)
                    v.drain()
                    sems["sDVE"].inc(v.tensor_scalar(cA[:], ecol0[:],
                                                     sb["w0col"][:, kc:kc + 1],
                                                     None, ALU.mult))
                @block.gpsimd
                def _(g, kc=kc):
                    g.wait_ge(sems["sDVE"].h, sems["sDVE"].n)
                    sems["sGP"].inc(g.iota(vrow[0:1, :], [[1, D]], base=-128 * kc,
                                           channel_multiplier=0,
                                           allow_small_or_imprecise_dtypes=True))
                @block.tensor
                def _(t, kc=kc):
                    t.wait_ge(sems["sDVE"].h, sems["sDVE"].n)
                    sems["sPE"].inc(t.transpose(ps[0][0:1, 0:128], cA[:],
                                                sb["ident"][:]))
                @block.vector
                def _(v, kc=kc):
                    v.wait_ge(sems["sPE"].h, sems["sPE"].n)
                    v.wait_ge(sems["sGP"].h, sems["sGP"].n)
                    v.drain()
                    v.tensor_copy(c0T[0:1, :], ps[0][0:1, 0:128])
                    v.tensor_scalar(vrow[0:1, :], vrow[0:1, :], 0.0, None, ALU.max)
                    v.drain()
                    v.tensor_scalar(r0[0:1, :], vrow[0:1, :], sb["lnd0"][0:1, 0:1],
                                    None, ALU.mult)
                    v.drain()
                    sems["sDVE"].inc(v.tensor_scalar(r1[0:1, :], vrow[0:1, :],
                                                     sb["lnd1"][0:1, 0:1], None, ALU.mult))
                @block.scalar
                def _(s, kc=kc):
                    s.wait_ge(sems["sDVE"].h, sems["sDVE"].n)
                    s.drain()
                    s.activation(r0[0:1, :], r0[0:1, :], AF.Exp)
                    s.drain()
                    sems["sACT"].inc(s.activation(r1[0:1, :], r1[0:1, :], AF.Exp))
                @block.vector
                def _(v, kc=kc):
                    v.wait_ge(sems["sACT"].h, sems["sACT"].n)
                    v.wait_ge(sems["dSc"].h, w1r_cnt)
                    v.drain()
                    sems["sDVE"].inc(v.tensor_tensor(r1[0:1, :], r1[0:1, :],
                                                     w1r[0:1, :], ALU.mult))
                pe_w = sems["sPE"].n
                dve_w = sems["sDVE"].n
                @block.tensor
                def _(t, kc=kc, pe_w=pe_w, dve_w=dve_w):
                    t.wait_ge(sems["sDVE"].h, dve_w)
                    for win in range(nwin):
                        if win >= 2:
                            t.wait_ge(sems["sDVE"].h, dve_w + win - 1)
                        bkw = 2 + win % 2
                        t.matmul(ps[bkw][:, 0:512], c0T[0:1, :],
                                 r0[0:1, win * 512:(win + 1) * 512],
                                 start=True, stop=False)
                        t.matmul(ps[bkw][:, 0:512], c1T[0:1, :],
                                 r1[0:1, win * 512:(win + 1) * 512],
                                 start=False, stop=True).then_inc(sems["sPE"].h, 1)
                sems["sPE"].n = pe_w + nwin
                dwd_prev = sems["dWd"].n
                @block.vector
                def _(v, kc=kc, pe_w=pe_w, dwd_prev=dwd_prev):
                    v.wait_ge(sems["dWd"].h, dwd_prev)
                    for win in range(nwin):
                        v.wait_ge(sems["sPE"].h, pe_w + win + 1)
                        v.drain()
                        v.tensor_tensor(
                            wbfF[:, win * 512:(win + 1) * 512],
                            ps[2 + win % 2][:, 0:512],
                            mbig0[:, D - 128 * kc + win * 512: D - 128 * kc + (win + 1) * 512],
                            ALU.mult).then_inc(sems["sDVE"].h, 1)
                sems["sDVE"].n = dve_w + nwin
                @block.gpsimd
                def _(sy, kc=kc):
                    sy.wait_ge(sems["sDVE"].h, sems["sDVE"].n)
                    sems["dWd"].inc(sy.dma_start(
                        out=dram["c_wdec"][:, :, kc * 256:(kc + 1) * 256].rearrange(
                            "dcp p j -> p dcp j"),
                        in_=wbfF[:].rearrange("p (dcp j) -> p dcp j", j=256)), 16)
            self.cast_ready["wdec"] = (sems["dWd"], sems["dWd"].n)
            self.scope_guards.append((sems["dWd"], sems["dWd"].n))
            self.scope_guards.append((sems["sDVE"], sems["sDVE"].n))

    def emit_stats_xn(self, tile, extra_guards=()):
        c, block = self.c, self.block
        sems, sb = self.sems, self.sb
        DC, TT, D = c["DC"], c["TT"], c["D"]
        t0 = tile * TT
        ps, pr = self.ps, self.pr
        xnb = sb["xnb"]
        xsq = sb["H"]

        @block.vector
        def _(v):
            v.wait_ge(sems["dIn"].h, 16)
            if self.xn_guard is not None:
                v.wait_ge(self.xn_guard[0].h, self.xn_guard[1])
            for g in extra_guards:
                v.wait_ge(g[0].h, g[1])
            v.drain()
            for dc in range(DC):
                v.tensor_copy(xnb[:, dc * TT:(dc + 1) * TT], self.xres(dc, t0, TT))
            for dc in range(DC - 1):
                v.tensor_tensor(xsq[:, dc * TT:(dc + 1) * TT],
                                self.xres(dc, t0, TT), self.xres(dc, t0, TT), ALU.mult)
            dc = DC - 1
            sems["sDVE"].inc(v.tensor_tensor(xsq[:, dc * TT:(dc + 1) * TT],
                                             self.xres(dc, t0, TT),
                                             self.xres(dc, t0, TT), ALU.mult))
        @block.tensor
        def _(t):
            t.wait_ge(sems["sDVE"].h, sems["sDVE"].n)
            if self.pr_guard is not None:
                t.wait_ge(self.pr_guard[0].h, self.pr_guard[1])
            for dc in range(DC):
                t.matmul(pr[0][0:1, 0:TT], sb["onesc"][:, 0:1],
                         xnb[:, dc * TT:(dc + 1) * TT],
                         start=(dc == 0), stop=(dc == DC - 1))
            for dc in range(DC):
                last = t.matmul(pr[1][0:1, 0:TT], sb["onesc"][:, 0:1],
                                xsq[:, dc * TT:(dc + 1) * TT],
                                start=(dc == 0), stop=(dc == DC - 1))
            sems["sPE"].inc(last)
        @block.vector
        def _(v):
            v.wait_ge(sems["sPE"].h, sems["sPE"].n)
            v.drain()
            v.tensor_scalar(sb["rowA"][0:1, 0:TT], pr[0][0:1, 0:TT], 1.0 / D, None, ALU.mult)
            v.tensor_scalar(sb["rowB"][0:1, 0:TT], pr[1][0:1, 0:TT], 1.0 / D, None, ALU.mult)
            v.drain()
            v.tensor_tensor(sb["rowC"][0:1, 0:TT], sb["rowA"][0:1, 0:TT],
                            sb["rowA"][0:1, 0:TT], ALU.mult)
            v.drain()
            v.tensor_tensor(sb["rowB"][0:1, 0:TT], sb["rowB"][0:1, 0:TT],
                            sb["rowC"][0:1, 0:TT], ALU.subtract)
            v.drain()
            sems["sDVE"].inc(v.tensor_scalar(sb["rowB"][0:1, 0:TT], sb["rowB"][0:1, 0:TT],
                                             EPS, None, ALU.add))
        self.pr_guard = (sems["sDVE"], sems["sDVE"].n)
        @block.scalar
        def _(s):
            s.wait_ge(sems["sDVE"].h, sems["sDVE"].n)
            s.drain()
            s.activation(sb["rowB"][0:1, 0:TT], sb["rowB"][0:1, 0:TT], AF.Ln)
            s.drain()
            sems["sACT"].inc(s.activation(sb["rowB"][0:1, 0:TT], sb["rowB"][0:1, 0:TT],
                                          AF.Exp, scale=-0.5))
        @block.tensor
        def _(t):
            t.wait_ge(sems["sACT"].h, sems["sACT"].n)
            bm = self.take_bank(t)
            br = self.take_bank(t)
            self._mb, self._rb = bm, br
            t.matmul(ps[bm][:, 0:TT], sb["ones1"][0:1, :], sb["rowA"][0:1, 0:TT],
                     start=True, stop=True)
            sems["sPE"].inc(t.matmul(ps[br][:, 0:TT], sb["ones1"][0:1, :],
                                     sb["rowB"][0:1, 0:TT], start=True, stop=True))
        @block.vector
        def _(v):
            v.wait_ge(sems["sPE"].h, sems["sPE"].n)
            v.drain()
            v.tensor_copy(sb["mr_b"][:, 0:TT], ps[self._mb][:, 0:TT])
            sems["sDVE"].inc(v.tensor_copy(sb["mr_b"][:, TT:2 * TT], ps[self._rb][:, 0:TT]))
        self.bank_guard[self._mb] = (sems["sDVE"], sems["sDVE"].n)
        self.bank_guard[self._rb] = (sems["sDVE"], sems["sDVE"].n)
        self.xn_chunk = {}
        @block.vector
        def _(v):
            v.drain()
            for dc in range(DC):
                v.tensor_tensor(sb["xtmp"][:, 0:TT], self.xres(dc, t0, TT),
                                sb["mr_b"][:, 0:TT], ALU.subtract)
                v.drain()
                v.tensor_tensor(xnb[:, dc * TT:(dc + 1) * TT], sb["xtmp"][:, 0:TT],
                                sb["mr_b"][:, TT:2 * TT], ALU.mult
                                ).then_inc(sems["sDVE"].h, 1)
                sems["sDVE"].n += 0
                self.xn_chunk[dc] = sems["sDVE"].n + dc + 1
                v.drain()
        sems["sDVE"].n += DC
        self.xn_ready = sems["sDVE"].n

    def emit_l0_attn(self, tile):
        c, block = self.c, self.block
        sems, sb, dram = self.sems, self.sb, self.dram
        DC, TT, TC = c["DC"], c["TT"], c["TC"]
        t0 = tile * TT
        ps = self.ps
        wdec_ready = self.cast_ready["wdec"]

        for dcp in range(DC // 2):
            slot = self.slot()
            src = dram["c_wdec"][dcp].rearrange("p (kc j) -> p kc j", j=256)
            buf, wcnt = self.load_slab(slot, src, ready=wdec_ready)
            for sub in range(2):
                dc = dcp * 2 + sub
                @block.tensor
                def _(t, dc=dc, sub=sub, buf=buf, wcnt=wcnt, slot=slot):
                    dW = sems["dWA" if slot == "A" else "dWB"]
                    t.wait_ge(dW.h, wcnt)
                    t.wait_ge(sems["sDVE"].h, self.xn_ready)
                    bk = self.take_bank(t)
                    self._bk = bk
                    for kc in range(DC):
                        last = t.matmul(ps[bk][:, 0:TT],
                                        buf[:, kc * 256 + sub * 128: kc * 256 + sub * 128 + 128],
                                        sb["xnb"][:, kc * TT:(kc + 1) * TT],
                                        start=(kc == 0), stop=(kc == DC - 1))
                    sems["sPE"].inc(last)
                self.wguard[slot] = (sems["sPE"], sems["sPE"].n)
                @block.vector
                def _(v, dc=dc):
                    v.wait_ge(sems["sPE"].h, sems["sPE"].n)
                    v.drain()
                    sems["sDVE"].inc(v.scalar_tensor_tensor(
                        self.xres(dc, t0, TT), ps[self._bk][:, 0:TT],
                        sb["bias_sb"][:, dc:dc + 1], self.xres(dc, t0, TT),
                        ALU.add, ALU.add))
                self.bank_guard[self._bk] = (sems["sDVE"], sems["sDVE"].n)

    def emit_router(self, tile, layer):
        c, block = self.c, self.block
        sems, sb = self.sems, self.sb
        DC, TT, CPT = c["DC"], c["TT"], c["CPT"]
        ps = self.ps
        wr = sb["wr0_sb"] if layer == 0 else sb["wr1_sb"]

        @block.tensor
        def _(t):
            rb = self.take_bank(t)
            self._rbank = rb
            for tsub in range(CPT):
                for dc in range(DC):
                    if tsub == 0:
                        t.wait_ge(sems["sDVE"].h, self.xn_chunk[dc])
                    last = t.matmul(ps[rb][:, tsub * 8:(tsub + 1) * 8],
                                    sb["xnb"][:, dc * TT + tsub * 128: dc * TT + tsub * 128 + 128],
                                    wr[:, dc * 8:(dc + 1) * 8],
                                    start=(dc == 0), stop=(dc == DC - 1))
            sems["sPE"].inc(last)
        @block.vector
        def _(v):
            v.wait_ge(sems["sPE"].h, sems["sPE"].n)
            v.drain()
            sems["sDVE"].inc(v.tensor_copy(sb["rlog"][:, 0:CPT * 8], ps[self._rbank][:, 0:CPT * 8]))
        self.bank_guard[self._rbank] = (sems["sDVE"], sems["sDVE"].n)

        for tsub in range(CPT):
            sl = slice(tsub * 8, tsub * 8 + 8)
            if layer == 0:
                @block.vector
                def _(v, sl=sl):
                    v.drain()
                    sems["sDVE"].inc(v.reduce_max(sb["r_a"][:], sb["rlog"][:, sl],
                                                  AX.X, negate=True))
                @block.scalar
                def _(s, sl=sl):
                    s.wait_ge(sems["sDVE"].h, sems["sDVE"].n)
                    s.drain()
                    sems["sACT"].inc(s.activation(sb["escr"][:], sb["rlog"][:, sl],
                                                  AF.Exp, bias=sb["r_a"][:, 0:1]))
                @block.vector
                def _(v, tsub=tsub, sl=sl):
                    v.wait_ge(sems["sACT"].h, sems["sACT"].n)
                    v.drain()
                    v.tensor_reduce(sb["r_b"][:], sb["escr"][:], AX.X, ALU.add)
                    v.drain()
                    v.reciprocal(sb["r_c"][:], sb["r_b"][:])
                    v.drain()
                    v.tensor_scalar(sb["escr"][:], sb["escr"][:], sb["r_c"][:, 0:1],
                                    None, ALU.mult)
                    v.drain()
                    v.reduce_max(sb["r_a"][:], sb["escr"][:], AX.X)
                    v.drain()
                    v.tensor_scalar(sb["escr2"][:], sb["escr"][:], sb["r_a"][:, 0:1],
                                    None, ALU.is_equal)
                    v.drain()
                    v.scalar_tensor_tensor(sb["e8"][:], sb["escr2"][:], NEG_BIG,
                                           sb["escr"][:], ALU.mult, ALU.add)
                    v.drain()
                    v.reduce_max(sb["r_b"][:], sb["e8"][:], AX.X)
                    v.drain()
                    v.tensor_scalar(sb["e8"][:], sb["escr"][:], sb["r_b"][:, 0:1],
                                    None, ALU.is_ge)
                    v.drain()
                    sems["sDVE"].inc(v.tensor_tensor(sb["cb_tm"][:, sl], sb["escr"][:],
                                                     sb["e8"][:], ALU.mult))
            else:
                @block.vector
                def _(v, sl=sl):
                    v.drain()
                    v.reduce_max(sb["r_a"][:], sb["rlog"][:, sl], AX.X)
                    v.drain()
                    v.tensor_scalar(sb["escr2"][:], sb["rlog"][:, sl], sb["r_a"][:, 0:1],
                                    None, ALU.is_equal)
                    v.drain()
                    v.scalar_tensor_tensor(sb["e8"][:], sb["escr2"][:], NEG_BIG,
                                           sb["rlog"][:, sl], ALU.mult, ALU.add)
                    v.drain()
                    v.reduce_max(sb["r_b"][:], sb["e8"][:], AX.X)
                    v.drain()
                    v.tensor_scalar(sb["escr"][:], sb["rlog"][:, sl], sb["r_b"][:, 0:1],
                                    None, ALU.is_equal)
                    v.drain()
                    sems["sDVE"].inc(v.tensor_tensor(sb["r_c"][:], sb["r_a"][:],
                                                     sb["r_b"][:], ALU.subtract))
                @block.scalar
                def _(s):
                    s.wait_ge(sems["sDVE"].h, sems["sDVE"].n)
                    s.drain()
                    sems["sACT"].inc(s.activation(sb["r_d"][:], sb["r_c"][:], AF.Exp,
                                                  scale=-1.0))
                @block.vector
                def _(v, sl=sl):
                    v.wait_ge(sems["sACT"].h, sems["sACT"].n)
                    v.drain()
                    v.tensor_scalar(sb["r_d"][:], sb["r_d"][:], 1.0, None, ALU.add)
                    v.drain()
                    v.reciprocal(sb["r_c"][:], sb["r_d"][:])
                    v.drain()
                    v.tensor_scalar(sb["r_d"][:], sb["r_c"][:], -1.0, 1.0,
                                    ALU.mult, ALU.add)
                    v.drain()
                    v.tensor_scalar(sb["escr2"][:], sb["escr2"][:], sb["r_c"][:, 0:1],
                                    None, ALU.mult)
                    v.drain()
                    sems["sDVE"].inc(v.scalar_tensor_tensor(
                        sb["cb_tm"][:, sl], sb["escr"][:], sb["r_d"][:, 0:1],
                        sb["escr2"][:], ALU.mult, ALU.add))
        @block.tensor
        def _(t):
            t.wait_ge(sems["sDVE"].h, sems["sDVE"].n)
            cbk = self.take_bank(t)
            self._cbk = cbk
            for tsub in range(CPT):
                last = t.transpose(ps[cbk][0:8, tsub * 128:(tsub + 1) * 128],
                                   sb["cb_tm"][:, tsub * 8:(tsub + 1) * 8], sb["ident"][:])
            sems["sPE"].inc(last)
        @block.vector
        def _(v):
            v.wait_ge(sems["sPE"].h, sems["sPE"].n)
            v.drain()
            sems["sDVE"].inc(v.tensor_copy(sb["cb_fm"][0:8, 0:TT],
                                           ps[self._cbk][0:8, 0:CPT * 128]))
        self.bank_guard[self._cbk] = (sems["sDVE"], sems["sDVE"].n)
        self.cb_ready = sems["sDVE"].n

    def emit_cb_broadcast(self, e):
        sems, sb, block = self.sems, self.sb, self.block
        TT = self.c["TT"]
        ps = self.ps

        @block.tensor
        def _(t):
            t.wait_ge(sems["sDVE"].h, self.cb_ready)
            bk = self.take_bank(t)
            self._cbb = bk
            sems["sPE"].inc(t.matmul(ps[bk][:, 0:TT],
                                     sb["selmat"][0:8, e * 128:(e + 1) * 128],
                                     sb["cb_fm"][0:8, 0:TT], start=True, stop=True))
        @block.vector
        def _(v):
            v.wait_ge(sems["sPE"].h, sems["sPE"].n)
            v.drain()
            sems["sDVE"].inc(v.tensor_copy(sb["cb_b"][:, 0:TT], ps[self._cbb][:, 0:TT]))
        self.bank_guard[self._cbb] = (sems["sDVE"], sems["sDVE"].n)
        self.cbb_ready = sems["sDVE"].n

    def emit_gate_up_down(self, tile, gsrc, usrc, dsrc, nhc, with_cb, ready):
        c, block = self.c, self.block
        sems, sb = self.sems, self.sb
        DC, TT, TC = c["DC"], c["TT"], c["TC"]
        t0 = tile * TT
        ps = self.ps
        H = sb["H"]
        pe0, act0, dve0 = sems["sPE"].n, sems["sACT"].n, sems["sDVE"].n

        wcnt = {}
        slots = {}
        for hc in range(nhc):
            slot = self.slot()
            slots[hc] = slot
            gs = gsrc[hc].rearrange("p (dc j) -> p dc j", j=128)
            us = usrc[hc].rearrange("p (dc j) -> p dc j", j=128)
            self.load_slab(slot, gs, off=0, ready=ready if hc == 0 else None)
            _, wcnt[hc] = self.load_slab(slot, us, off=DC * 128, first=False)
            self.wguard[slot] = (sems["sPE"], pe0 + hc + 1)

        banks = {}
        @block.tensor
        def _(t):
            for hc in range(nhc):
                slot = slots[hc]
                buf = sb["wA" if slot == "A" else "wB"]
                t.wait_ge(sems["dWA" if slot == "A" else "dWB"].h, wcnt[hc])
                bg = self.take_bank(t)
                bu = self.take_bank(t)
                banks[hc] = (bg, bu)
                for dc in range(DC):
                    if hc == 0:
                        t.wait_ge(sems["sDVE"].h, self.xn_chunk[dc])
                    t.matmul(ps[bg][:, 0:TT], buf[:, dc * 128:(dc + 1) * 128],
                             sb["xnb"][:, dc * TT:(dc + 1) * TT],
                             start=(dc == 0), stop=(dc == DC - 1))
                for dc in range(DC):
                    last = t.matmul(ps[bu][:, 0:TT],
                                    buf[:, DC * 128 + dc * 128: DC * 128 + (dc + 1) * 128],
                                    sb["xnb"][:, dc * TT:(dc + 1) * TT],
                                    start=(dc == 0), stop=(dc == DC - 1))
                last.then_inc(sems["sPE"].h, 1)
                self.bank_guard[bg] = (sems["sACT"], act0 + hc + 1)
                self.bank_guard[bu] = (sems["sDVE"], dve0 + 2 * (hc + 1))
        sems["sPE"].n = pe0 + nhc

        @block.scalar
        def _(s):
            for hc in range(nhc):
                s.wait_ge(sems["sPE"].h, pe0 + hc + 1)
                s.drain()
                s.activation(H[:, hc * TT:(hc + 1) * TT], ps[banks[hc][0]][:, 0:TT],
                             AF.Silu).then_inc(sems["sACT"].h, 1)
        sems["sACT"].n = act0 + nhc

        @block.vector
        def _(v):
            if with_cb:
                v.wait_ge(sems["sDVE"].h, self.cbb_ready)
            for hc in range(nhc):
                v.wait_ge(sems["sPE"].h, pe0 + hc + 1)
                v.wait_ge(sems["sACT"].h, act0 + hc + 1)
                v.drain()
                if with_cb:
                    v.tensor_tensor(sb["ue"][:, 0:TT], ps[banks[hc][1]][:, 0:TT],
                                    sb["cb_b"][:, 0:TT], ALU.mult
                                    ).then_inc(sems["sDVE"].h, 1)
                else:
                    v.tensor_copy(sb["ue"][:, 0:TT], ps[banks[hc][1]][:, 0:TT]
                                  ).then_inc(sems["sDVE"].h, 1)
                v.drain()
                v.tensor_tensor(H[:, hc * TT:(hc + 1) * TT],
                                H[:, hc * TT:(hc + 1) * TT],
                                sb["ue"][:, 0:TT], ALU.mult).then_inc(sems["sDVE"].h, 1)
        sems["sDVE"].n = dve0 + 2 * nhc
        self.h_ready = sems["sDVE"].n

        pe1, dve1 = sems["sPE"].n, sems["sDVE"].n
        pair = (2 * nhc * 128 <= 4096)
        step = 2 if pair else 1
        dwcnt = {}
        dslots = {}
        for dc0 in range(0, DC, step):
            slot = self.slot()
            dsl = dsrc[dc0:dc0 + step].rearrange("s p c -> p s c")
            _, wc = self.load_slab(slot, dsl)
            for dc in range(dc0, dc0 + step):
                dslots[dc] = (slot, dc - dc0)
                dwcnt[dc] = wc
            self.wguard[slot] = (sems["sPE"], pe1 + dc0 + step)
        dbanks = {}
        @block.tensor
        def _(t):
            t.wait_ge(sems["sDVE"].h, self.h_ready)
            for dc in range(DC):
                slot, sub = dslots[dc]
                buf = sb["wA" if slot == "A" else "wB"]
                t.wait_ge(sems["dWA" if slot == "A" else "dWB"].h, dwcnt[dc])
                bk = self.take_bank(t)
                dbanks[dc] = bk
                for hc in range(nhc):
                    last = t.matmul(ps[bk][:, 0:TT],
                                    buf[:, (sub * nhc + hc) * 128:(sub * nhc + hc + 1) * 128],
                                    H[:, hc * TT:(hc + 1) * TT],
                                    start=(hc == 0), stop=(hc == nhc - 1))
                last.then_inc(sems["sPE"].h, 1)
                self.bank_guard[bk] = (sems["sDVE"], dve1 + dc + 1)
        sems["sPE"].n = pe1 + DC
        @block.vector
        def _(v):
            for dc in range(DC):
                v.wait_ge(sems["sPE"].h, pe1 + dc + 1)
                v.drain()
                v.scalar_tensor_tensor(
                    self.xres(dc, t0, TT), ps[dbanks[dc]][:, 0:TT], 0.0,
                    self.xres(dc, t0, TT), ALU.add, ALU.add).then_inc(sems["sDVE"].h, 1)
        sems["sDVE"].n = dve1 + DC

    def emit_l0_tile(self, tile, guards=()):
        c, dram = self.c, self.dram
        self.emit_stats_xn(tile, extra_guards=guards)
        self.emit_router(tile, layer=0)
        if tile == 0:
            self.emit_l0_attn(tile)
        self.emit_gate_up_down(tile, dram["c_wsg"], dram["c_wsu"], dram["c_wsd"],
                               c["SHC"], with_cb=False, ready=self.cast_ready["ws"])
        for e in range(c["E"]):
            self.emit_cb_broadcast(e)
            self.emit_gate_up_down(tile, dram["c_weg"][e], dram["c_weu"][e],
                                   dram["c_wed"][e], c["EHC"], with_cb=True,
                                   ready=self.cast_ready[f"we{e}"])
        if tile != 0:
            self.emit_l0_attn(tile)
        self.xn_guard = (self.sems["sPE"], self.sems["sPE"].n)

    def emit_kvq_tile(self, tile):
        c, block = self.c, self.block
        sems, sb, dram = self.sems, self.sb, self.dram
        DC, TT, TC, NH, CPT, D = c["DC"], c["TT"], c["TC"], c["NH"], c["CPT"], c["D"]
        ps = self.ps
        qk_ready = self.cast_ready["qkvo"]
        nh_half = DC // 2
        nhb = 512 // 128

        units = []
        for h in range(NH):
            units.append(("K", h))
            units.append(("Q", h))
        for eb in range(D // 512):
            for tsub in range(CPT):
                units.append(("V", (eb, tsub)))

        pe0 = sems["sPE"].n
        wcnt = {}
        uslots = {}
        vslabs = {}
        ui = 0
        for kind, arg in units:
            if kind in ("K", "Q"):
                slot = self.slot()
                uslots[ui] = slot
                src = dram["c_kw" if kind == "K" else "c_qw"][arg].rearrange(
                    "p (dc j) -> p dc j", j=128)
                _, wcnt[ui] = self.load_slab(
                    slot, src, ready=qk_ready if (ui == 0 and tile == 0) else None)
                self.wguard[slot] = (sems["sPE"], pe0 + ui + 1)
            else:
                eb, tsub = arg
                if tsub == 0:
                    slot = self.slot()
                    srcA = dram["c_vw"][eb, 0].rearrange("p (di e) -> p di e", e=512)
                    _, wA_c = self.load_slab(slot, srcA)
                    slotB = self.slot()
                    srcB = dram["c_vw"][eb, 1].rearrange("p (di e) -> p di e", e=512)
                    _, wB_c = self.load_slab(slotB, srcB)
                    vslabs[eb] = (slot, slotB, wA_c, wB_c)
                    self.wguard[slot] = (sems["sPE"], pe0 + ui + CPT)
                    self.wguard[slotB] = (sems["sPE"], pe0 + ui + CPT)
                uslots[ui] = vslabs[eb]
            ui += 1

        @block.tensor
        def _(t):
            ui = 0
            for kind, arg in units:
                bk = self.take_bank(t)
                self._kv_banks = getattr(self, "_kv_banks", {})
                self._kv_banks[ui] = bk
                if kind in ("K", "Q"):
                    slot = uslots[ui]
                    buf = sb["wA" if slot == "A" else "wB"]
                    t.wait_ge(sems["dWA" if slot == "A" else "dWB"].h, wcnt[ui])
                    for dc in range(DC):
                        if ui == 0:
                            t.wait_ge(sems["sDVE"].h, self.xn_chunk[dc])
                        last = t.matmul(ps[bk][:, 0:TT], buf[:, dc * 128:(dc + 1) * 128],
                                        sb["xnb"][:, dc * TT:(dc + 1) * TT],
                                        start=(dc == 0), stop=(dc == DC - 1))
                else:
                    eb, tsub = arg
                    slot, slotB, wA_c, wB_c = uslots[ui]
                    t.wait_ge(sems["dWA" if slot == "A" else "dWB"].h, wA_c)
                    t.wait_ge(sems["dWA" if slotB == "A" else "dWB"].h, wB_c)
                    for dc in range(DC):
                        sl = slot if dc < nh_half else slotB
                        buf = sb["wA" if sl == "A" else "wB"]
                        di = dc if dc < nh_half else dc - nh_half
                        last = t.matmul(ps[bk][:, 0:512],
                                        sb["xnb"][:, dc * TT + tsub * 128: dc * TT + tsub * 128 + 128],
                                        buf[:, di * 512:(di + 1) * 512],
                                        start=(dc == 0), stop=(dc == DC - 1))
                last.then_inc(sems["sPE"].h, 1)
                self.bank_guard[bk] = (sems["sDVE"], sems["sDVE"].n + ui + 1)
                ui += 1
        n_units = len(units)
        pe_end = pe0 + n_units
        sems["sPE"].n = pe_end

        dve0 = sems["sDVE"].n
        ktn = {}
        ktsem_cnt = {}
        for ui in range(n_units):
            i = (self._kvt_rr if hasattr(self, "_kvt_rr") else 0)
            name = "ktmpA" if (ui % 2 == 0) else "ktmpB"
            ktn[ui] = name
        kt_base = {"ktmpA": sems["dKtA"].n, "ktmpB": sems["dKtB"].n}
        kt_cnt = {"ktmpA": 0, "ktmpB": 0}
        @block.vector
        def _(v):
            for ui, (kind, arg) in enumerate(units):
                ncols = TT if kind in ("K", "Q") else 512
                ktsem = sems["dKtA" if ui % 2 == 0 else "dKtB"]
                v.wait_ge(ktsem.h, kt_base[ktn[ui]] + 16 * kt_cnt[ktn[ui]])
                kt_cnt[ktn[ui]] += 1
                v.wait_ge(sems["sPE"].h, pe0 + ui + 1)
                v.drain()
                v.tensor_copy(sb[ktn[ui]][:, 0:ncols],
                              ps[self._kv_banks[ui]][:, 0:ncols]).then_inc(sems["sDVE"].h, 1)
        sems["sDVE"].n = dve0 + n_units

        @block.gpsimd
        def _(g):
            for ui, (kind, arg) in enumerate(units):
                ktsem = sems["dKtA" if ui % 2 == 0 else "dKtB"]
                g.wait_ge(sems["sDVE"].h, dve0 + ui + 1)
                if kind == "K":
                    dst = dram["kvloc"][arg * 128:(arg + 1) * 128, tile * TT:(tile + 1) * TT]
                    ktsem.inc(g.dma_start(out=dst, in_=sb[ktn[ui]][:, 0:TT]), 16)
                elif kind == "Q":
                    dst = dram["qloc"][arg * 128:(arg + 1) * 128, tile * TT:(tile + 1) * TT]
                    ktsem.inc(g.dma_start(out=dst, in_=sb[ktn[ui]][:, 0:TT]), 16)
                else:
                    eb, tsub = arg
                    h0 = eb * nhb
                    cloc = tile * CPT + tsub
                    dst = bass.AP(
                        dram["kvloc"].tensor,
                        (NH * 128 + h0 * 128) * TC + cloc * 128,
                        [[TC, 128], [128 * TC, nhb], [1, 128]])
                    ktsem.inc(g.dma_start(
                        out=dst,
                        in_=sb[ktn[ui]][:, 0:512].rearrange("p (hh hd) -> p hh hd", hd=128)), 16)
        self.xn_guard = (sems["sPE"], sems["sPE"].n)

    def emit_collective(self):
        sems, dram, block = self.sems, self.dram, self.block

        @block.gpsimd
        def _(g):
            g.wait_ge(sems["dKtA"].h, sems["dKtA"].n)
            g.wait_ge(sems["dKtB"].h, sems["dKtB"].n)
            sems["sCC"].inc(g.collective_compute(
                "AllGather", ALU.bypass, replica_groups=GROUPS,
                ins=[dram["kvloc"]], outs=[dram["kvg"]]))

    def head_sched(self, parity):
        tiles = EVEN_TILES if parity == 0 else ODD_TILES
        CPT, TT = self.c["CPT"], self.c["TT"]
        out = []
        for s in range(4):
            gt = tiles[s]
            q0 = gt * TT
            lst = []
            for G in range(CPT * (gt + 1)):
                d = 128 * G - q0
                lst.append((self.bufcol[G], (TT - d) if d >= 0 else None))
            out.append(lst)
        return out

    def emit_attention(self):
        c, block, nc = self.c, self.block, self.nc
        sems, sb, dram = self.sems, self.sb, self.dram
        DC, TT, TC, NH, CPT, D = c["DC"], c["TT"], c["TC"], c["NH"], c["CPT"], c["D"]
        KCH = c["KCH"]
        ps, pr = self.ps, self.pr
        nhh = NH // 2
        sc = [self.head_sched(0), self.head_sched(1)]
        inv_sqrt_hd = 1.0 / math.sqrt(c["HD"])

        SBANKS = [0, 1, 5]
        Ks = [sb["KsA"], sb["KsB"]]
        Vs = [sb["VsA"], sb["VsB"]]
        Qs = [sb["QsA"], sb["QsB"]]
        dKs = [sems["dKsA"], sems["dKsB"]]
        dVs = [sems["dVsA"], sems["dVsB"]]
        dQs = [sems["dQsA"], sems["dQsB"]]

        @block.vector
        def _(v):
            for g in self.scope_guards2:
                v.wait_ge(g[0].h, g[1])
            v.drain()
            sems["sDVE"].inc(v.memset(sb["Mbig"][:], 1.0))
        @block.gpsimd
        def _(g):
            g.wait_ge(sems["sDVE"].h, sems["sDVE"].n)
            sems["sGP"].inc(g.affine_select(sb["Mbig"][:], sb["Mbig"][:], [[1, 3 * TT]],
                                            ALU.is_ge, 0.0, base=-TT, channel_multiplier=-1))
        mbig_ready = (sems["sGP"], sems["sGP"].n)

        pe0, act0, dve0 = sems["sPE"].n, sems["sACT"].n, sems["sDVE"].n

        def nmax_s(s):
            return max(len(sc[0][s]), len(sc[1][s]))

        def bases(par, s, h):
            n_max = nmax_s(s)
            pe, act, dve = pe0, act0, dve0
            for s2 in range(s):
                nm2 = nmax_s(s2)
                pe += NH * (3 * nm2 + 2) + 2 * DC
                act += NH * (1 + nm2)
                dve += NH * (CPT + 6) + 2 * DC
            pe += h * (3 * n_max + 2) + (DC if h > nhh - 1 else 0)
            act += h * (1 + n_max)
            dve += h * (CPT + 6) + (DC if h > nhh - 1 else 0)
            return pe, act, dve, n_max

        def pe_head_end(s, h):
            pe, _, _, n_max = bases(0, s, h)
            return pe + 3 * n_max + 2

        last_use = {}
        load_cnt = {}
        oslab = {}
        first_load = True

        for s in range(4):
            for h in range(NH):
                sl = h % 2
                @block.sync
                def _(sy, s=s, h=h, sl=sl, first=first_load):
                    if first:
                        for g in self.scope_guards2:
                            sy.wait_ge(g[0].h, g[1])
                        sy.wait_ge(sems["sCC"].h, 1)
                        sy.wait_ge(sems["sGP"].h, mbig_ready[1])
                    lu = last_use.get(("K", sl))
                    if lu is not None:
                        sy.wait_ge(sems["sPE"].h, lu)
                    dKs[sl].inc(sy.dma_start(
                        out=Ks[sl][:, 0:TC], in_=dram["kvg"][h * 128:(h + 1) * 128, :]), 16)
                    dKs[sl].inc(sy.dma_start(
                        out=Ks[sl][:, TC:2 * TC],
                        in_=dram["kvg"][2 * D + h * 128: 2 * D + (h + 1) * 128, :]), 16)
                    dVs[sl].inc(sy.dma_start(
                        out=Vs[sl][:, 0:TC],
                        in_=dram["kvg"][D + h * 128: D + (h + 1) * 128, :]), 16)
                    dVs[sl].inc(sy.dma_start(
                        out=Vs[sl][:, TC:2 * TC],
                        in_=dram["kvg"][3 * D + h * 128: 3 * D + (h + 1) * 128, :]), 16)
                    dQs[sl].inc(sy.dma_start(
                        out=Qs[sl][:, 0:TT],
                        in_=dram["qloc"][h * 128:(h + 1) * 128, s * TT:(s + 1) * TT]), 16)
                first_load = False
                load_cnt[("K", s, h)] = dKs[sl].n
                load_cnt[("V", s, h)] = dVs[sl].n
                load_cnt[("Q", s, h)] = dQs[sl].n
                last_use[("K", sl)] = pe_head_end(s, h)
                if h == nhh - 1 or h == NH - 1:
                    half = 0 if h == nhh - 1 else 1
                    nq = DC // 4
                    pe_done_half = pe_head_end(s, h)
                    for q in range(nq):
                        slot = self.slot()
                        osrc = dram["c_ow"][half, q].rearrange("p (kc j) -> p kc j", j=512)
                        gspec = self.wguard[slot]
                        @block.sync
                        def _(sy, slot=slot, osrc=osrc, gspec=gspec):
                            if gspec is not None:
                                sy.wait_ge(gspec[0].h, gspec[1])
                            dW = sems["dWA" if slot == "A" else "dWB"]
                            a, b = int(osrc.shape[1]), int(osrc.shape[2])
                            buf = sb["wA" if slot == "A" else "wB"]
                            dest = buf[:, 0:a * b].rearrange("p (a b) -> p a b", b=b)
                            dW.inc(sy.dma_start(out=dest, in_=osrc), 16)
                        dWs = sems["dWA" if slot == "A" else "dWB"]
                        oslab[(s, half, q)] = (slot, sb["wA" if slot == "A" else "wB"], dWs.n)
                        self.wguard[slot] = (sems["sPE"], pe_done_half + 4 * (q + 1))

        def slot_end(s):
            pe, act, dve, _ = bases(0, s, NH - 1)
            nm = max(len(sc[0][s]), len(sc[1][s]))
            return (pe + (3 * nm + 2) + DC, act + (1 + nm), dve + (CPT + 6) + DC)

        for s in range(4):
            @block.tensor
            def _(t, s=s):
                def body(par):
                    for h in range(NH):
                        pe_b, act_b, dve_b, n_max = bases(par, s, h)
                        n = len(sc[par][s])
                        t.wait_ge(dQs[h % 2].h, load_cnt[("Q", s, h)])
                        t.wait_ge(sems["sACT"].h, act_b)
                        t.matmul(ps[3][:, 0:TT], sb["gwT_sb"][:], Qs[h % 2][:, 0:TT],
                                 start=True, stop=True, skip_group_check=True
                                 ).then_inc(sems["sPE"].h, 1)
                        t.wait_ge(dKs[h % 2].h, load_cnt[("K", s, h)])
                        t.wait_ge(dVs[h % 2].h, load_cnt[("V", s, h)])

                        def score(j):
                            bc, off = sc[par][s][j]
                            if j >= 3:
                                t.wait_ge(sems["sACT"].h, act_b + 2 + (j - 3))
                            else:
                                t.wait_ge(sems["sACT"].h, act_b)
                            if j == 2:
                                t.wait_ge(sems["sDVE"].h, dve_b)
                            t.matmul(ps[SBANKS[j % 3]][:, 0:TT],
                                     Ks[h % 2][:, bc * 128:(bc + 1) * 128],
                                     Qs[h % 2][:, 0:TT], start=True, stop=True,
                                     skip_group_check=True).then_inc(sems["sPE"].h, 1)

                        def avsum(j):
                            bc, off = sc[par][s][j]
                            t.wait_ge(sems["sACT"].h,
                                      act_b + 2 + j if j < n - 1 else act_b + 1 + n_max)
                            if off is not None:
                                jd = j - (n - CPT)
                                t.wait_ge(sems["sDVE"].h, dve_b + 3 + jd)
                            if j == 0:
                                t.wait_ge(sems["sDVE"].h, dve_b)
                            t.matmul(ps[2][:, 0:TT],
                                     Vs[h % 2][:, bc * 128:(bc + 1) * 128],
                                     sb["PT" + str(j % 3)][:, 0:TT],
                                     start=(j == 0), stop=(j == n - 1),
                                     skip_group_check=True).then_inc(sems["sPE"].h, 1)
                            t.matmul(pr[0][0:1, 0:TT], sb["onesc"][:, 0:1],
                                     sb["PT" + str(j % 3)][:, 0:TT],
                                     start=(j == 0), stop=(j == n - 1),
                                     skip_group_check=True).then_inc(
                                sems["sPE"].h, 1 + (3 * (n_max - n) if j == n - 1 else 0))

                        score(0)
                        if n > 1:
                            score(1)
                        for j in range(2, n):
                            score(j)
                            avsum(j - 2)
                        if n > 1:
                            avsum(n - 2)
                        avsum(n - 1)
                        t.wait_ge(sems["sDVE"].h, dve_b + 3 + CPT)
                        t.matmul(ps[4][:, 0:TT], sb["ones1"][0:1, :],
                                 sb["rowA"][0:1, 0:TT], start=True, stop=True,
                                 skip_group_check=True).then_inc(sems["sPE"].h, 1)
                        if h == nhh - 1 or h == NH - 1:
                            half = 0 if h == nhh - 1 else 1
                            pe_done = pe_b + 3 * n_max + 2
                            dve_done = dve_b + CPT + 6
                            for dc in range(DC):
                                slot, buf, wcnt = oslab[(s, half, dc // 4)]
                                dsub = dc % 4
                                t.wait_ge(sems["dWA" if slot == "A" else "dWB"].h, wcnt)
                                t.wait_ge(sems["sDVE"].h, dve_done)
                                bk = 5 if dc % 2 == 0 else 4
                                if dc >= 2:
                                    t.wait_ge(sems["sDVE"].h, dve_done + dc - 1)
                                for kc in range(nhh):
                                    last = t.matmul(
                                        ps[bk][:, 0:TT],
                                        buf[:, kc * 512 + dsub * 128: kc * 512 + dsub * 128 + 128],
                                        sb["out8"][:, kc * TT:(kc + 1) * TT],
                                        start=(kc == 0), stop=(kc == nhh - 1),
                                        skip_group_check=True)
                                last.then_inc(sems["sPE"].h, 1)
                with t.register(f"prt_t{s}") as prt:
                    t.cc_rank_ld(prt, replica_groups=GROUPS)
                    with t.If_eq(prt, 0):
                        body(0)
                    with t.Else():
                        body(1)

            @block.scalar
            def _(sx, s=s):
                def body(par):
                    for h in range(NH):
                        pe_b, act_b, dve_b, n_max = bases(par, s, h)
                        n = len(sc[par][s])
                        sx.wait_ge(sems["sPE"].h, pe_b + 1)
                        sx.wait_ge(sems["sDVE"].h, dve_b)
                        sx.drain()
                        sx.activation(sb["gatebuf"][:, 0:TT], ps[3][:, 0:TT], AF.Exp,
                                      scale=-1.0, bias=sb["gbneg"][:, 0:1]
                                      ).then_inc(sems["sACT"].h, 1)
                        for j in range(n):
                            pos = 2 if j == 0 else (3 if j == 1 else 3 * j - 2)
                            sx.wait_ge(sems["sPE"].h, pe_b + pos)
                            sx.drain()
                            a = sx.activation(sb["PT" + str(j % 3)][:, 0:TT],
                                              ps[SBANKS[j % 3]][:, 0:TT], AF.Exp,
                                              scale=inv_sqrt_hd)
                            a.then_inc(sems["sACT"].h,
                                       1 + (n_max - n) if j == n - 1 else 1)
                with sx.register(f"prt_s{s}") as prt:
                    sx.cc_rank_ld(prt, replica_groups=GROUPS)
                    with sx.If_eq(prt, 0):
                        body(0)
                    with sx.Else():
                        body(1)

            @block.vector
            def _(v, s=s):
                def body(par):
                    for h in range(NH):
                        pe_b, act_b, dve_b, n_max = bases(par, s, h)
                        n = len(sc[par][s])
                        v.wait_ge(sems["sACT"].h, act_b + 1)
                        v.drain()
                        v.tensor_scalar(sb["gatebuf"][:, 0:TT], sb["gatebuf"][:, 0:TT],
                                        1.0, None, ALU.add).then_inc(sems["sDVE"].h, 1)
                        v.drain()
                        v.reciprocal(sb["gatebuf"][:, 0:TT], sb["gatebuf"][:, 0:TT]
                                     ).then_inc(sems["sDVE"].h, 1)
                        for jd in range(CPT):
                            j = n - CPT + jd
                            bc, off = sc[par][s][j]
                            v.wait_ge(sems["sACT"].h,
                                      act_b + 2 + j if j < n - 1 else act_b + 1 + n_max)
                            v.drain()
                            v.tensor_tensor(sb["PT" + str(j % 3)][:, 0:TT],
                                            sb["PT" + str(j % 3)][:, 0:TT],
                                            sb["Mbig"][:, off:off + TT], ALU.mult
                                            ).then_inc(sems["sDVE"].h, 1)
                        v.wait_ge(sems["sPE"].h, pe_b + 3 * n_max + 1)
                        v.drain()
                        v.reciprocal(sb["rowA"][0:1, 0:TT], pr[0][0:1, 0:TT]
                                     ).then_inc(sems["sDVE"].h, 1)
                        v.wait_ge(sems["sPE"].h, pe_b + 3 * n_max + 2)
                        v.drain()
                        v.tensor_copy(sb["rb_sb"][:, 0:TT], ps[4][:, 0:TT]
                                      ).then_inc(sems["sDVE"].h, 1)
                        if h >= nhh:
                            pass
                        v.drain()
                        v.tensor_tensor(sb["out8"][:, (h % nhh) * TT:(h % nhh + 1) * TT],
                                        ps[2][:, 0:TT], sb["rb_sb"][:, 0:TT], ALU.mult
                                        ).then_inc(sems["sDVE"].h, 1)
                        v.drain()
                        v.tensor_tensor(sb["out8"][:, (h % nhh) * TT:(h % nhh + 1) * TT],
                                        sb["out8"][:, (h % nhh) * TT:(h % nhh + 1) * TT],
                                        sb["gatebuf"][:, 0:TT], ALU.mult
                                        ).then_inc(sems["sDVE"].h, 1)
                        if h == nhh - 1 or h == NH - 1:
                            pe_done = pe_b + 3 * n_max + 2
                            t0 = s * TT
                            for dc in range(DC):
                                v.wait_ge(sems["sPE"].h, pe_done + dc + 1)
                                v.drain()
                                v.scalar_tensor_tensor(
                                    self.xres(dc, t0, TT),
                                    ps[5 if dc % 2 == 0 else 4][:, 0:TT], 0.0,
                                    self.xres(dc, t0, TT), ALU.add, ALU.add
                                    ).then_inc(sems["sDVE"].h, 1)
                with v.register(f"prt_v{s}") as prt:
                    v.cc_rank_ld(prt, replica_groups=GROUPS)
                    with v.If_eq(prt, 0):
                        body(0)
                    with v.Else():
                        body(1)

            pe_e, act_e, dve_e = slot_end(s)
            sems["sPE"].n = pe_e
            sems["sACT"].n = act_e
            sems["sDVE"].n = dve_e
        for i in range(6):
            self.bank_guard[i] = (sems["sDVE"], sems["sDVE"].n)
        self.pr_guard = (sems["sDVE"], sems["sDVE"].n)

    def emit_l1_ffn_tile(self, tile):
        c, dram = self.c, self.dram
        self.emit_stats_xn(tile)
        self.emit_router(tile, layer=1)
        for e in range(c["E"]):
            self.emit_cb_broadcast(e)
            self.emit_gate_up_down(tile, dram["c_gp"][e], dram["c_up"][e],
                                   dram["c_dp"][e], c["EHC"], with_cb=True,
                                   ready=self.cast_ready[f"l1e{e}"])
        self.xn_guard = (self.sems["sPE"], self.sems["sPE"].n)

    def emit_epilogue(self):
        sems, sb, dram, block = self.sems, self.sb, self.dram, self.block
        TC = self.c["TC"]

        @block.scalar
        def _(s):
            s.wait_ge(sems["sDVE"].h, sems["sDVE"].n)
            sems["dOut"].inc(s.dma_start(out=dram[self.out_name], in_=sb["x_res"][:]), 16)
            s.wait_ge(sems["dOut"].h, sems["dOut"].n)


def build_program(cfg, layers=(0, 1), in_name="x_in", out_name="out"):
    nc = bass.Bass("TRN2", target_bir_lowering=False)
    b = B(nc, cfg, layers, in_name, out_name)
    b.declare()
    c = b.c
    TT, DC, SHC, EHC = c["TT"], c["DC"], c["SHC"], c["EHC"]
    with b.es:
        b.alloc_sems()
        b.alloc_psum()
        b.alloc_persistent()
        with nc.Block() as block:
            b.block = block
            b.emit_prologue()
            with contextlib.ExitStack() as s1:
                def S1(name, shape, dt):
                    b.sb[name] = s1.enter_context(nc.sbuf_tensor(name, shape, dt))
                hchunks = max(DC, EHC, SHC if 0 in layers else 0)
                S1("xnb", [128, DC * TT], BF16)
                S1("H", [128, hchunks * TT], BF16)
                S1("mr_b", [128, 2 * TT], F32)
                S1("rowB", [1, TT], F32)
                S1("rowC", [1, TT], F32)
                S1("ktmpA2", [128, 512], BF16)
                S1("ktmpB2", [128, 512], BF16)
                S1("ue", [128, TT], BF16)
                b.sb["ktmpA"] = b.sb["ktmpA2"]
                b.sb["ktmpB"] = b.sb["ktmpB2"]
                if 0 in layers:
                    for tile in range(4):
                        b.emit_l0_tile(tile, guards=tuple(b.scope_guards) if tile == 0 else ())
                if 1 in layers:
                    g0 = tuple(b.scope_guards) if 0 not in layers else ()
                    for tile in range(4):
                        b.emit_stats_xn(tile, extra_guards=g0 if tile == 0 else ())
                        b.emit_kvq_tile(tile)
                    b.emit_collective()
                    for tile in range(4):
                        b.emit_l1_ffn_tile(tile)
                b.scope_guards2 = [(b.sems["sPE"], b.sems["sPE"].n),
                                   (b.sems["sDVE"], b.sems["sDVE"].n),
                                   (b.sems["sACT"], b.sems["sACT"].n)]
            if 1 in layers:
                with contextlib.ExitStack() as s2:
                    def S2(name, shape, dt):
                        b.sb[name] = s2.enter_context(nc.sbuf_tensor(name, shape, dt))
                    TC, NH = c["TC"], c["NH"]
                    S2("KsA", [128, 2 * TC], BF16)
                    S2("KsB", [128, 2 * TC], BF16)
                    S2("VsA", [128, 2 * TC], BF16)
                    S2("VsB", [128, 2 * TC], BF16)
                    S2("QsA", [128, TT], BF16)
                    S2("QsB", [128, TT], BF16)
                    S2("PT0", [128, TT], BF16)
                    S2("PT1", [128, TT], BF16)
                    S2("PT2", [128, TT], BF16)
                    S2("out8", [128, (NH // 2) * TT], BF16)
                    S2("rb_sb", [128, TT], F32)
                    S2("Mbig", [128, 3 * TT], BF16)
                    b.sb["gatebuf"] = b.sb["xtmp"]
                    b.emit_attention()
            b.emit_epilogue()
    return nc


def _slabify(a, inC, outC):
    return np.ascontiguousarray(
        a.reshape(inC, 128, outC, 128).transpose(2, 1, 0, 3).reshape(outC, 128, inC * 128))


def _shard_inputs(cfg, inputs):
    c = derive(cfg)
    TT, TC, D = c["TT"], c["TC"], c["D"]
    DC, EHC, SHC, NH, E = c["DC"], c["EHC"], c["SHC"], c["NH"], c["E"]
    nhh = NH // 2
    rows = snake_rows(TT)
    x = np.asarray(inputs["L0_attn_x"])
    maps = []
    common0 = {}
    common1 = {}
    common0["w_dec"] = np.ascontiguousarray(inputs["L0_attn_weight"])
    common0["b_dec"] = np.ascontiguousarray(inputs["L0_attn_bias"]).reshape(1, -1)
    common0["dvv"] = np.ascontiguousarray(inputs["L0_attn_decay_value"])
    common0["wr0"] = np.ascontiguousarray(np.asarray(inputs["L0_ffn_W_router"]).T)
    common0["wsg"] = _slabify(np.asarray(inputs["L0_ffn_Ws_gate"]), DC, SHC)
    common0["wsu"] = _slabify(np.asarray(inputs["L0_ffn_Ws_up"]), DC, SHC)
    common0["wsd"] = _slabify(np.asarray(inputs["L0_ffn_Ws_down"]), SHC, DC)
    common0["weg"] = np.stack([_slabify(np.asarray(inputs["L0_ffn_We_gate"][e]), DC, EHC)
                               for e in range(E)])
    common0["weu"] = np.stack([_slabify(np.asarray(inputs["L0_ffn_We_up"][e]), DC, EHC)
                               for e in range(E)])
    common0["wed"] = np.stack([_slabify(np.asarray(inputs["L0_ffn_We_down"][e]), EHC, DC)
                               for e in range(E)])
    qwT = np.asarray(inputs["L1_attn_q_w"]).T
    kwT = np.asarray(inputs["L1_attn_k_w"]).T
    vwT = np.asarray(inputs["L1_attn_v_w"]).T
    owT = np.asarray(inputs["L1_attn_o_w"]).T
    common1["qw"] = _slabify(qwT, DC, NH)
    common1["kw"] = _slabify(kwT, DC, NH)
    common1["vw"] = np.ascontiguousarray(
        vwT.reshape(2, DC // 2, 128, D // 512, 512).transpose(3, 0, 2, 1, 4)
        .reshape(D // 512, 2, 128, (DC // 2) * 512))
    common1["ow"] = np.ascontiguousarray(
        owT.reshape(2, nhh, 128, DC // 4, 512).transpose(0, 3, 2, 1, 4)
        .reshape(2, DC // 4, 128, nhh * 512))
    common1["gw"] = np.ascontiguousarray(np.asarray(inputs["L1_attn_gate_w"]).T)
    common1["gb"] = np.ascontiguousarray(inputs["L1_attn_gate_b"]).reshape(1, -1)
    common1["wr1"] = np.ascontiguousarray(np.asarray(inputs["L1_ffn_gate_w"]))
    gpT = np.asarray(inputs["L1_ffn_gate_proj"]).transpose(0, 2, 1)
    upT = np.asarray(inputs["L1_ffn_up_proj"]).transpose(0, 2, 1)
    dpT = np.asarray(inputs["L1_ffn_down_proj"]).transpose(0, 2, 1)
    common1["gp"] = np.stack([_slabify(gpT[e], DC, EHC) for e in range(E)])
    common1["up"] = np.stack([_slabify(upT[e], DC, EHC) for e in range(E)])
    common1["dp"] = np.stack([_slabify(dpT[e], EHC, DC) for e in range(E)])
    for core in range(8):
        bat, par = core // 2, core % 2
        xc = x[bat][rows[par], :].T
        xr = np.ascontiguousarray(
            xc.reshape(DC, 128, TC).transpose(1, 0, 2).reshape(128, DC * TC))
        maps.append((xr, dict(common0), dict(common1)))
    return maps, rows


def _gather_output(cfg, results, rows, out_name="out"):
    c = derive(cfg)
    D, TC, TT = c["D"], c["TC"], c["TT"]
    B = 4
    S = c["S"]
    DC = c["DC"]
    out = np.empty((B, S, D), dtype=np.float32)
    for core in range(8):
        bat, par = core // 2, core % 2
        oc = np.asarray(results[core][out_name])
        od = oc.reshape(128, DC, TC).transpose(1, 0, 2).reshape(D, TC)
        out[bat][rows[par], :] = od.T
    return out


def kernel(**inputs):
    cfg = TINY_CFG if os.environ.get("MK_TINY") else FULL_CFG
    mode = os.environ.get("MK_MODE", "fused")
    maps, rows = _shard_inputs(cfg, inputs)

    if mode == "fused":
        nc = build_program(cfg, layers=(0, 1))
        in_maps = []
        for xc, c0, c1 in maps:
            m = {"x_in": xc}
            m.update(c0)
            m.update(c1)
            in_maps.append(m)
        res = run_bass_kernel_spmd(nc, in_maps, core_ids=list(range(8)))
        return _gather_output(cfg, res.results, rows)

    nc0 = build_program(cfg, layers=(0,), in_name="x_in", out_name="out")
    in_maps0 = []
    for xc, c0, c1 in maps:
        m = {"x_in": xc}
        m.update(c0)
        in_maps0.append(m)
    res0 = run_bass_kernel_spmd(nc0, in_maps0, core_ids=list(range(8)))
    nc1 = build_program(cfg, layers=(1,), in_name="x_in", out_name="out")
    in_maps1 = []
    for i, (xc, c0, c1) in enumerate(maps):
        m = {"x_in": np.ascontiguousarray(np.asarray(res0.results[i]["out"]))}
        m.update(c1)
        in_maps1.append(m)
    res1 = run_bass_kernel_spmd(nc1, in_maps1, core_ids=list(range(8)))
    return _gather_output(cfg, res1.results, rows)


# version 10
# speedup vs baseline: 1.3358x; 1.0860x over previous
import os
import math
import contextlib
import numpy as np
import ml_dtypes
import concourse.bass as bass
import concourse.mybir as mybir
from concourse.bass_utils import run_bass_kernel_spmd

F32 = mybir.dt.float32
BF16 = mybir.dt.bfloat16
AF = mybir.ActivationFunctionType
ALU = mybir.AluOpType
AX = mybir.AxisListType

GROUPS = [[0, 1], [2, 3], [4, 5], [6, 7]]
EVEN_TILES = [0, 3, 4, 7]
ODD_TILES = [1, 2, 5, 6]
NEG_BIG = -1.0e30
EPS = 1e-5

FULL_CFG = dict(D=2048, DE=1408, DS=2816, E=8, HD=128, NH=16, TT=512)
TINY_CFG = dict(D=512, DE=256, DS=512, E=8, HD=128, NH=4, TT=128)
MIDI_CFG = dict(D=1024, DE=512, DS=1024, E=8, HD=128, NH=8, TT=256)


def derive(cfg):
    c = dict(cfg)
    c["DC"] = c["D"] // 128
    c["EHC"] = c["DE"] // 128
    c["SHC"] = c["DS"] // 128
    c["TC"] = 4 * c["TT"]
    c["S"] = 8 * c["TT"]
    c["CPT"] = c["TT"] // 128
    c["KCH"] = c["S"] // 128
    assert c["NH"] * c["HD"] == c["D"]
    return c


def snake_rows(TT):
    rows = {}
    for par, tiles in ((0, EVEN_TILES), (1, ODD_TILES)):
        r = []
        for t in tiles:
            r.extend(range(t * TT, (t + 1) * TT))
        rows[par] = np.array(r, dtype=np.int64)
    return rows


def bufcol_map(CPT):
    m = {}
    for par, tiles in ((0, EVEN_TILES), (1, ODD_TILES)):
        for i, t in enumerate(tiles):
            for j in range(CPT):
                m[t * CPT + j] = par * 4 * CPT + i * CPT + j
    return m


class Sem:
    def __init__(self, handle, name):
        self.h = handle
        self.name = name
        self.n = 0

    def inc(self, instr, k=1):
        instr.then_inc(self.h, k)
        self.n += k
        return self.n


class B:

    def __init__(self, nc, cfg, layers=(0, 1), in_name="x_in", out_name="out"):
        self.nc = nc
        self.c = derive(cfg)
        self.layers = layers
        self.in_name = in_name
        self.out_name = out_name
        self.es = contextlib.ExitStack()
        self.dram = {}
        self.sb = {}
        self.sems = {}
        self.cast_ready = {}
        self.bufcol = bufcol_map(self.c["CPT"])
        self.wguard = {"A": None, "B": None}
        self.next_slot = "A"
        self.xn_guard = None
        self.scope_guards = []
        self.h_ready = 0
        self.kvq_extra_guard = None

    def declare(self):
        nc, c = self.nc, self.c
        D, DE, DS, E = c["D"], c["DE"], c["DS"], c["E"]
        TC, HD, NH = c["TC"], c["HD"], c["NH"]

        def din(name, shape):
            self.dram[name] = nc.dram_tensor(name, shape, F32, kind="ExternalInput").ap()

        def dint(name, shape, dt=BF16):
            self.dram[name] = nc.dram_tensor(name, shape, dt).ap()

        DC, EHC, SHC, NH2 = c["DC"], c["EHC"], c["SHC"], c["NH"]
        nhh = NH2 // 2
        din(self.in_name, [128, c["DC"] * TC])
        shp = {
            "wsg": [SHC, 128, DC * 128], "wsu": [SHC, 128, DC * 128],
            "wsd": [DC, 128, SHC * 128],
            "weg": [E, EHC, 128, DC * 128], "weu": [E, EHC, 128, DC * 128],
            "wed": [E, DC, 128, EHC * 128],
            "qw": [NH2, 128, DC * 128], "kw": [NH2, 128, DC * 128],
            "vw": [D // 512, 2, 128, (DC // 2) * 512],
            "ow": [2, DC // 4, 128, nhh * 512],
            "gp": [E, EHC, 128, DC * 128], "up": [E, EHC, 128, DC * 128],
            "dp": [E, DC, 128, EHC * 128],
        }
        if 0 in self.layers:
            din("w_dec", [2, D]); din("b_dec", [1, D]); din("dvv", [2, 1])
            din("wr0", [E, D])
            for nm in ("wsg", "wsu", "wsd", "weg", "weu", "wed"):
                din(nm, shp[nm]); dint("c_" + nm, shp[nm])
            dint("c_wdec", [c["DC"] // 2, 128, c["DC"] * 256])
        if 1 in self.layers:
            din("gw", [HD, HD]); din("gb", [1, HD]); din("wr1", [E, D])
            for nm in ("qw", "kw", "vw", "ow", "gp", "up", "dp"):
                din(nm, shp[nm]); dint("c_" + nm, shp[nm])
            dint("kvloc", [2 * D, TC])
            dint("kvg", [4 * D, TC])
            dint("qloc", [D, TC])
        self.dram[self.out_name] = nc.dram_tensor(self.out_name, [128, c["DC"] * TC],
                                                  F32, kind="ExternalOutput").ap()

    def alloc_sems(self):
        names = ["dIn", "dWA", "dWB", "dWd", "dSc", "dGp", "dKtA", "dKtB",
                 "sPE", "sDVE", "sACT", "sGP", "sCC", "dOut", "sAH",
                 "dKsA", "dKsB", "dVsA", "dVsB", "dQsA", "dQsB"]
        if 0 in self.layers:
            names += ["cgws"] + [f"cgwe{e}" for e in range(self.c["E"])]
        if 1 in self.layers:
            names += ["cgqk"] + [f"cgl1e{e}" for e in range(self.c["E"])]
        for n in names:
            self.sems[n] = Sem(self.es.enter_context(self.nc.semaphore(n)), n)

    def alloc_psum(self):
        self.ps = [self.es.enter_context(self.nc.psum_tensor(f"ps{i}", [128, 512], F32))
                   for i in range(6)]
        self.pr = [self.es.enter_context(self.nc.psum_tensor(f"pr{i}", [1, 512], F32))
                   for i in range(2)]
        self.bank_guard = [None] * 6
        self.pr_guard = None
        self._rr = 0

    def alloc_persistent(self):
        nc, c = self.nc, self.c
        TT, DC = c["TT"], c["DC"]

        def S(name, shape, dt):
            self.sb[name] = self.es.enter_context(nc.sbuf_tensor(name, shape, dt))

        S("x_res", [128, DC * c["TC"]], F32)
        S("wA", [128, max(4096, 2 * DC * 128)], BF16)
        S("wB", [128, max(4096, 2 * DC * 128)], BF16)
        S("ident", [128, 128], F32)
        S("onesc", [128, 1], BF16)
        S("ones1", [1, 128], F32)
        S("selmat", [8, 8 * 128], BF16)
        S("wr0_sb", [128, DC * 8], BF16)
        S("wr1_sb", [128, DC * 8], BF16)
        S("gwT_sb", [128, 128], BF16)
        S("gbneg", [128, 1], F32)
        S("bias_sb", [128, DC], F32)
        S("rlog", [128, 4 * 8], F32)
        S("escr", [128, 16], F32)
        S("escr2", [128, 32], F32)
        S("e8", [128, 32], F32)
        S("r_a", [128, 1], F32)
        S("r_b", [128, 1], F32)
        S("r_c", [128, 1], F32)
        S("r_d", [128, 1], F32)
        S("cb_tm", [128, 4 * 8], F32)
        S("cb_fm", [8, TT], BF16)
        S("cb_b", [128, TT], BF16)
        S("xtmp", [128, TT], F32)
        S("rowA", [1, TT], F32)
        S("lnd0", [128, 1], F32)
        S("lnd1", [128, 1], F32)
        S("w0col", [128, DC], F32)
        S("w1col", [128, DC], F32)
        S("stg16", [16, 128], F32)
        S("identB", [128, 128], BF16)
        S("dvt", [2, 1], F32)

    def take_bank(self, t):
        i = self._rr
        self._rr = (self._rr + 1) % 6
        g = self.bank_guard[i]
        if g is not None:
            t.wait_ge(g[0].h, g[1])
        return i

    def slot(self):
        s = self.next_slot
        self.next_slot = "B" if s == "A" else "A"
        return s

    def load_slab(self, slot, src_ap, off=0, first=True, ready=None):
        sems = self.sems
        dW = sems["dWA" if slot == "A" else "dWB"]
        buf = self.sb["wA" if slot == "A" else "wB"]
        guard = self.wguard[slot] if first else None
        a, bcols = int(src_ap.shape[1]), int(src_ap.shape[2])
        dest = buf[:, off:off + a * bcols].rearrange("p (a b) -> p a b", b=bcols)

        @self.block.sync
        def _(sy):
            if guard is not None:
                sy.wait_ge(guard[0].h, guard[1])
            if ready is not None:
                sy.wait_ge(ready[0].h, ready[1])
            dW.inc(sy.dma_start(out=dest, in_=src_ap), 16)
        return buf, dW.n

    def xres(self, dc, t0, n):
        c = self.c
        return self.sb["x_res"][:, dc * c["TC"] + t0: dc * c["TC"] + t0 + n]

    def emit_prologue(self):
        nc, c, block = self.nc, self.c, self.block
        sems, sb, dram = self.sems, self.sb, self.dram
        D, DC, TC, E = c["D"], c["DC"], c["TC"], c["E"]

        @block.scalar
        def _(s):
            sems["dIn"].inc(s.dma_start(out=sb["x_res"][:], in_=dram[self.in_name]), 16)
        @block.sync
        def _(s):
            if 0 in self.layers:
                sems["dSc"].inc(s.dma_start(out=sb["dvt"][:], in_=dram["dvv"]), 16)
                s.wait_ge(sems["dSc"].h, sems["dSc"].n)

        @block.gpsimd
        def _(g):
            g.iota(sb["ident"][:], [[1, 128]], base=0, channel_multiplier=-1,
                   allow_small_or_imprecise_dtypes=True)
            sems["sGP"].inc(g.iota(sb["wA"][0:8, 0:8 * 128], [[1, 8], [0, 128]],
                                   base=0, channel_multiplier=-1,
                                   allow_small_or_imprecise_dtypes=True))
        @block.vector
        def _(v):
            v.wait_ge(sems["sGP"].h, sems["sGP"].n)
            v.tensor_scalar(sb["ident"][:], sb["ident"][:], 0.0, None, ALU.is_equal)
            v.tensor_scalar(sb["selmat"][:], sb["wA"][0:8, 0:8 * 128], 0.0, None, ALU.is_equal)
            v.memset(sb["onesc"][:], 1.0)
            v.memset(sb["ones1"][:], 1.0)
            v.drain()
            sems["sDVE"].inc(v.tensor_copy(sb["identB"][:], sb["ident"][:]))
        self.wguard["A"] = (sems["sDVE"], sems["sDVE"].n)
        self.wguard["B"] = (sems["sDVE"], sems["sDVE"].n)

        ps = self.ps
        stage_jobs = []
        if 0 in self.layers:
            stage_jobs.append((dram["b_dec"], "bias_sb"))
            stage_jobs.append((dram["w_dec"][0:1, :], "w0col"))
            stage_jobs.append((dram["w_dec"][1:2, :], "w1col"))
        for srow, dname in stage_jobs:
            @block.sync
            def _(s, srow=srow):
                s.wait_ge(sems["sPE"].h, sems["sPE"].n)
                sems["dSc"].inc(s.dma_start(
                    out=sb["stg16"][0:DC, :],
                    in_=srow.rearrange("o (dc p) -> (o dc) p", p=128)), 16)
                s.wait_ge(sems["dSc"].h, sems["dSc"].n)
            @block.tensor
            def _(t):
                t.wait_ge(sems["dSc"].h, sems["dSc"].n)
                t.wait_ge(sems["sDVE"].h, sems["sDVE"].n)
                sems["sPE"].inc(t.transpose(ps[0][:, 0:DC], sb["stg16"][0:DC, :],
                                            sb["ident"][0:DC, 0:DC]))
            @block.vector
            def _(v, dname=dname):
                v.wait_ge(sems["sPE"].h, sems["sPE"].n)
                v.drain()
                sems["sDVE"].inc(v.tensor_copy(sb[dname][:, 0:DC], ps[0][:, 0:DC]))

        for li, dname in ((0, "wr0"), (1, "wr1")):
            if li not in self.layers:
                continue
            for dc in range(DC):
                @block.sync
                def _(s, li=li, dc=dc):
                    s.wait_ge(sems["sPE"].h, sems["sPE"].n)
                    sems["dSc"].inc(s.dma_start(
                        out=sb["stg16"][0:8, :],
                        in_=dram["wr0" if li == 0 else "wr1"][:, dc * 128:(dc + 1) * 128]), 16)
                    s.wait_ge(sems["dSc"].h, sems["dSc"].n)
                @block.tensor
                def _(t, dc=dc):
                    t.wait_ge(sems["dSc"].h, sems["dSc"].n)
                    if dc == 0:
                        t.wait_ge(sems["sDVE"].h, sems["sDVE"].n)
                    sems["sPE"].inc(t.transpose(ps[1][:, dc * 8:(dc + 1) * 8],
                                                sb["stg16"][0:8, :],
                                                sb["ident"][0:8, 0:8]))
            @block.vector
            def _(v, dname=dname):
                v.wait_ge(sems["sPE"].h, sems["sPE"].n)
                v.drain()
                sems["sDVE"].inc(v.tensor_copy(sb[dname + "_sb"][:], ps[1][:, 0:DC * 8]))
        self.bank_guard[0] = (sems["sDVE"], sems["sDVE"].n)
        self.bank_guard[1] = (sems["sDVE"], sems["sDVE"].n)

        @block.sync
        def _(s):
            if 1 in self.layers:
                s.wait_ge(sems["sPE"].h, sems["sPE"].n)
                sems["dSc"].inc(s.dma_start(out=sb["xtmp"][:, 0:128], in_=dram["gw"]), 16)
                s.wait_ge(sems["dSc"].h, sems["dSc"].n)
                sems["dSc"].inc(s.dma_start(out=sb["stg16"][0:1, 0:128], in_=dram["gb"]), 16)
                s.wait_ge(sems["dSc"].h, sems["dSc"].n)
        @block.vector
        def _(v):
            if 1 in self.layers:
                v.wait_ge(sems["dSc"].h, sems["dSc"].n)
                v.drain()
                sems["sDVE"].inc(v.tensor_copy(sb["gwT_sb"][:], sb["xtmp"][:, 0:128]))
        @block.tensor
        def _(t):
            if 1 in self.layers:
                t.wait_ge(sems["dSc"].h, sems["dSc"].n)
                sems["sPE"].inc(t.transpose(ps[2][:, 0:1], sb["stg16"][0:1, 0:128],
                                            sb["ident"][0:1, 0:1]))
        @block.vector
        def _(v):
            if 1 in self.layers:
                v.wait_ge(sems["sPE"].h, sems["sPE"].n)
                v.drain()
                sems["sDVE"].inc(v.tensor_scalar(sb["gbneg"][:], ps[2][:, 0:1],
                                                 -1.0, None, ALU.mult))
        self.bank_guard[2] = (sems["sDVE"], sems["sDVE"].n)

        if 0 in self.layers:
            self.emit_wdecay()
        self.emit_casts()

    def emit_casts(self):
        c, block = self.c, self.block
        sems, dram = self.sems, self.dram
        E = c["E"]
        CH = 1 << 19

        def pieces(g, cg, dst, srcname):
            if "!" in srcname:
                nm, e = srcname.split("!")
                srcap = dram[nm][int(e)]
            else:
                srcap = dram[srcname]
            n = 1
            for d in srcap.shape:
                n *= d
            flat_src = srcap.rearrange(
                " ".join(f"d{i}" for i in range(len(srcap.shape))) + " -> ("
                + " ".join(f"d{i}" for i in range(len(srcap.shape))) + ")")
            flat_dst = dst.rearrange(
                " ".join(f"d{i}" for i in range(len(dst.shape))) + " -> ("
                + " ".join(f"d{i}" for i in range(len(dst.shape))) + ")")
            o = 0
            while o < n:
                m = min(CH, n - o)
                cg.inc(g.dma_start(out=flat_dst[o:o + m], in_=flat_src[o:o + m]), 16)
                g.wait_ge(cg.h, cg.n)
                o += m

        @block.gpsimd
        def _(g):
            g.wait_ge(sems["dIn"].h, 16)
            g.wait_ge(sems["dSc"].h, sems["dSc"].n)
            g.wait_ge(sems["dGp"].h, sems["dGp"].n)
            if 0 in self.layers:
                cg = sems["cgws"]
                for nm in ("wsg", "wsu", "wsd"):
                    pieces(g, cg, dram["c_" + nm], nm)
                self.cast_ready["ws"] = (cg, cg.n)
                for e in range(E):
                    cge = sems[f"cgwe{e}"]
                    for nm in ("weg", "weu", "wed"):
                        pieces(g, cge, dram["c_" + nm][e], nm + f"!{e}")
                    self.cast_ready[f"we{e}"] = (cge, cge.n)
            if 1 in self.layers:
                cg = sems["cgqk"]
                for nm in ("qw", "kw", "vw", "ow"):
                    pieces(g, cg, dram["c_" + nm], nm)
                self.cast_ready["qkvo"] = (cg, cg.n)
                for e in range(E):
                    cge = sems[f"cgl1e{e}"]
                    for nm in ("gp", "up", "dp"):
                        pieces(g, cge, dram["c_" + nm][e], nm + f"!{e}")
                    self.cast_ready[f"l1e{e}"] = (cge, cge.n)

    def emit_wdecay(self):
        nc, c, block = self.nc, self.c, self.block
        sems, sb, dram = self.sems, self.sb, self.dram
        D, DC = c["D"], c["DC"]
        ps, pr = self.ps, self.pr

        @block.vector
        def _(v):
            v.wait_ge(sems["dSc"].h, sems["dSc"].n)
            sems["sDVE"].inc(v.tensor_scalar(sb["dvt"][:], sb["dvt"][:], 0.9, 1.0,
                                             ALU.max, ALU.min))
        @block.scalar
        def _(s):
            s.wait_ge(sems["sDVE"].h, sems["sDVE"].n)
            sems["sACT"].inc(s.activation(sb["dvt"][:], sb["dvt"][:], AF.Ln))
        @block.sync
        def _(s):
            s.wait_ge(sems["sACT"].h, sems["sACT"].n)
            sems["dSc"].inc(s.dma_start(out=sb["r_a"][0:1, 0:1], in_=sb["dvt"][0:1, 0:1]), 16)
            s.wait_ge(sems["dSc"].h, sems["dSc"].n)
            sems["dSc"].inc(s.dma_start(out=sb["r_b"][0:1, 0:1], in_=sb["dvt"][1:2, 0:1]), 16)
            s.wait_ge(sems["dSc"].h, sems["dSc"].n)
        @block.tensor
        def _(t):
            t.wait_ge(sems["dSc"].h, sems["dSc"].n)
            t.wait_ge(sems["sDVE"].h, sems["sDVE"].n)
            t.matmul(ps[0][:, 0:1], sb["ones1"][0:1, :], sb["r_a"][0:1, 0:1],
                     start=True, stop=True)
            sems["sPE"].inc(t.matmul(ps[1][:, 0:1], sb["ones1"][0:1, :],
                                     sb["r_b"][0:1, 0:1], start=True, stop=True))
        @block.vector
        def _(v):
            v.wait_ge(sems["sPE"].h, sems["sPE"].n)
            v.tensor_copy(sb["lnd0"][:], ps[0][:, 0:1])
            sems["sDVE"].inc(v.tensor_copy(sb["lnd1"][:], ps[1][:, 0:1]))

        with contextlib.ExitStack() as scope:
            vrow = scope.enter_context(nc.sbuf_tensor("wd_v", [1, D], F32))
            r0 = scope.enter_context(nc.sbuf_tensor("wd_r0", [1, D], F32))
            r1 = scope.enter_context(nc.sbuf_tensor("wd_r1", [1, D], F32))
            w1r = scope.enter_context(nc.sbuf_tensor("wd_w1r", [1, D], F32))
            mbig0 = scope.enter_context(nc.sbuf_tensor("wd_m", [128, 2 * D], BF16))
            wbfF = scope.enter_context(nc.sbuf_tensor("wd_bf", [128, D], BF16))
            ecol0 = scope.enter_context(nc.sbuf_tensor("wd_e0", [128, 1], F32))
            ecol1 = scope.enter_context(nc.sbuf_tensor("wd_e1", [128, 1], F32))
            cA = scope.enter_context(nc.sbuf_tensor("wd_ca", [128, 1], F32))
            c0T = scope.enter_context(nc.sbuf_tensor("wd_c0t", [1, 128], F32))
            c1T = scope.enter_context(nc.sbuf_tensor("wd_c1t", [1, 128], F32))

            @block.vector
            def _(v):
                v.drain()
                sems["sDVE"].inc(v.memset(mbig0[:], 1.0))
            @block.gpsimd
            def _(g):
                g.wait_ge(sems["sDVE"].h, sems["sDVE"].n)
                g.affine_select(mbig0[:], mbig0[:], [[1, 2 * D]], ALU.is_ge, 0.0,
                                base=-D, channel_multiplier=-1)
                sems["sGP"].inc(g.iota(ecol0[:], [[1, 1]], base=0, channel_multiplier=-1,
                                       allow_small_or_imprecise_dtypes=True))
            @block.sync
            def _(s):
                sems["dSc"].inc(s.dma_start(out=w1r[0:1, :], in_=dram["w_dec"][1:2, :]), 16)
                s.wait_ge(sems["dSc"].h, sems["dSc"].n)
            w1r_cnt = sems["dSc"].n
            @block.vector
            def _(v):
                v.wait_ge(sems["sGP"].h, sems["sGP"].n)
                v.drain()
                v.tensor_scalar(ecol1[:], ecol0[:], sb["lnd1"][:, 0:1], None, ALU.mult)
                v.drain()
                sems["sDVE"].inc(v.tensor_scalar(ecol0[:], ecol0[:], sb["lnd0"][:, 0:1],
                                                 None, ALU.mult))
            @block.scalar
            def _(s):
                s.wait_ge(sems["sDVE"].h, sems["sDVE"].n)
                s.drain()
                s.activation(ecol0[:], ecol0[:], AF.Exp)
                s.drain()
                sems["sACT"].inc(s.activation(ecol1[:], ecol1[:], AF.Exp))
            @block.tensor
            def _(t):
                t.wait_ge(sems["sACT"].h, sems["sACT"].n)
                sems["sPE"].inc(t.transpose(ps[0][0:1, 0:128], ecol1[:],
                                            sb["ident"][:]))
            @block.vector
            def _(v):
                v.wait_ge(sems["sPE"].h, sems["sPE"].n)
                v.drain()
                sems["sDVE"].inc(v.tensor_copy(c1T[0:1, :], ps[0][0:1, 0:128]))

            nwin = D // 512
            for kc in range(DC):
                @block.vector
                def _(v, kc=kc):
                    v.wait_ge(sems["sPE"].h, sems["sPE"].n)
                    v.drain()
                    sems["sDVE"].inc(v.tensor_scalar(cA[:], ecol0[:],
                                                     sb["w0col"][:, kc:kc + 1],
                                                     None, ALU.mult))
                @block.gpsimd
                def _(g, kc=kc):
                    g.wait_ge(sems["sDVE"].h, sems["sDVE"].n)
                    sems["sGP"].inc(g.iota(vrow[0:1, :], [[1, D]], base=-128 * kc,
                                           channel_multiplier=0,
                                           allow_small_or_imprecise_dtypes=True))
                @block.tensor
                def _(t, kc=kc):
                    t.wait_ge(sems["sDVE"].h, sems["sDVE"].n)
                    sems["sPE"].inc(t.transpose(ps[0][0:1, 0:128], cA[:],
                                                sb["ident"][:]))
                @block.vector
                def _(v, kc=kc):
                    v.wait_ge(sems["sPE"].h, sems["sPE"].n)
                    v.wait_ge(sems["sGP"].h, sems["sGP"].n)
                    v.drain()
                    v.tensor_copy(c0T[0:1, :], ps[0][0:1, 0:128])
                    v.tensor_scalar(vrow[0:1, :], vrow[0:1, :], 0.0, None, ALU.max)
                    v.drain()
                    v.tensor_scalar(r0[0:1, :], vrow[0:1, :], sb["lnd0"][0:1, 0:1],
                                    None, ALU.mult)
                    v.drain()
                    sems["sDVE"].inc(v.tensor_scalar(r1[0:1, :], vrow[0:1, :],
                                                     sb["lnd1"][0:1, 0:1], None, ALU.mult))
                @block.scalar
                def _(s, kc=kc):
                    s.wait_ge(sems["sDVE"].h, sems["sDVE"].n)
                    s.drain()
                    s.activation(r0[0:1, :], r0[0:1, :], AF.Exp)
                    s.drain()
                    sems["sACT"].inc(s.activation(r1[0:1, :], r1[0:1, :], AF.Exp))
                @block.vector
                def _(v, kc=kc):
                    v.wait_ge(sems["sACT"].h, sems["sACT"].n)
                    v.wait_ge(sems["dSc"].h, w1r_cnt)
                    v.drain()
                    sems["sDVE"].inc(v.tensor_tensor(r1[0:1, :], r1[0:1, :],
                                                     w1r[0:1, :], ALU.mult))
                pe_w = sems["sPE"].n
                dve_w = sems["sDVE"].n
                @block.tensor
                def _(t, kc=kc, pe_w=pe_w, dve_w=dve_w):
                    t.wait_ge(sems["sDVE"].h, dve_w)
                    for win in range(nwin):
                        if win >= 2:
                            t.wait_ge(sems["sDVE"].h, dve_w + win - 1)
                        bkw = 2 + win % 2
                        t.matmul(ps[bkw][:, 0:512], c0T[0:1, :],
                                 r0[0:1, win * 512:(win + 1) * 512],
                                 start=True, stop=False)
                        t.matmul(ps[bkw][:, 0:512], c1T[0:1, :],
                                 r1[0:1, win * 512:(win + 1) * 512],
                                 start=False, stop=True).then_inc(sems["sPE"].h, 1)
                sems["sPE"].n = pe_w + nwin
                dwd_prev = sems["dWd"].n
                @block.vector
                def _(v, kc=kc, pe_w=pe_w, dwd_prev=dwd_prev):
                    v.wait_ge(sems["dWd"].h, dwd_prev)
                    for win in range(nwin):
                        v.wait_ge(sems["sPE"].h, pe_w + win + 1)
                        v.drain()
                        v.tensor_tensor(
                            wbfF[:, win * 512:(win + 1) * 512],
                            ps[2 + win % 2][:, 0:512],
                            mbig0[:, D - 128 * kc + win * 512: D - 128 * kc + (win + 1) * 512],
                            ALU.mult).then_inc(sems["sDVE"].h, 1)
                sems["sDVE"].n = dve_w + nwin
                @block.gpsimd
                def _(sy, kc=kc):
                    sy.wait_ge(sems["sDVE"].h, sems["sDVE"].n)
                    sems["dWd"].inc(sy.dma_start(
                        out=dram["c_wdec"][:, :, kc * 256:(kc + 1) * 256].rearrange(
                            "dcp p j -> p dcp j"),
                        in_=wbfF[:].rearrange("p (dcp j) -> p dcp j", j=256)), 16)
            self.cast_ready["wdec"] = (sems["dWd"], sems["dWd"].n)
            self.scope_guards.append((sems["dWd"], sems["dWd"].n))
            self.scope_guards.append((sems["sDVE"], sems["sDVE"].n))

    def emit_stats_xn(self, tile, extra_guards=()):
        c, block = self.c, self.block
        sems, sb = self.sems, self.sb
        DC, TT, D = c["DC"], c["TT"], c["D"]
        t0 = tile * TT
        ps, pr = self.ps, self.pr
        xnb = sb["xnb"]
        xsq = sb["H"]

        @block.vector
        def _(v):
            v.wait_ge(sems["dIn"].h, 16)
            if self.xn_guard is not None:
                v.wait_ge(self.xn_guard[0].h, self.xn_guard[1])
            for g in extra_guards:
                v.wait_ge(g[0].h, g[1])
            v.drain()
            for dc in range(DC):
                v.tensor_copy(xnb[:, dc * TT:(dc + 1) * TT], self.xres(dc, t0, TT))
            for dc in range(DC - 1):
                v.tensor_tensor(xsq[:, dc * TT:(dc + 1) * TT],
                                self.xres(dc, t0, TT), self.xres(dc, t0, TT), ALU.mult)
            dc = DC - 1
            sems["sDVE"].inc(v.tensor_tensor(xsq[:, dc * TT:(dc + 1) * TT],
                                             self.xres(dc, t0, TT),
                                             self.xres(dc, t0, TT), ALU.mult))
        @block.tensor
        def _(t):
            t.wait_ge(sems["sDVE"].h, sems["sDVE"].n)
            if self.pr_guard is not None:
                t.wait_ge(self.pr_guard[0].h, self.pr_guard[1])
            for dc in range(DC):
                t.matmul(pr[0][0:1, 0:TT], sb["onesc"][:, 0:1],
                         xnb[:, dc * TT:(dc + 1) * TT],
                         start=(dc == 0), stop=(dc == DC - 1))
            for dc in range(DC):
                last = t.matmul(pr[1][0:1, 0:TT], sb["onesc"][:, 0:1],
                                xsq[:, dc * TT:(dc + 1) * TT],
                                start=(dc == 0), stop=(dc == DC - 1))
            sems["sPE"].inc(last)
        @block.vector
        def _(v):
            v.wait_ge(sems["sPE"].h, sems["sPE"].n)
            v.drain()
            v.tensor_scalar(sb["rowA"][0:1, 0:TT], pr[0][0:1, 0:TT], 1.0 / D, None, ALU.mult)
            v.tensor_scalar(sb["rowB"][0:1, 0:TT], pr[1][0:1, 0:TT], 1.0 / D, None, ALU.mult)
            v.drain()
            v.tensor_tensor(sb["rowC"][0:1, 0:TT], sb["rowA"][0:1, 0:TT],
                            sb["rowA"][0:1, 0:TT], ALU.mult)
            v.drain()
            v.tensor_tensor(sb["rowB"][0:1, 0:TT], sb["rowB"][0:1, 0:TT],
                            sb["rowC"][0:1, 0:TT], ALU.subtract)
            v.drain()
            sems["sDVE"].inc(v.tensor_scalar(sb["rowB"][0:1, 0:TT], sb["rowB"][0:1, 0:TT],
                                             EPS, None, ALU.add))
        self.pr_guard = (sems["sDVE"], sems["sDVE"].n)
        @block.scalar
        def _(s):
            s.wait_ge(sems["sDVE"].h, sems["sDVE"].n)
            s.drain()
            s.activation(sb["rowB"][0:1, 0:TT], sb["rowB"][0:1, 0:TT], AF.Ln)
            s.drain()
            sems["sACT"].inc(s.activation(sb["rowB"][0:1, 0:TT], sb["rowB"][0:1, 0:TT],
                                          AF.Exp, scale=-0.5))
        @block.tensor
        def _(t):
            t.wait_ge(sems["sACT"].h, sems["sACT"].n)
            bm = self.take_bank(t)
            br = self.take_bank(t)
            self._mb, self._rb = bm, br
            t.matmul(ps[bm][:, 0:TT], sb["ones1"][0:1, :], sb["rowA"][0:1, 0:TT],
                     start=True, stop=True)
            sems["sPE"].inc(t.matmul(ps[br][:, 0:TT], sb["ones1"][0:1, :],
                                     sb["rowB"][0:1, 0:TT], start=True, stop=True))
        @block.vector
        def _(v):
            v.wait_ge(sems["sPE"].h, sems["sPE"].n)
            v.drain()
            v.tensor_copy(sb["mr_b"][:, 0:TT], ps[self._mb][:, 0:TT])
            sems["sDVE"].inc(v.tensor_copy(sb["mr_b"][:, TT:2 * TT], ps[self._rb][:, 0:TT]))
        self.bank_guard[self._mb] = (sems["sDVE"], sems["sDVE"].n)
        self.bank_guard[self._rb] = (sems["sDVE"], sems["sDVE"].n)
        self.xn_chunk = {}
        @block.vector
        def _(v):
            v.drain()
            for dc in range(DC):
                v.tensor_tensor(sb["xtmp"][:, 0:TT], self.xres(dc, t0, TT),
                                sb["mr_b"][:, 0:TT], ALU.subtract)
                v.drain()
                v.tensor_tensor(xnb[:, dc * TT:(dc + 1) * TT], sb["xtmp"][:, 0:TT],
                                sb["mr_b"][:, TT:2 * TT], ALU.mult
                                ).then_inc(sems["sDVE"].h, 1)
                sems["sDVE"].n += 0
                self.xn_chunk[dc] = sems["sDVE"].n + dc + 1
                v.drain()
        sems["sDVE"].n += DC
        self.xn_ready = sems["sDVE"].n

    def emit_l0_attn(self, tile):
        c, block = self.c, self.block
        sems, sb, dram = self.sems, self.sb, self.dram
        DC, TT, TC = c["DC"], c["TT"], c["TC"]
        t0 = tile * TT
        ps = self.ps
        wdec_ready = self.cast_ready["wdec"]

        for dcp in range(DC // 2):
            slot = self.slot()
            src = dram["c_wdec"][dcp].rearrange("p (kc j) -> p kc j", j=256)
            buf, wcnt = self.load_slab(slot, src, ready=wdec_ready)
            for sub in range(2):
                dc = dcp * 2 + sub
                @block.tensor
                def _(t, dc=dc, sub=sub, buf=buf, wcnt=wcnt, slot=slot):
                    dW = sems["dWA" if slot == "A" else "dWB"]
                    t.wait_ge(dW.h, wcnt)
                    t.wait_ge(sems["sDVE"].h, self.xn_ready)
                    bk = self.take_bank(t)
                    self._bk = bk
                    for kc in range(DC):
                        last = t.matmul(ps[bk][:, 0:TT],
                                        buf[:, kc * 256 + sub * 128: kc * 256 + sub * 128 + 128],
                                        sb["xnb"][:, kc * TT:(kc + 1) * TT],
                                        start=(kc == 0), stop=(kc == DC - 1))
                    sems["sPE"].inc(last)
                self.wguard[slot] = (sems["sPE"], sems["sPE"].n)
                @block.vector
                def _(v, dc=dc):
                    v.wait_ge(sems["sPE"].h, sems["sPE"].n)
                    v.drain()
                    sems["sDVE"].inc(v.scalar_tensor_tensor(
                        self.xres(dc, t0, TT), ps[self._bk][:, 0:TT],
                        sb["bias_sb"][:, dc:dc + 1], self.xres(dc, t0, TT),
                        ALU.add, ALU.add))
                self.bank_guard[self._bk] = (sems["sDVE"], sems["sDVE"].n)

    def emit_router(self, tile, layer):
        c, block = self.c, self.block
        sems, sb = self.sems, self.sb
        DC, TT, CPT = c["DC"], c["TT"], c["CPT"]
        ps = self.ps
        wr = sb["wr0_sb"] if layer == 0 else sb["wr1_sb"]

        @block.tensor
        def _(t):
            rb = self.take_bank(t)
            self._rbank = rb
            for tsub in range(CPT):
                for dc in range(DC):
                    if tsub == 0:
                        t.wait_ge(sems["sDVE"].h, self.xn_chunk[dc])
                    last = t.matmul(ps[rb][:, tsub * 8:(tsub + 1) * 8],
                                    sb["xnb"][:, dc * TT + tsub * 128: dc * TT + tsub * 128 + 128],
                                    wr[:, dc * 8:(dc + 1) * 8],
                                    start=(dc == 0), stop=(dc == DC - 1))
            sems["sPE"].inc(last)
        @block.vector
        def _(v):
            v.wait_ge(sems["sPE"].h, sems["sPE"].n)
            v.drain()
            sems["sDVE"].inc(v.tensor_copy(sb["rlog"][:, 0:CPT * 8], ps[self._rbank][:, 0:CPT * 8]))
        self.bank_guard[self._rbank] = (sems["sDVE"], sems["sDVE"].n)

        rl = sb["rlog"][:, 0:CPT * 8].rearrange("p (t e) -> p t e", e=8)
        cbv = sb["cb_tm"][:, 0:CPT * 8].rearrange("p (t e) -> p t e", e=8)
        e8v = sb["e8"][:, 0:CPT * 8].rearrange("p (t e) -> p t e", e=8)
        eqv = sb["escr2"][:, 0:CPT * 8].rearrange("p (t e) -> p t e", e=8)
        mx = sb["escr"][:, 0:CPT].rearrange("p (t o) -> p t o", o=1)
        mx2 = sb["escr"][:, CPT:2 * CPT].rearrange("p (t o) -> p t o", o=1)
        if layer == 0:
            @block.scalar
            def _(s):
                s.wait_ge(sems["sDVE"].h, sems["sDVE"].n)
                s.drain()
                sems["sACT"].inc(s.activation(sb["rlog"][:, 0:CPT * 8],
                                              sb["rlog"][:, 0:CPT * 8], AF.Exp))
            @block.vector
            def _(v):
                v.wait_ge(sems["sACT"].h, sems["sACT"].n)
                v.drain()
                v.tensor_reduce(sb["escr"][:, 0:CPT], rl, AX.X, ALU.add)
                v.drain()
                v.reciprocal(sb["escr"][:, 0:CPT], sb["escr"][:, 0:CPT])
                v.drain()
                v.tensor_tensor(rl, rl, bass.broadcast_tensor_aps(rl, mx)[1], ALU.mult)
                v.drain()
                v.reduce_max(sb["escr"][:, 0:CPT], rl, AX.X)
                v.drain()
                v.tensor_tensor(eqv, rl, bass.broadcast_tensor_aps(rl, mx)[1], ALU.is_equal)
                v.drain()
                v.scalar_tensor_tensor(e8v, eqv, NEG_BIG, rl, ALU.mult, ALU.add)
                v.drain()
                v.reduce_max(sb["escr"][:, CPT:2 * CPT], e8v, AX.X)
                v.drain()
                v.tensor_tensor(e8v, rl, bass.broadcast_tensor_aps(rl, mx2)[1], ALU.is_ge)
                v.drain()
                sems["sDVE"].inc(v.tensor_tensor(cbv, rl, e8v, ALU.mult))
        else:
            @block.vector
            def _(v):
                v.drain()
                v.reduce_max(sb["escr"][:, 0:CPT], rl, AX.X)
                v.drain()
                v.tensor_tensor(eqv, rl, bass.broadcast_tensor_aps(rl, mx)[1], ALU.is_equal)
                v.drain()
                v.scalar_tensor_tensor(e8v, eqv, NEG_BIG, rl, ALU.mult, ALU.add)
                v.drain()
                v.reduce_max(sb["escr"][:, CPT:2 * CPT], e8v, AX.X)
                v.drain()
                v.tensor_tensor(e8v, rl, bass.broadcast_tensor_aps(rl, mx2)[1], ALU.is_ge)
                v.drain()
                sems["sDVE"].inc(v.tensor_tensor(sb["escr"][:, 2 * CPT:3 * CPT],
                                                 sb["escr"][:, 0:CPT],
                                                 sb["escr"][:, CPT:2 * CPT], ALU.subtract))
            @block.scalar
            def _(s):
                s.wait_ge(sems["sDVE"].h, sems["sDVE"].n)
                s.drain()
                sems["sACT"].inc(s.activation(sb["escr"][:, 2 * CPT:3 * CPT],
                                              sb["escr"][:, 2 * CPT:3 * CPT], AF.Exp,
                                              scale=-1.0))
            @block.vector
            def _(v):
                v.wait_ge(sems["sACT"].h, sems["sACT"].n)
                v.drain()
                v.tensor_scalar(sb["escr"][:, 2 * CPT:3 * CPT],
                                sb["escr"][:, 2 * CPT:3 * CPT], 1.0, None, ALU.add)
                v.drain()
                v.reciprocal(sb["escr"][:, 2 * CPT:3 * CPT],
                             sb["escr"][:, 2 * CPT:3 * CPT])
                v.drain()
                s12 = sb["escr"][:, 2 * CPT:3 * CPT].rearrange("p (t o) -> p t o", o=1)
                v.tensor_tensor(e8v, rl, bass.broadcast_tensor_aps(rl, mx2)[1], ALU.is_equal)
                v.drain()
                v.tensor_tensor(eqv, eqv, bass.broadcast_tensor_aps(eqv, s12)[1], ALU.mult)
                v.drain()
                v.tensor_scalar(sb["escr"][:, 2 * CPT:3 * CPT],
                                sb["escr"][:, 2 * CPT:3 * CPT], -1.0, 1.0, ALU.mult, ALU.add)
                v.drain()
                v.tensor_tensor(e8v, e8v, bass.broadcast_tensor_aps(e8v, s12)[1], ALU.mult)
                v.drain()
                sems["sDVE"].inc(v.tensor_tensor(cbv, eqv, e8v, ALU.add))
        @block.tensor
        def _(t):
            t.wait_ge(sems["sDVE"].h, sems["sDVE"].n)
            cbk = self.take_bank(t)
            self._cbk = cbk
            for tsub in range(CPT):
                last = t.transpose(ps[cbk][0:8, tsub * 128:(tsub + 1) * 128],
                                   sb["cb_tm"][:, tsub * 8:(tsub + 1) * 8], sb["ident"][:])
            sems["sPE"].inc(last)
        @block.vector
        def _(v):
            v.wait_ge(sems["sPE"].h, sems["sPE"].n)
            v.drain()
            sems["sDVE"].inc(v.tensor_copy(sb["cb_fm"][0:8, 0:TT],
                                           ps[self._cbk][0:8, 0:CPT * 128]))
        self.bank_guard[self._cbk] = (sems["sDVE"], sems["sDVE"].n)
        self.cb_ready = sems["sDVE"].n

    def emit_cb_broadcast(self, e):
        sems, sb, block = self.sems, self.sb, self.block
        TT = self.c["TT"]
        ps = self.ps

        @block.tensor
        def _(t):
            t.wait_ge(sems["sDVE"].h, self.cb_ready)
            bk = self.take_bank(t)
            self._cbb = bk
            sems["sPE"].inc(t.matmul(ps[bk][:, 0:TT],
                                     sb["selmat"][0:8, e * 128:(e + 1) * 128],
                                     sb["cb_fm"][0:8, 0:TT], start=True, stop=True))
        @block.vector
        def _(v):
            v.wait_ge(sems["sPE"].h, sems["sPE"].n)
            v.drain()
            sems["sDVE"].inc(v.tensor_copy(sb["cb_b"][:, 0:TT], ps[self._cbb][:, 0:TT]))
        self.bank_guard[self._cbb] = (sems["sDVE"], sems["sDVE"].n)
        self.cbb_ready = sems["sDVE"].n

    def emit_gate_up_down(self, tile, gsrc, usrc, dsrc, nhc, with_cb, ready):
        c, block = self.c, self.block
        sems, sb = self.sems, self.sb
        DC, TT, TC = c["DC"], c["TT"], c["TC"]
        t0 = tile * TT
        ps = self.ps
        H = sb["H"]
        pe0, act0, dve0 = sems["sPE"].n, sems["sACT"].n, sems["sDVE"].n

        wcnt = {}
        slots = {}
        for hc in range(nhc):
            slot = self.slot()
            slots[hc] = slot
            gs = gsrc[hc].rearrange("p (dc j) -> p dc j", j=128)
            us = usrc[hc].rearrange("p (dc j) -> p dc j", j=128)
            self.load_slab(slot, gs, off=0, ready=ready if hc == 0 else None)
            _, wcnt[hc] = self.load_slab(slot, us, off=DC * 128, first=False)
            self.wguard[slot] = (sems["sPE"], pe0 + hc + 1)

        banks = {}
        @block.tensor
        def _(t):
            for hc in range(nhc):
                slot = slots[hc]
                buf = sb["wA" if slot == "A" else "wB"]
                t.wait_ge(sems["dWA" if slot == "A" else "dWB"].h, wcnt[hc])
                bg = self.take_bank(t)
                bu = self.take_bank(t)
                banks[hc] = (bg, bu)
                for dc in range(DC):
                    if hc == 0:
                        t.wait_ge(sems["sDVE"].h, self.xn_chunk[dc])
                    t.matmul(ps[bg][:, 0:TT], buf[:, dc * 128:(dc + 1) * 128],
                             sb["xnb"][:, dc * TT:(dc + 1) * TT],
                             start=(dc == 0), stop=(dc == DC - 1))
                for dc in range(DC):
                    last = t.matmul(ps[bu][:, 0:TT],
                                    buf[:, DC * 128 + dc * 128: DC * 128 + (dc + 1) * 128],
                                    sb["xnb"][:, dc * TT:(dc + 1) * TT],
                                    start=(dc == 0), stop=(dc == DC - 1))
                last.then_inc(sems["sPE"].h, 1)
                self.bank_guard[bg] = (sems["sACT"], act0 + hc + 1)
                self.bank_guard[bu] = (sems["sDVE"], dve0 + 2 * (hc + 1))
        sems["sPE"].n = pe0 + nhc

        @block.scalar
        def _(s):
            for hc in range(nhc):
                s.wait_ge(sems["sPE"].h, pe0 + hc + 1)
                s.drain()
                s.activation(H[:, hc * TT:(hc + 1) * TT], ps[banks[hc][0]][:, 0:TT],
                             AF.Silu).then_inc(sems["sACT"].h, 1)
        sems["sACT"].n = act0 + nhc

        @block.vector
        def _(v):
            if with_cb:
                v.wait_ge(sems["sDVE"].h, self.cbb_ready)
            for hc in range(nhc):
                v.wait_ge(sems["sPE"].h, pe0 + hc + 1)
                v.wait_ge(sems["sACT"].h, act0 + hc + 1)
                v.drain()
                if with_cb:
                    v.tensor_tensor(sb["ue"][:, 0:TT], ps[banks[hc][1]][:, 0:TT],
                                    sb["cb_b"][:, 0:TT], ALU.mult
                                    ).then_inc(sems["sDVE"].h, 1)
                else:
                    v.tensor_copy(sb["ue"][:, 0:TT], ps[banks[hc][1]][:, 0:TT]
                                  ).then_inc(sems["sDVE"].h, 1)
                v.drain()
                v.tensor_tensor(H[:, hc * TT:(hc + 1) * TT],
                                H[:, hc * TT:(hc + 1) * TT],
                                sb["ue"][:, 0:TT], ALU.mult).then_inc(sems["sDVE"].h, 1)
        sems["sDVE"].n = dve0 + 2 * nhc
        self.h_ready = sems["sDVE"].n

        pe1, dve1 = sems["sPE"].n, sems["sDVE"].n
        pair = (2 * nhc * 128 <= 4096)
        step = 2 if pair else 1
        dwcnt = {}
        dslots = {}
        for dc0 in range(0, DC, step):
            slot = self.slot()
            dsl = dsrc[dc0:dc0 + step].rearrange("s p c -> p s c")
            _, wc = self.load_slab(slot, dsl)
            for dc in range(dc0, dc0 + step):
                dslots[dc] = (slot, dc - dc0)
                dwcnt[dc] = wc
            self.wguard[slot] = (sems["sPE"], pe1 + dc0 + step)
        dbanks = {}
        @block.tensor
        def _(t):
            t.wait_ge(sems["sDVE"].h, self.h_ready)
            for dc in range(DC):
                slot, sub = dslots[dc]
                buf = sb["wA" if slot == "A" else "wB"]
                t.wait_ge(sems["dWA" if slot == "A" else "dWB"].h, dwcnt[dc])
                bk = self.take_bank(t)
                dbanks[dc] = bk
                for hc in range(nhc):
                    last = t.matmul(ps[bk][:, 0:TT],
                                    buf[:, (sub * nhc + hc) * 128:(sub * nhc + hc + 1) * 128],
                                    H[:, hc * TT:(hc + 1) * TT],
                                    start=(hc == 0), stop=(hc == nhc - 1))
                last.then_inc(sems["sPE"].h, 1)
                self.bank_guard[bk] = (sems["sDVE"], dve1 + dc + 1)
        sems["sPE"].n = pe1 + DC
        @block.vector
        def _(v):
            for dc in range(DC):
                v.wait_ge(sems["sPE"].h, pe1 + dc + 1)
                v.drain()
                v.scalar_tensor_tensor(
                    self.xres(dc, t0, TT), ps[dbanks[dc]][:, 0:TT], 0.0,
                    self.xres(dc, t0, TT), ALU.add, ALU.add).then_inc(sems["sDVE"].h, 1)
        sems["sDVE"].n = dve1 + DC

    def emit_l0_tile(self, tile, guards=()):
        c, dram = self.c, self.dram
        self.emit_stats_xn(tile, extra_guards=guards)
        self.emit_router(tile, layer=0)
        if tile == 0:
            self.emit_l0_attn(tile)
        self.emit_gate_up_down(tile, dram["c_wsg"], dram["c_wsu"], dram["c_wsd"],
                               c["SHC"], with_cb=False, ready=self.cast_ready["ws"])
        for e in range(c["E"]):
            self.emit_cb_broadcast(e)
            self.emit_gate_up_down(tile, dram["c_weg"][e], dram["c_weu"][e],
                                   dram["c_wed"][e], c["EHC"], with_cb=True,
                                   ready=self.cast_ready[f"we{e}"])
        if tile != 0:
            self.emit_l0_attn(tile)
        self.xn_guard = (self.sems["sPE"], self.sems["sPE"].n)

    def emit_kvq_tile(self, tile):
        c, block = self.c, self.block
        sems, sb, dram = self.sems, self.sb, self.dram
        DC, TT, TC, NH, CPT, D = c["DC"], c["TT"], c["TC"], c["NH"], c["CPT"], c["D"]
        ps = self.ps
        qk_ready = self.cast_ready["qkvo"]
        nh_half = DC // 2
        nhb = 512 // 128

        units = []
        for h in range(NH):
            units.append(("K", h))
            units.append(("Q", h))
        for eb in range(D // 512):
            for tsub in range(CPT):
                units.append(("V", (eb, tsub)))

        pe0 = sems["sPE"].n
        wcnt = {}
        uslots = {}
        vslabs = {}
        ui = 0
        for kind, arg in units:
            if kind in ("K", "Q"):
                slot = self.slot()
                uslots[ui] = slot
                src = dram["c_kw" if kind == "K" else "c_qw"][arg].rearrange(
                    "p (dc j) -> p dc j", j=128)
                _, wcnt[ui] = self.load_slab(
                    slot, src, ready=qk_ready if (ui == 0 and tile == 0) else None)
                self.wguard[slot] = (sems["sPE"], pe0 + ui + 1)
            else:
                eb, tsub = arg
                if tsub == 0:
                    slot = self.slot()
                    srcA = dram["c_vw"][eb, 0].rearrange("p (di e) -> p di e", e=512)
                    _, wA_c = self.load_slab(slot, srcA)
                    slotB = self.slot()
                    srcB = dram["c_vw"][eb, 1].rearrange("p (di e) -> p di e", e=512)
                    _, wB_c = self.load_slab(slotB, srcB)
                    vslabs[eb] = (slot, slotB, wA_c, wB_c)
                    self.wguard[slot] = (sems["sPE"], pe0 + ui + CPT)
                    self.wguard[slotB] = (sems["sPE"], pe0 + ui + CPT)
                uslots[ui] = vslabs[eb]
            ui += 1

        @block.tensor
        def _(t):
            ui = 0
            for kind, arg in units:
                bk = self.take_bank(t)
                self._kv_banks = getattr(self, "_kv_banks", {})
                self._kv_banks[ui] = bk
                if kind in ("K", "Q"):
                    slot = uslots[ui]
                    buf = sb["wA" if slot == "A" else "wB"]
                    t.wait_ge(sems["dWA" if slot == "A" else "dWB"].h, wcnt[ui])
                    for dc in range(DC):
                        if ui == 0:
                            t.wait_ge(sems["sDVE"].h, self.xn_chunk[dc])
                        last = t.matmul(ps[bk][:, 0:TT], buf[:, dc * 128:(dc + 1) * 128],
                                        sb["xnb"][:, dc * TT:(dc + 1) * TT],
                                        start=(dc == 0), stop=(dc == DC - 1))
                else:
                    eb, tsub = arg
                    slot, slotB, wA_c, wB_c = uslots[ui]
                    t.wait_ge(sems["dWA" if slot == "A" else "dWB"].h, wA_c)
                    t.wait_ge(sems["dWA" if slotB == "A" else "dWB"].h, wB_c)
                    for dc in range(DC):
                        sl = slot if dc < nh_half else slotB
                        buf = sb["wA" if sl == "A" else "wB"]
                        di = dc if dc < nh_half else dc - nh_half
                        last = t.matmul(ps[bk][:, 0:512],
                                        sb["xnb"][:, dc * TT + tsub * 128: dc * TT + tsub * 128 + 128],
                                        buf[:, di * 512:(di + 1) * 512],
                                        start=(dc == 0), stop=(dc == DC - 1))
                last.then_inc(sems["sPE"].h, 1)
                self.bank_guard[bk] = (sems["sDVE"], sems["sDVE"].n + ui + 1)
                ui += 1
        n_units = len(units)
        pe_end = pe0 + n_units
        sems["sPE"].n = pe_end

        dve0 = sems["sDVE"].n
        ktn = {}
        ktsem_cnt = {}
        for ui in range(n_units):
            i = (self._kvt_rr if hasattr(self, "_kvt_rr") else 0)
            name = "ktmpA" if (ui % 2 == 0) else "ktmpB"
            ktn[ui] = name
        kt_base = {"ktmpA": sems["dKtA"].n, "ktmpB": sems["dKtB"].n}
        kt_cnt = {"ktmpA": 0, "ktmpB": 0}
        @block.vector
        def _(v):
            for ui, (kind, arg) in enumerate(units):
                ncols = TT if kind in ("K", "Q") else 512
                ktsem = sems["dKtA" if ui % 2 == 0 else "dKtB"]
                v.wait_ge(ktsem.h, kt_base[ktn[ui]] + 16 * kt_cnt[ktn[ui]])
                kt_cnt[ktn[ui]] += 1
                v.wait_ge(sems["sPE"].h, pe0 + ui + 1)
                v.drain()
                v.tensor_copy(sb[ktn[ui]][:, 0:ncols],
                              ps[self._kv_banks[ui]][:, 0:ncols]).then_inc(sems["sDVE"].h, 1)
        sems["sDVE"].n = dve0 + n_units

        @block.gpsimd
        def _(g):
            for ui, (kind, arg) in enumerate(units):
                ktsem = sems["dKtA" if ui % 2 == 0 else "dKtB"]
                g.wait_ge(sems["sDVE"].h, dve0 + ui + 1)
                if kind == "K":
                    dst = dram["kvloc"][arg * 128:(arg + 1) * 128, tile * TT:(tile + 1) * TT]
                    ktsem.inc(g.dma_start(out=dst, in_=sb[ktn[ui]][:, 0:TT]), 16)
                elif kind == "Q":
                    dst = dram["qloc"][arg * 128:(arg + 1) * 128, tile * TT:(tile + 1) * TT]
                    ktsem.inc(g.dma_start(out=dst, in_=sb[ktn[ui]][:, 0:TT]), 16)
                else:
                    eb, tsub = arg
                    h0 = eb * nhb
                    cloc = tile * CPT + tsub
                    dst = bass.AP(
                        dram["kvloc"].tensor,
                        (NH * 128 + h0 * 128) * TC + cloc * 128,
                        [[TC, 128], [128 * TC, nhb], [1, 128]])
                    ktsem.inc(g.dma_start(
                        out=dst,
                        in_=sb[ktn[ui]][:, 0:512].rearrange("p (hh hd) -> p hh hd", hd=128)), 16)
        self.xn_guard = (sems["sPE"], sems["sPE"].n)

    def emit_collective(self):
        sems, dram, block = self.sems, self.dram, self.block

        @block.gpsimd
        def _(g):
            g.wait_ge(sems["dKtA"].h, sems["dKtA"].n)
            g.wait_ge(sems["dKtB"].h, sems["dKtB"].n)
            sems["sCC"].inc(g.collective_compute(
                "AllGather", ALU.bypass, replica_groups=GROUPS,
                ins=[dram["kvloc"]], outs=[dram["kvg"]]))

    def head_sched(self, parity):
        tiles = EVEN_TILES if parity == 0 else ODD_TILES
        CPT, TT = self.c["CPT"], self.c["TT"]
        out = []
        for s in range(4):
            gt = tiles[s]
            q0 = gt * TT
            lst = []
            for G in range(CPT * (gt + 1)):
                d = 128 * G - q0
                lst.append((self.bufcol[G], (TT - d) if d >= 0 else None))
            out.append(lst)
        return out

    def emit_attention(self):
        c, block, nc = self.c, self.block, self.nc
        sems, sb, dram = self.sems, self.sb, self.dram
        DC, TT, TC, NH, CPT, D = c["DC"], c["TT"], c["TC"], c["NH"], c["CPT"], c["D"]
        KCH = c["KCH"]
        ps, pr = self.ps, self.pr
        nhh = NH // 2
        sc = [self.head_sched(0), self.head_sched(1)]
        inv_sqrt_hd = 1.0 / math.sqrt(c["HD"])

        SBANKS = [0, 1, 5]
        Ks = [sb["KsA"], sb["KsB"]]
        Vs = [sb["VsA"], sb["VsB"]]
        Qs = [sb["QsA"], sb["QsB"]]
        dKs = [sems["dKsA"], sems["dKsB"]]
        dVs = [sems["dVsA"], sems["dVsB"]]
        dQs = [sems["dQsA"], sems["dQsB"]]

        @block.vector
        def _(v):
            for g in self.scope_guards2:
                v.wait_ge(g[0].h, g[1])
            v.drain()
            sems["sDVE"].inc(v.memset(sb["Mbig"][:], 1.0))
        @block.gpsimd
        def _(g):
            g.wait_ge(sems["sDVE"].h, sems["sDVE"].n)
            sems["sGP"].inc(g.affine_select(sb["Mbig"][:], sb["Mbig"][:], [[1, 3 * TT]],
                                            ALU.is_ge, 0.0, base=-TT, channel_multiplier=-1))
        mbig_ready = (sems["sGP"], sems["sGP"].n)

        pe0, act0, dve0 = sems["sPE"].n, sems["sACT"].n, sems["sDVE"].n

        def nmax_s(s):
            return max(len(sc[0][s]), len(sc[1][s]))

        def bases(par, s, h):
            n_max = nmax_s(s)
            pe, act, dve = pe0, act0, dve0
            for s2 in range(s):
                nm2 = nmax_s(s2)
                pe += NH * (3 * nm2 + 2) + 2 * DC
                act += NH * (1 + nm2)
                dve += NH * (CPT + 6) + 2 * DC
            pe += h * (3 * n_max + 2) + (DC if h > nhh - 1 else 0)
            act += h * (1 + n_max)
            dve += h * (CPT + 6) + (DC if h > nhh - 1 else 0)
            return pe, act, dve, n_max

        def pe_head_end(s, h):
            pe, _, _, n_max = bases(0, s, h)
            return pe + 3 * n_max + 2

        last_use = {}
        load_cnt = {}
        oslab = {}
        first_load = True

        for s in range(4):
            for h in range(NH):
                sl = h % 2
                @block.sync
                def _(sy, s=s, h=h, sl=sl, first=first_load):
                    if first:
                        for g in self.scope_guards2:
                            sy.wait_ge(g[0].h, g[1])
                        sy.wait_ge(sems["sCC"].h, 1)
                        sy.wait_ge(sems["sGP"].h, mbig_ready[1])
                    lu = last_use.get(("K", sl))
                    if lu is not None:
                        sy.wait_ge(sems["sPE"].h, lu)
                    dKs[sl].inc(sy.dma_start(
                        out=Ks[sl][:, 0:TC], in_=dram["kvg"][h * 128:(h + 1) * 128, :]), 16)
                    dKs[sl].inc(sy.dma_start(
                        out=Ks[sl][:, TC:2 * TC],
                        in_=dram["kvg"][2 * D + h * 128: 2 * D + (h + 1) * 128, :]), 16)
                    dVs[sl].inc(sy.dma_start(
                        out=Vs[sl][:, 0:TC],
                        in_=dram["kvg"][D + h * 128: D + (h + 1) * 128, :]), 16)
                    dVs[sl].inc(sy.dma_start(
                        out=Vs[sl][:, TC:2 * TC],
                        in_=dram["kvg"][3 * D + h * 128: 3 * D + (h + 1) * 128, :]), 16)
                    dQs[sl].inc(sy.dma_start(
                        out=Qs[sl][:, 0:TT],
                        in_=dram["qloc"][h * 128:(h + 1) * 128, s * TT:(s + 1) * TT]), 16)
                first_load = False
                load_cnt[("K", s, h)] = dKs[sl].n
                load_cnt[("V", s, h)] = dVs[sl].n
                load_cnt[("Q", s, h)] = dQs[sl].n
                last_use[("K", sl)] = pe_head_end(s, h)
                if h == nhh - 1 or h == NH - 1:
                    half = 0 if h == nhh - 1 else 1
                    nq = DC // 4
                    pe_done_half = pe_head_end(s, h)
                    for q in range(nq):
                        slot = self.slot()
                        osrc = dram["c_ow"][half, q].rearrange("p (kc j) -> p kc j", j=512)
                        gspec = self.wguard[slot]
                        @block.sync
                        def _(sy, slot=slot, osrc=osrc, gspec=gspec):
                            if gspec is not None:
                                sy.wait_ge(gspec[0].h, gspec[1])
                            dW = sems["dWA" if slot == "A" else "dWB"]
                            a, b = int(osrc.shape[1]), int(osrc.shape[2])
                            buf = sb["wA" if slot == "A" else "wB"]
                            dest = buf[:, 0:a * b].rearrange("p (a b) -> p a b", b=b)
                            dW.inc(sy.dma_start(out=dest, in_=osrc), 16)
                        dWs = sems["dWA" if slot == "A" else "dWB"]
                        oslab[(s, half, q)] = (slot, sb["wA" if slot == "A" else "wB"], dWs.n)
                        self.wguard[slot] = (sems["sPE"], pe_done_half + 4 * (q + 1))

        def slot_end(s):
            pe, act, dve, _ = bases(0, s, NH - 1)
            nm = max(len(sc[0][s]), len(sc[1][s]))
            return (pe + (3 * nm + 2) + DC, act + (1 + nm), dve + (CPT + 6) + DC)

        for s in range(4):
            @block.tensor
            def _(t, s=s):
                def body(par):
                    for h in range(NH):
                        pe_b, act_b, dve_b, n_max = bases(par, s, h)
                        n = len(sc[par][s])
                        t.wait_ge(dQs[h % 2].h, load_cnt[("Q", s, h)])
                        t.wait_ge(sems["sACT"].h, act_b)
                        t.matmul(ps[3][:, 0:TT], sb["gwT_sb"][:], Qs[h % 2][:, 0:TT],
                                 start=True, stop=True, skip_group_check=True
                                 ).then_inc(sems["sPE"].h, 1)
                        t.wait_ge(dKs[h % 2].h, load_cnt[("K", s, h)])
                        t.wait_ge(dVs[h % 2].h, load_cnt[("V", s, h)])

                        def score(j):
                            bc, off = sc[par][s][j]
                            if j >= 3:
                                t.wait_ge(sems["sACT"].h, act_b + 2 + (j - 3))
                            else:
                                t.wait_ge(sems["sACT"].h, act_b)
                            if j == 2:
                                t.wait_ge(sems["sDVE"].h, dve_b)
                            t.matmul(ps[SBANKS[j % 3]][:, 0:TT],
                                     Ks[h % 2][:, bc * 128:(bc + 1) * 128],
                                     Qs[h % 2][:, 0:TT], start=True, stop=True,
                                     skip_group_check=True).then_inc(sems["sPE"].h, 1)

                        def avsum(j):
                            bc, off = sc[par][s][j]
                            t.wait_ge(sems["sACT"].h,
                                      act_b + 2 + j if j < n - 1 else act_b + 1 + n_max)
                            if off is not None:
                                jd = j - (n - CPT)
                                t.wait_ge(sems["sDVE"].h, dve_b + 3 + jd)
                            if j == 0:
                                t.wait_ge(sems["sDVE"].h, dve_b)
                            t.matmul(ps[2][:, 0:TT],
                                     Vs[h % 2][:, bc * 128:(bc + 1) * 128],
                                     sb["PT" + str(j % 3)][:, 0:TT],
                                     start=(j == 0), stop=(j == n - 1),
                                     skip_group_check=True).then_inc(sems["sPE"].h, 1)
                            t.matmul(pr[0][0:1, 0:TT], sb["onesc"][:, 0:1],
                                     sb["PT" + str(j % 3)][:, 0:TT],
                                     start=(j == 0), stop=(j == n - 1),
                                     skip_group_check=True).then_inc(
                                sems["sPE"].h, 1 + (3 * (n_max - n) if j == n - 1 else 0))

                        score(0)
                        if n > 1:
                            score(1)
                        for j in range(2, n):
                            score(j)
                            avsum(j - 2)
                        if n > 1:
                            avsum(n - 2)
                        avsum(n - 1)
                        t.wait_ge(sems["sDVE"].h, dve_b + 3 + CPT)
                        t.matmul(ps[4][:, 0:TT], sb["ones1"][0:1, :],
                                 sb["rowA"][0:1, 0:TT], start=True, stop=True,
                                 skip_group_check=True).then_inc(sems["sPE"].h, 1)
                        if h == nhh - 1 or h == NH - 1:
                            half = 0 if h == nhh - 1 else 1
                            pe_done = pe_b + 3 * n_max + 2
                            dve_done = dve_b + CPT + 6
                            for dc in range(DC):
                                slot, buf, wcnt = oslab[(s, half, dc // 4)]
                                dsub = dc % 4
                                t.wait_ge(sems["dWA" if slot == "A" else "dWB"].h, wcnt)
                                t.wait_ge(sems["sDVE"].h, dve_done)
                                bk = 5 if dc % 2 == 0 else 4
                                if dc >= 2:
                                    t.wait_ge(sems["sDVE"].h, dve_done + dc - 1)
                                for kc in range(nhh):
                                    last = t.matmul(
                                        ps[bk][:, 0:TT],
                                        buf[:, kc * 512 + dsub * 128: kc * 512 + dsub * 128 + 128],
                                        sb["out8"][:, kc * TT:(kc + 1) * TT],
                                        start=(kc == 0), stop=(kc == nhh - 1),
                                        skip_group_check=True)
                                last.then_inc(sems["sPE"].h, 1)
                with t.register(f"prt_t{s}") as prt:
                    t.cc_rank_ld(prt, replica_groups=GROUPS)
                    with t.If_eq(prt, 0):
                        body(0)
                    with t.Else():
                        body(1)

            @block.scalar
            def _(sx, s=s):
                def body(par):
                    for h in range(NH):
                        pe_b, act_b, dve_b, n_max = bases(par, s, h)
                        n = len(sc[par][s])
                        sx.wait_ge(sems["sPE"].h, pe_b + 1)
                        sx.wait_ge(sems["sDVE"].h, dve_b)
                        sx.drain()
                        sx.activation(sb["gatebuf"][:, 0:TT], ps[3][:, 0:TT], AF.Exp,
                                      scale=-1.0, bias=sb["gbneg"][:, 0:1]
                                      ).then_inc(sems["sACT"].h, 1)
                        for j in range(n):
                            pos = 2 if j == 0 else (3 if j == 1 else 3 * j - 2)
                            sx.wait_ge(sems["sPE"].h, pe_b + pos)
                            sx.drain()
                            a = sx.activation(sb["PT" + str(j % 3)][:, 0:TT],
                                              ps[SBANKS[j % 3]][:, 0:TT], AF.Exp,
                                              scale=inv_sqrt_hd)
                            a.then_inc(sems["sACT"].h,
                                       1 + (n_max - n) if j == n - 1 else 1)
                with sx.register(f"prt_s{s}") as prt:
                    sx.cc_rank_ld(prt, replica_groups=GROUPS)
                    with sx.If_eq(prt, 0):
                        body(0)
                    with sx.Else():
                        body(1)

            @block.vector
            def _(v, s=s):
                def body(par):
                    for h in range(NH):
                        pe_b, act_b, dve_b, n_max = bases(par, s, h)
                        n = len(sc[par][s])
                        v.wait_ge(sems["sACT"].h, act_b + 1)
                        v.drain()
                        v.tensor_scalar(sb["gatebuf"][:, 0:TT], sb["gatebuf"][:, 0:TT],
                                        1.0, None, ALU.add).then_inc(sems["sDVE"].h, 1)
                        v.drain()
                        v.reciprocal(sb["gatebuf"][:, 0:TT], sb["gatebuf"][:, 0:TT]
                                     ).then_inc(sems["sDVE"].h, 1)
                        for jd in range(CPT):
                            j = n - CPT + jd
                            bc, off = sc[par][s][j]
                            v.wait_ge(sems["sACT"].h,
                                      act_b + 2 + j if j < n - 1 else act_b + 1 + n_max)
                            v.drain()
                            v.tensor_tensor(sb["PT" + str(j % 3)][:, 0:TT],
                                            sb["PT" + str(j % 3)][:, 0:TT],
                                            sb["Mbig"][:, off:off + TT], ALU.mult
                                            ).then_inc(sems["sDVE"].h, 1)
                        v.wait_ge(sems["sPE"].h, pe_b + 3 * n_max + 1)
                        v.drain()
                        v.reciprocal(sb["rowA"][0:1, 0:TT], pr[0][0:1, 0:TT]
                                     ).then_inc(sems["sDVE"].h, 1)
                        v.wait_ge(sems["sPE"].h, pe_b + 3 * n_max + 2)
                        v.drain()
                        v.tensor_copy(sb["rb_sb"][:, 0:TT], ps[4][:, 0:TT]
                                      ).then_inc(sems["sDVE"].h, 1)
                        if h >= nhh:
                            pass
                        v.drain()
                        v.tensor_tensor(sb["out8"][:, (h % nhh) * TT:(h % nhh + 1) * TT],
                                        ps[2][:, 0:TT], sb["rb_sb"][:, 0:TT], ALU.mult
                                        ).then_inc(sems["sDVE"].h, 1)
                        v.drain()
                        v.tensor_tensor(sb["out8"][:, (h % nhh) * TT:(h % nhh + 1) * TT],
                                        sb["out8"][:, (h % nhh) * TT:(h % nhh + 1) * TT],
                                        sb["gatebuf"][:, 0:TT], ALU.mult
                                        ).then_inc(sems["sDVE"].h, 1)
                        if h == nhh - 1 or h == NH - 1:
                            pe_done = pe_b + 3 * n_max + 2
                            t0 = s * TT
                            for dc in range(DC):
                                v.wait_ge(sems["sPE"].h, pe_done + dc + 1)
                                v.drain()
                                v.scalar_tensor_tensor(
                                    self.xres(dc, t0, TT),
                                    ps[5 if dc % 2 == 0 else 4][:, 0:TT], 0.0,
                                    self.xres(dc, t0, TT), ALU.add, ALU.add
                                    ).then_inc(sems["sDVE"].h, 1)
                with v.register(f"prt_v{s}") as prt:
                    v.cc_rank_ld(prt, replica_groups=GROUPS)
                    with v.If_eq(prt, 0):
                        body(0)
                    with v.Else():
                        body(1)

            pe_e, act_e, dve_e = slot_end(s)
            sems["sPE"].n = pe_e
            sems["sACT"].n = act_e
            sems["sDVE"].n = dve_e
            @block.scalar
            def _(sx, s=s):
                sx.wait_ge(sems["sDVE"].h, dve_e)
                sems["dOut"].inc(sx.dma_start(
                    out=self.dram[self.out_name].rearrange(
                        "p (dc t) -> p dc t", t=TC)[:, :, s * TT:(s + 1) * TT],
                    in_=sb["x_res"][:].rearrange(
                        "p (dc t) -> p dc t", t=TC)[:, :, s * TT:(s + 1) * TT]), 16)
        for i in range(6):
            self.bank_guard[i] = (sems["sDVE"], sems["sDVE"].n)
        self.pr_guard = (sems["sDVE"], sems["sDVE"].n)

    def emit_l1_ffn_tile(self, tile):
        c, dram = self.c, self.dram
        self.emit_stats_xn(tile)
        self.emit_router(tile, layer=1)
        for e in range(c["E"]):
            self.emit_cb_broadcast(e)
            self.emit_gate_up_down(tile, dram["c_gp"][e], dram["c_up"][e],
                                   dram["c_dp"][e], c["EHC"], with_cb=True,
                                   ready=self.cast_ready[f"l1e{e}"])
        self.xn_guard = (self.sems["sPE"], self.sems["sPE"].n)

    def emit_epilogue(self):
        sems, sb, dram, block = self.sems, self.sb, self.dram, self.block

        @block.scalar
        def _(s):
            if sems["dOut"].n == 0:
                s.wait_ge(sems["sDVE"].h, sems["sDVE"].n)
                sems["dOut"].inc(s.dma_start(out=dram[self.out_name],
                                             in_=sb["x_res"][:]), 16)
            s.wait_ge(sems["dOut"].h, sems["dOut"].n)


def build_program(cfg, layers=(0, 1), in_name="x_in", out_name="out"):
    nc = bass.Bass("TRN2", target_bir_lowering=False)
    b = B(nc, cfg, layers, in_name, out_name)
    b.declare()
    c = b.c
    TT, DC, SHC, EHC = c["TT"], c["DC"], c["SHC"], c["EHC"]
    with b.es:
        b.alloc_sems()
        b.alloc_psum()
        b.alloc_persistent()
        with nc.Block() as block:
            b.block = block
            b.emit_prologue()
            with contextlib.ExitStack() as s1:
                def S1(name, shape, dt):
                    b.sb[name] = s1.enter_context(nc.sbuf_tensor(name, shape, dt))
                hchunks = max(DC, EHC, SHC if 0 in layers else 0)
                S1("xnb", [128, DC * TT], BF16)
                S1("H", [128, hchunks * TT], BF16)
                S1("mr_b", [128, 2 * TT], F32)
                S1("rowB", [1, TT], F32)
                S1("rowC", [1, TT], F32)
                S1("ktmpA2", [128, 512], BF16)
                S1("ktmpB2", [128, 512], BF16)
                S1("ue", [128, TT], BF16)
                b.sb["ktmpA"] = b.sb["ktmpA2"]
                b.sb["ktmpB"] = b.sb["ktmpB2"]
                if 0 in layers:
                    for tile in range(4):
                        b.emit_l0_tile(tile, guards=tuple(b.scope_guards) if tile == 0 else ())
                if 1 in layers:
                    g0 = tuple(b.scope_guards) if 0 not in layers else ()
                    for tile in range(4):
                        b.emit_stats_xn(tile, extra_guards=g0 if tile == 0 else ())
                        b.emit_kvq_tile(tile)
                    b.emit_collective()
                    for tile in range(4):
                        b.emit_l1_ffn_tile(tile)
                b.scope_guards2 = [(b.sems["sPE"], b.sems["sPE"].n),
                                   (b.sems["sDVE"], b.sems["sDVE"].n),
                                   (b.sems["sACT"], b.sems["sACT"].n)]
            if 1 in layers:
                with contextlib.ExitStack() as s2:
                    def S2(name, shape, dt):
                        b.sb[name] = s2.enter_context(nc.sbuf_tensor(name, shape, dt))
                    TC, NH = c["TC"], c["NH"]
                    S2("KsA", [128, 2 * TC], BF16)
                    S2("KsB", [128, 2 * TC], BF16)
                    S2("VsA", [128, 2 * TC], BF16)
                    S2("VsB", [128, 2 * TC], BF16)
                    S2("QsA", [128, TT], BF16)
                    S2("QsB", [128, TT], BF16)
                    S2("PT0", [128, TT], BF16)
                    S2("PT1", [128, TT], BF16)
                    S2("PT2", [128, TT], BF16)
                    S2("out8", [128, (NH // 2) * TT], BF16)
                    S2("rb_sb", [128, TT], F32)
                    S2("Mbig", [128, 3 * TT], BF16)
                    b.sb["gatebuf"] = b.sb["xtmp"]
                    b.emit_attention()
            b.emit_epilogue()
    return nc


def _slabify(a, inC, outC):
    return np.ascontiguousarray(
        a.reshape(inC, 128, outC, 128).transpose(2, 1, 0, 3).reshape(outC, 128, inC * 128))


def _shard_inputs(cfg, inputs):
    c = derive(cfg)
    TT, TC, D = c["TT"], c["TC"], c["D"]
    DC, EHC, SHC, NH, E = c["DC"], c["EHC"], c["SHC"], c["NH"], c["E"]
    nhh = NH // 2
    rows = snake_rows(TT)
    x = np.asarray(inputs["L0_attn_x"])
    maps = []
    common0 = {}
    common1 = {}
    common0["w_dec"] = np.ascontiguousarray(inputs["L0_attn_weight"])
    common0["b_dec"] = np.ascontiguousarray(inputs["L0_attn_bias"]).reshape(1, -1)
    common0["dvv"] = np.ascontiguousarray(inputs["L0_attn_decay_value"])
    common0["wr0"] = np.ascontiguousarray(np.asarray(inputs["L0_ffn_W_router"]).T)
    common0["wsg"] = _slabify(np.asarray(inputs["L0_ffn_Ws_gate"]), DC, SHC)
    common0["wsu"] = _slabify(np.asarray(inputs["L0_ffn_Ws_up"]), DC, SHC)
    common0["wsd"] = _slabify(np.asarray(inputs["L0_ffn_Ws_down"]), SHC, DC)
    common0["weg"] = np.stack([_slabify(np.asarray(inputs["L0_ffn_We_gate"][e]), DC, EHC)
                               for e in range(E)])
    common0["weu"] = np.stack([_slabify(np.asarray(inputs["L0_ffn_We_up"][e]), DC, EHC)
                               for e in range(E)])
    common0["wed"] = np.stack([_slabify(np.asarray(inputs["L0_ffn_We_down"][e]), EHC, DC)
                               for e in range(E)])
    qwT = np.asarray(inputs["L1_attn_q_w"]).T
    kwT = np.asarray(inputs["L1_attn_k_w"]).T
    vwT = np.asarray(inputs["L1_attn_v_w"]).T
    owT = np.asarray(inputs["L1_attn_o_w"]).T
    common1["qw"] = _slabify(qwT, DC, NH)
    common1["kw"] = _slabify(kwT, DC, NH)
    common1["vw"] = np.ascontiguousarray(
        vwT.reshape(2, DC // 2, 128, D // 512, 512).transpose(3, 0, 2, 1, 4)
        .reshape(D // 512, 2, 128, (DC // 2) * 512))
    common1["ow"] = np.ascontiguousarray(
        owT.reshape(2, nhh, 128, DC // 4, 512).transpose(0, 3, 2, 1, 4)
        .reshape(2, DC // 4, 128, nhh * 512))
    common1["gw"] = np.ascontiguousarray(np.asarray(inputs["L1_attn_gate_w"]).T)
    common1["gb"] = np.ascontiguousarray(inputs["L1_attn_gate_b"]).reshape(1, -1)
    common1["wr1"] = np.ascontiguousarray(np.asarray(inputs["L1_ffn_gate_w"]))
    gpT = np.asarray(inputs["L1_ffn_gate_proj"]).transpose(0, 2, 1)
    upT = np.asarray(inputs["L1_ffn_up_proj"]).transpose(0, 2, 1)
    dpT = np.asarray(inputs["L1_ffn_down_proj"]).transpose(0, 2, 1)
    common1["gp"] = np.stack([_slabify(gpT[e], DC, EHC) for e in range(E)])
    common1["up"] = np.stack([_slabify(upT[e], DC, EHC) for e in range(E)])
    common1["dp"] = np.stack([_slabify(dpT[e], EHC, DC) for e in range(E)])
    for core in range(8):
        bat, par = core // 2, core % 2
        xc = x[bat][rows[par], :].T
        xr = np.ascontiguousarray(
            xc.reshape(DC, 128, TC).transpose(1, 0, 2).reshape(128, DC * TC))
        maps.append((xr, dict(common0), dict(common1)))
    return maps, rows


def _gather_output(cfg, results, rows, out_name="out"):
    c = derive(cfg)
    D, TC, TT = c["D"], c["TC"], c["TT"]
    B = 4
    S = c["S"]
    DC = c["DC"]
    out = np.empty((B, S, D), dtype=np.float32)
    for core in range(8):
        bat, par = core // 2, core % 2
        oc = np.asarray(results[core][out_name])
        od = oc.reshape(128, DC, TC).transpose(1, 0, 2).reshape(D, TC)
        out[bat][rows[par], :] = od.T
    return out


def kernel(**inputs):
    cfg = TINY_CFG if os.environ.get("MK_TINY") else FULL_CFG
    mode = os.environ.get("MK_MODE", "fused")
    maps, rows = _shard_inputs(cfg, inputs)

    if mode == "fused":
        nc = build_program(cfg, layers=(0, 1))
        in_maps = []
        for xc, c0, c1 in maps:
            m = {"x_in": xc}
            m.update(c0)
            m.update(c1)
            in_maps.append(m)
        res = run_bass_kernel_spmd(nc, in_maps, core_ids=list(range(8)))
        return _gather_output(cfg, res.results, rows)

    nc0 = build_program(cfg, layers=(0,), in_name="x_in", out_name="out")
    in_maps0 = []
    for xc, c0, c1 in maps:
        m = {"x_in": xc}
        m.update(c0)
        in_maps0.append(m)
    res0 = run_bass_kernel_spmd(nc0, in_maps0, core_ids=list(range(8)))
    nc1 = build_program(cfg, layers=(1,), in_name="x_in", out_name="out")
    in_maps1 = []
    for i, (xc, c0, c1) in enumerate(maps):
        m = {"x_in": np.ascontiguousarray(np.asarray(res0.results[i]["out"]))}
        m.update(c1)
        in_maps1.append(m)
    res1 = run_bass_kernel_spmd(nc1, in_maps1, core_ids=list(range(8)))
    return _gather_output(cfg, res1.results, rows)
